# version 06
# baseline (speedup 1.0000x reference)
import contextlib
import numpy as np
import ml_dtypes
import concourse.bass as bass
import concourse.mybir as mybir
from concourse.bass_utils import run_bass_kernel_spmd

F32 = mybir.dt.float32
BF16 = mybir.dt.bfloat16
I32 = mybir.dt.int32
AF = mybir.ActivationFunctionType
ALU = mybir.AluOpType
AX = mybir.AxisListType

D = 4096
DC = 32
T = 1024
TH = 512
NKV = 8
NQH = 32
NB = 32
EPS = 1e-5
QKV_OC = 48
OC_K0, OC_V0, OC_Q0 = 0, 8, 16

_ALLOWED = [{j for j in range(NB) if j <= i and (i - j < 16 or (j + 1) % 8 == 0)}
            for i in range(NB)]

def _attn_units():
    units_per_m = []
    for m in range(8):
        U = set()
        for h in (0, 1):
            for i in (4 * m + h, 4 * m + 2 + h):
                U |= _ALLOWED[i]
        units = []
        for side in (0, 1):
            slots = sorted({j // 2 for j in U if j % 2 == side})
            runs = []
            for s in slots:
                if runs and s == runs[-1][1] + 1:
                    runs[-1][1] = s
                else:
                    runs.append([s, s])
            ext = []
            for a, b in runs:
                a -= a % 2
                b += (1 - b % 2)
                if ext and a <= ext[-1][1] + 1:
                    ext[-1][1] = max(ext[-1][1], b)
                else:
                    ext.append([a, b])
            for a, b in ext:
                for s in range(a, b + 1, 2):
                    units.append((side, s))
        units_per_m.append(units)
    return units_per_m

UNITS = _attn_units()

def _unit_mask(half, m, side, s0):
    msk = np.zeros((128, 128), dtype=np.float32)
    for kk in (0, 1):
        j = 2 * (s0 + kk) + side
        for t in (0, 1):
            qb = 4 * m + 2 * t + half
            blk = msk[kk * 64:(kk + 1) * 64, t * 64:(t + 1) * 64]
            if j == qb:
                blk[:] = (np.arange(64)[:, None] <= np.arange(64)[None, :])
            elif j in _ALLOWED[qb]:
                blk[:] = 1.0
    return msk

def _mask_plan():
    flagged, off, col = [], {}, 0
    for m in range(8):
        for ui, (side, s0) in enumerate(UNITS[m]):
            m0 = _unit_mask(0, m, side, s0)
            m1 = _unit_mask(1, m, side, s0)
            if not (np.all(m0 == 1.0) and np.all(m1 == 1.0)):
                flagged.append((m, ui))
                off[(m, ui)] = col
                col += 128
    return flagged, off, col

MASK_FLAGGED, MASK_OFF, MASK_COLS = _mask_plan()
MASK_FLAG_SET = set(MASK_FLAGGED)

def _vis_ranges():
    out = []
    for m in range(8):
        cols = []
        for side, s0 in UNITS[m]:
            a = side * 1024 + s0 * 64
            cols.append((a, a + 128))
        cols.sort()
        merged = []
        for a, b in cols:
            if merged and a <= merged[-1][1]:
                merged[-1][1] = max(merged[-1][1], b)
            else:
                merged.append([a, b])
        pieces = []
        for a, b in merged:
            while a < b:
                q = a // 512
                e = min(b, (q + 1) * 512)
                pieces.append((q, a - q * 512, e - q * 512))
                a = e
        out.append(pieces)
    return out

VIS_RANGES = _vis_ranges()

class Ctr:
    def __init__(self, name):
        self.name = name
        self.n = 0
        self.h = None


class Rec:
    def __init__(self, prog, name, ctr):
        self.p = prog
        self.name = name
        self.c = ctr
        self.ops = []

    def _waits(self, waits):
        for (c, n) in waits:
            if c is None or n is None or n <= 0:
                continue
            key = (self.name, c.name)
            if n > self.p.wm.get(key, 0):
                self.p.wm[key] = n
                self.ops.append(("wait", c, n))

    def op(self, fname, *a, waits=(), inc=True, **kw):
        self._waits(waits)
        if inc:
            self.c.n += 1
        self.ops.append(("op", fname, a, kw, self.c if inc else None))
        return self.c.n

    def dma(self, q, out, in_, waits=()):
        self._waits(waits)
        q.n += 16
        self.ops.append(("dma", out, in_, q))
        return q.n

    def collective(self, ccctr, kind, aop, groups, ins, outs, waits=()):
        self._waits(waits)
        ccctr.n += 1
        self.ops.append(("op", "collective_compute", (kind, aop),
                         dict(replica_groups=groups, ins=ins, outs=outs), ccctr))
        return ccctr.n

    def wait_only(self, waits):
        self._waits(waits)

    def replay(self, eng):
        for rec in self.ops:
            if rec[0] == "wait":
                _, c, n = rec
                eng.wait_ge(c.h, n)
            elif rec[0] == "op":
                _, fname, a, kw, ctr = rec
                r = getattr(eng, fname)(*a, **kw)
                if ctr is not None:
                    r.then_inc(ctr.h, 1)
            else:
                _, out, in_, q = rec
                eng.dma_start(out=out, in_=in_).then_inc(q.h, 16)


class Prog:
    def __init__(self):
        self.wm = {}
        self.ctr = {n: Ctr(n) for n in
                    ("pe", "ve", "se", "po", "qsy", "qsc", "qgp", "cc")}
        self.pe = Rec(self, "pe", self.ctr["pe"])
        self.ve = Rec(self, "ve", self.ctr["ve"])
        self.se = Rec(self, "se", self.ctr["se"])
        self.po = Rec(self, "po", self.ctr["po"])
        self.sy = Rec(self, "sy", self.ctr["qsy"])

    def barrier(self):
        recs = [self.pe, self.ve, self.se, self.po]
        counts = [(r.c, r.c.n) for r in recs]
        for r in recs:
            r.wait_only([(c, n) for (c, n) in counts if c is not r.c])
        return {r.name: r.c.n for r in recs}


def build(taps=(), attn_serial=0, ffn_serial=0):
    nc = bass.Bass("TRN2", target_bir_lowering=False)
    dt = nc.dram_tensor
    xin = dt("xin", [D, T], F32, kind="ExternalInput").ap()
    wq = dt("wq", [QKV_OC * 128 * DC * 128], F32, kind="ExternalInput").ap()
    wd = dt("wd", [DC * 128 * DC * 128], F32, kind="ExternalInput").ap()
    wf = dt("wf", [DC * 128 * DC * 128], F32, kind="ExternalInput").ap()
    lA = dt("lA", [DC * 128, 48], F32, kind="ExternalInput").ap()
    lB = dt("lB", [48, D], F32, kind="ExternalInput").ap()
    bq = dt("bq", [128, QKV_OC], F32, kind="ExternalInput").ap()
    bv_r = dt("bv_r", [1, 1024], F32, kind="ExternalInput").ap()
    bd = dt("bd", [128, DC], F32, kind="ExternalInput").ap()
    bf_ = dt("bf", [128, DC], F32, kind="ExternalInput").ap()
    cosq = dt("cosq", [128, T], F32, kind="ExternalInput").ap()
    sinq = dt("sinq", [128, T], F32, kind="ExternalInput").ap()
    masks = dt("masks", [128, max(MASK_COLS, 128)], BF16, kind="ExternalInput").ap()
    rowc = dt("rowc", [1, 384], F32, kind="ExternalInput").ap()
    iotab = dt("iotab", [48, 1], F32, kind="ExternalInput").ap()
    bval = dt("bval", [1, 1], F32, kind="ExternalInput").ap()
    cstb = dt("cstb", [128, 130], BF16, kind="ExternalInput").ap()
    bcon = dt("bcon", [128, 256], F32, kind="ExternalInput").ap()
    lsi = dt("lsi", [1, 4], I32, kind="ExternalInput").ap()
    out = dt("out", [D, T], F32, kind="ExternalOutput").ap()
    x1s = dt("x1s", [D, T], F32).ap()
    x2s = dt("x2s", [D, T], F32).ap()
    ccin = dt("ccin", [2048, 1024], BF16)
    ccout = dt("ccout", [4096, 1024], BF16)

    tapd = {}
    tap_specs = {
        "t_ln1": ([128, DC * T], BF16), "t_x1": ([D, T], F32),
        "t_ln2": ([128, DC * T], BF16), "t_q": ([128, NQH * T], BF16),
        "t_kv": ([128, 4096], BF16), "t_ctx": ([128, NQH * T], BF16),
        "t_x2": ([D, T], F32), "t_sel": ([48, 1], F32),
        "t_negc": ([128, 64], F32), "t_u": ([48, T], BF16),
        "t_mu": ([128, 2 * T], F32), "t_b1raw": ([128, DC * T], BF16),
        "t_sums": ([1, 2 * T], F32),
    }
    for tname in taps:
        shape, tdt = tap_specs[tname]
        tapd[tname] = dt(tname, shape, tdt, kind="ExternalOutput").ap()

    p = Prog()
    pe_c, ve_c, se_c, po_c = (p.ctr[n] for n in ("pe", "ve", "se", "po"))
    qsy, qsc, qgp, cc_c = (p.ctr[n] for n in ("qsy", "qsc", "qgp", "cc"))
    PE, VE, SE, PO, SY = p.pe, p.ve, p.se, p.po, p.sy

    with contextlib.ExitStack() as es:
        def sb(name, shape, dtp, stack=None):
            return (stack or es).enter_context(nc.sbuf_tensor(name, shape, dtp))

        B1 = sb("B1", [128, DC * T], BF16)
        B2 = sb("B2", [128, DC * T], BF16)
        WST = sb("WST", [128, 2 * 1024], F32)
        WBF = sb("WBF", [128, 2 * 4096], BF16)
        LAB = sb("LAB", [128, DC * 48], BF16)
        LBB = sb("LBB", [128, D], BF16)
        BQS = sb("BQS", [128, QKV_OC], F32)
        BDS = sb("BDS", [128, DC], F32)
        BFS = sb("BFS", [128, DC], F32)
        ROWC = sb("ROWC", [1, 384], F32)
        IOTA = sb("IOTA", [48, 1], F32)
        BVAL = sb("BVAL", [1, 1], F32)
        CSTB = sb("CSTB", [128, 130], BF16)
        BCON = sb("BCON", [128, 256], F32)
        LSIS = sb("LSIS", [1, 4], I32)
        SELC = sb("SELC", [48, 1], F32)
        BQSK = sb("BQSK", [128, 8], F32)
        FSCR = sb("FSCR", [128, 8], BF16)
        U1S = sb("U1S", [128, T], BF16)
        TINY = sb("TINY", [128, 48], F32)
        QN2M = sb("QN2M", [128, 256], F32)
        pb = [es.enter_context(nc.psum_tensor(f"pb{i}", [128, 512], F32)) for i in range(8)]

        xin3 = xin.rearrange("(c p) t -> p c t", p=128)
        x1s3 = x1s.rearrange("(c p) t -> p c t", p=128)
        x2s3 = x2s.rearrange("(c p) t -> p c t", p=128)
        out3 = out.rearrange("(c p) t -> p c t", p=128)
        wqv = wq.rearrange("(g p n) -> g p n", g=QKV_OC, p=128)
        wdv = wd.rearrange("(g p n) -> g p n", g=DC, p=128)
        wfv = wf.rearrange("(g p n) -> g p n", g=DC, p=128)
        b1v = B1.rearrange("p (hh t) -> p hh t", t=T)
        b2v = B2.rearrange("p (hh t) -> p hh t", t=T)

        def b1c(c):
            return B1[:, c * T:(c + 1) * T]

        def b2c(c):
            return B2[:, c * T:(c + 1) * T]

        d_rowc = SY.dma(qsy, ROWC[:], rowc)
        d_iota = SY.dma(qsy, IOTA[:], iotab)
        d_bval = SY.dma(qsy, BVAL[:], bval)
        d_cstb = SY.dma(qsy, CSTB[:], cstb)
        d_bcon = SY.dma(qsy, BCON[:], bcon)
        d_lsi = SY.dma(qsy, LSIS[:], lsi)
        SY.dma(qsy, BQS[:], bq)
        SY.dma(qsy, BDS[:], bd)
        d_bias = SY.dma(qsy, BFS[:], bf_)
        d_lA = SY.dma(qsy, WST[:, 0:DC * 48].rearrange("p (c r) -> p c r", r=48),
                      lA.rearrange("(c p) r -> p c r", p=128))
        ms1 = PO.op("memset", TINY[:], 0.0)
        ms2 = PO.op("memset", QN2M[:], 0.0)
        ms3 = PO.op("memset", U1S[:], 0.0)
        c_lA = PO.op("tensor_copy", waits=[(qsy, d_lA)], out=LAB[:], in_=WST[:, 0:DC * 48])
        c_lB = PO.op("memset", LBB[:], 0.0, waits=[(po_c, c_lA)])
        for hb in (0, 1):
            d_lB = SY.dma(qsy, WST[0:48, 0:2048], lB[:, hb * 2048:(hb + 1) * 2048],
                          waits=[(po_c, c_lB)])
            c_lB = PO.op("tensor_copy", waits=[(qsy, d_lB)],
                         out=LBB[0:48, hb * 2048:(hb + 1) * 2048], in_=WST[0:48, 0:2048])
        lora_w_ready = c_lB
        bqsk_ready = VE.op("tensor_scalar", waits=[(qsy, d_bias)], out=BQSK[:],
                           in0=BQS[:, 0:8], scalar1=128.0, scalar2=None, op0=ALU.mult)
        v = VE.op("tensor_copy", waits=[(qsy, d_lsi), (po_c, ms3)],
                  out=TINY[0:1, 0:4], in_=LSIS[:])
        v = VE.op("tensor_scalar", waits=[(ve_c, v), (qsy, d_bval)], out=TINY[0:1, 0:4],
                  in0=TINY[0:1, 0:4], scalar1=BVAL[0:1, 0:1], scalar2=None, op0=ALU.is_le)
        v = VE.op("reduce_sum", waits=[(ve_c, v)], out=TINY[0:1, 4:5],
                  in_=TINY[0:1, 0:4], axis=AX.X)
        v_use = VE.op("tensor_scalar", waits=[(ve_c, v)], out=TINY[0:1, 5:6],
                      in0=TINY[0:1, 4:5], scalar1=2.0, scalar2=None, op0=ALU.is_ge)
        v_a = VE.op("tensor_scalar", waits=[(ve_c, v)], out=TINY[0:1, 6:7],
                    in0=TINY[0:1, 4:5], scalar1=2.0, scalar2=0.0,
                    op0=ALU.subtract, op1=ALU.max)
        v_a = VE.op("tensor_scalar", waits=[(ve_c, v_a)], out=TINY[0:1, 6:7],
                    in0=TINY[0:1, 6:7], scalar1=2.0, scalar2=None, op0=ALU.min)
        sel_po = PO.op("tensor_copy", waits=[(ve_c, max(v_use, v_a))],
                       out=TINY[0:1, 5:7], in_=TINY[0:1, 5:7])
        PE.op("drain", inc=False)
        m_bc = PE.op("matmul", waits=[(po_c, sel_po), (qsy, d_bcon)],
                     out=pb[7][0:48, 0:2], lhsT=BCON[:, 0:48], rhs=TINY[:, 5:7],
                     start=True, stop=True)
        PE.op("drain", inc=False)
        v = VE.op("tensor_tensor", waits=[(pe_c, m_bc), (qsy, d_iota)], out=SELC[:],
                  in0=IOTA[:], in1=pb[7][0:48, 1:2], op=ALU.is_equal)
        sel_ready = VE.op("tensor_tensor", waits=[(ve_c, v)], out=SELC[:],
                          in0=SELC[:], in1=pb[7][0:48, 0:1], op=ALU.mult)
        if "t_sel" in tapd:
            SE.dma(qsc, tapd["t_sel"], SELC[:], waits=[(ve_c, sel_ready)])

        def new_wstream(wv_ap, nog):
            return {"issued": set(), "cast": {}, "pedone": {}, "nog": nog, "wv": wv_ap,
                    "slot_cast": {}, "war_po": max(po_c.n, c_lB), "war_pe": pe_c.n,
                    "war_ve": ve_c.n}

        def stream_weights(st, og):
            if og in st["issued"] or og >= st["nog"]:
                return
            st["issued"].add(og)
            wv = st["wv"]
            for q in range(4):
                s_ = q % 2
                w = [(po_c, st["war_po"]),
                     (ve_c, max(st["slot_cast"].get(s_, 0), st["war_ve"]))]
                dq = SY.dma(qsy, WST[:, s_ * 1024:(s_ + 1) * 1024],
                            wv[og, :, q * 1024:(q + 1) * 1024], waits=w)
                w = [(qsy, dq),
                     (pe_c, max(st["pedone"].get(og - 2, 0), st["war_pe"]))]
                settle = PO.op("tensor_copy", waits=w, out=FSCR[:, 0:8],
                               in_=FSCR[:, 0:8])
                wslice = WBF[:, (og % 2) * 4096 + q * 1024:
                             (og % 2) * 4096 + (q + 1) * 1024]
                cvv = VE.op("tensor_copy", waits=[(po_c, settle)],
                            out=wslice, in_=WST[:, s_ * 1024:(s_ + 1) * 1024])
                st["slot_cast"][s_] = cvv
                st["cast"][(og, q)] = PO.op("tensor_copy", waits=[(ve_c, cvv)],
                                            out=wslice, in_=wslice)

        def wtile(og, dc):
            return WBF[:, (og % 2) * 4096 + dc * 128:(og % 2) * 4096 + (dc + 1) * 128]

        def emit_stats_mms(stat, rhs_bf, sq_bf, first, last, c):
            for h in (0, 1):
                for which, rhs in (("s", rhs_bf), ("q", sq_bf)):
                    bk = stat["bank"][(which, h)]
                    w = [stat["ready"][(c, h, which)]]
                    stat["mm"][(c, h, which)] = PE.op(
                        "matmul", waits=w, out=pb[bk][:, 0:512],
                        lhsT=CSTB[:, 1:129], rhs=rhs[:, h * TH:(h + 1) * TH],
                        start=first, stop=last)

        def finalize_ln(stat, MUBC, SROW, sums_tap=None):
            lastmm = max(stat["mm"].values())
            po_now = po_c.n
            SE.op("copy", waits=[(pe_c, lastmm), (po_c, po_now)], out=SROW[0:1, 0:TH],
                  in_=pb[stat["bank"][("s", 0)]][0:1, 0:512])
            SE.op("copy", out=SROW[0:1, TH:T],
                  in_=pb[stat["bank"][("s", 1)]][0:1, 0:512])
            SE.op("copy", out=SROW[0:1, T:T + TH],
                  in_=pb[stat["bank"][("q", 0)]][0:1, 0:512])
            v4se = SE.op("copy", out=SROW[0:1, T + TH:2 * T],
                         in_=pb[stat["bank"][("q", 1)]][0:1, 0:512])
            v2 = v4 = 0
            tap_q = 0
            if sums_tap is not None:
                tap_q = SE.dma(qsc, sums_tap, SROW[0:1, 0:2 * T], waits=[(se_c, v4se)])
            bcs = []
            for i in (0, 1):
                bk = stat["bank"][("s", i)]
                mm = PE.op("matmul", waits=[(se_c, v4se)], out=pb[bk][:, 0:512],
                           lhsT=BCON[:, 0:128], rhs=SROW[:, i * TH:(i + 1) * TH],
                           start=True, stop=True)
                bcs.append(SE.op("copy", waits=[(pe_c, mm)],
                                 out=MUBC[:, i * TH:(i + 1) * TH], in_=pb[bk][:, 0:512]))
            mu_bc_mm = bcs[:]
            v5 = VE.op("tensor_tensor",
                       waits=[(se_c, max(bcs)), (qsc, tap_q)],
                       out=SROW[0:1, 0:T],
                       in0=SROW[0:1, 0:T], in1=SROW[0:1, 0:T], op=ALU.mult)
            v6 = VE.op("tensor_tensor", waits=[(ve_c, v5)], out=SROW[0:1, T:2 * T],
                       in0=SROW[0:1, T:2 * T], in1=SROW[0:1, 0:T], op=ALU.subtract)
            v6 = VE.op("tensor_scalar", waits=[(ve_c, v6)], out=SROW[0:1, T:2 * T],
                       in0=SROW[0:1, T:2 * T], scalar1=EPS, scalar2=None, op0=ALU.add)
            s1 = SE.op("activation", waits=[(ve_c, v6)], out=SROW[0:1, 0:T],
                       in_=SROW[0:1, T:2 * T], func=AF.Ln, scale=1.0)
            s2 = SE.op("activation", waits=[(se_c, s1)], out=SROW[0:1, T:2 * T],
                       in_=SROW[0:1, 0:T], func=AF.Exp, scale=-0.5)
            for i in (0, 1):
                bk = stat["bank"][("q", i)]
                mm = PE.op("matmul", waits=[(se_c, s2)], out=pb[bk][:, 0:512],
                           lhsT=BCON[:, 0:128], rhs=SROW[:, T + i * TH:T + (i + 1) * TH],
                           start=True, stop=True)
                bcs.append(SE.op("copy", waits=[(pe_c, mm)],
                                 out=MUBC[:, T + i * TH:T + (i + 1) * TH],
                                 in_=pb[bk][:, 0:512]))
            return max(bcs)

        def apply_ln(Bt, MUBC, bc_ready, cast_events):
            done = {}
            for c in range(DC):
                w = [(se_c, bc_ready), cast_events.get(c, (po_c, 0))]
                v1 = PO.op("tensor_tensor", waits=w, out=Bt[:, c * T:(c + 1) * T],
                           in0=Bt[:, c * T:(c + 1) * T], in1=MUBC[:, 0:T], op=ALU.subtract)
                PO.op("tensor_tensor", waits=[(po_c, v1)],
                      out=Bt[:, c * T:(c + 1) * T],
                      in0=Bt[:, c * T:(c + 1) * T],
                      in1=MUBC[:, T:2 * T], op=ALU.mult)
                done[c] = (po_c, PO.op("tensor_copy", out=FSCR[:, 4:8],
                                       in_=Bt[:, c * T + T - 4:(c + 1) * T]))
            return done

        def emit_ffn(Bt_in, Bt_cast, wv_ap, bias_sb, scale, lora, resid3, write3,
                     stats, ln_ready, XSTG, EVF, XOUT, SQB):
            bar = p.barrier()
            stW = new_wstream(wv_ap, DC)
            stream_weights(stW, 0)
            stream_weights(stW, 1)
            st2 = None
            if stats:
                st2 = {"bank": {("s", 0): 4, ("s", 1): 5, ("q", 0): 6, ("q", 1): 7},
                       "ready": {}, "mm": {}}
            u_evac = 0
            if lora:
                umm = {}
                PE.op("drain", inc=False)
                for h in (0, 1):
                    for dc in range(DC):
                        umm[h] = PE.op("matmul",
                                       waits=[ln_ready[dc], (po_c, lora_w_ready)],
                                       out=pb[h][0:48, 0:512],
                                       lhsT=LAB[:, dc * 48:(dc + 1) * 48],
                                       rhs=Bt_in[:, dc * T + h * TH: dc * T + h * TH + TH],
                                       start=(dc == 0), stop=(dc == DC - 1),
                                       inc=(dc == DC - 1))
                for h in (0, 1):
                    u_evac = VE.op("tensor_scalar",
                                   waits=[(pe_c, umm[h]), (ve_c, sel_ready)],
                                   out=U1S[0:48, h * TH:(h + 1) * TH],
                                   in0=pb[h][0:48, 0:512], scalar1=SELC[0:48, 0:1],
                                   scalar2=None, op0=ALU.mult)
                PE.op("drain", inc=False)
                u_evac_po = PO.op("tensor_copy", waits=[(ve_c, u_evac)],
                                  out=U1S[:], in_=U1S[:])
            evac, stt, xw, castc, xdma, sqd = {}, {}, {}, {}, {}, {}
            xldr = {}
            for oc in range(DC):
                war = max(stt.get((oc - 2, 1), 0), bar["ve"])
                xdma[oc] = SY.dma(qsy, XSTG[:, (oc % 2) * T:(oc % 2 + 1) * T],
                                  resid3[:, oc, :], waits=[(ve_c, war)])
                xldr[oc] = PO.op("tensor_copy", waits=[(qsy, xdma[oc])],
                                 out=XSTG[:, (oc % 2) * T:(oc % 2 + 1) * T],
                                 in_=XSTG[:, (oc % 2) * T:(oc % 2 + 1) * T])
                lastmm = 0
                for h in (0, 1):
                    bk = (2 * oc + h) % 4
                    w = [(po_c, stW["cast"][(oc, 3)]),
                         (se_c, evac.get((oc - 2, h), 0))]
                    if lora and oc < 2:
                        w.append((po_c, u_evac_po))
                    for dc in range(DC):
                        w2 = w + [ln_ready[dc]]
                        last = (dc == DC - 1)
                        lastmm = PE.op("matmul", waits=w2, out=pb[bk][:, 0:512],
                                       lhsT=wtile(oc, dc),
                                       rhs=Bt_in[:, dc * T + h * TH: dc * T + h * TH + TH],
                                       start=(dc == 0), stop=(last and not lora),
                                       inc=(last and not lora))
                        w = []
                    if lora:
                        lastmm = PE.op("matmul", waits=[(po_c, u_evac_po)],
                                       out=pb[bk][:, 0:512],
                                       lhsT=LBB[:, oc * 128:(oc + 1) * 128],
                                       rhs=U1S[:, h * TH:(h + 1) * TH],
                                       start=False, stop=True)
                    eslot = h
                    if ffn_serial >= 2:
                        p.barrier()
                    w = [(pe_c, lastmm), (ve_c, stt.get((oc - 1, h), 0))]
                    evac[(oc, h)] = SE.op(
                        "activation", waits=w,
                        out=EVF[:, eslot * 512:(eslot + 1) * 512],
                        in_=pb[bk][:, 0:512], func=AF.Identity,
                        bias=bias_sb[:, oc:oc + 1], scale=1.0)
                    if ffn_serial >= 2:
                        p.barrier()
                    w = [(se_c, evac[(oc, h)]), (po_c, xldr[oc]),
                         (qsc, xw.get((oc - 1, h), 0)),
                         (po_c, castc.get(oc - 1, 0))]
                    xoslot = h
                    stt[(oc, h)] = VE.op(
                        "scalar_tensor_tensor", waits=w,
                        out=XOUT[:, xoslot * 512:(xoslot + 1) * 512],
                        in0=EVF[:, eslot * 512:(eslot + 1) * 512], scalar=scale,
                        in1=XSTG[:, (oc % 2) * T + h * TH:(oc % 2) * T + h * TH + TH],
                        op0=ALU.mult, op1=ALU.add)
                    stt_f = VE.op("tensor_copy", out=FSCR[:, 0:4],
                                  in_=XOUT[:, xoslot * 512 + 508:xoslot * 512 + 512])
                    stt[(oc, h)] = stt_f
                    if ffn_serial >= 2:
                        p.barrier()
                    xw[(oc, h)] = SE.dma(
                        qsc, write3[:, oc, h * TH:(h + 1) * TH],
                        XOUT[:, xoslot * 512:(xoslot + 1) * 512],
                        waits=[(ve_c, stt_f)])
                    if Bt_cast is not None:
                        cv = PO.op("tensor_copy", waits=[(ve_c, stt[(oc, h)])],
                                   out=Bt_cast[:, oc * T + h * TH: oc * T + h * TH + TH],
                                   in_=XOUT[:, xoslot * 512:(xoslot + 1) * 512])
                        castc[oc] = cv
                        if stats:
                            sq = PO.op(
                                "tensor_tensor",
                                waits=[(po_c, cv),
                                       (pe_c, st2["mm"].get((oc - 2, h, "q"), 0))],
                                out=SQB[:, (oc % 2) * T + h * TH:(oc % 2) * T + h * TH + TH],
                                in0=Bt_cast[:, oc * T + h * TH: oc * T + h * TH + TH],
                                in1=Bt_cast[:, oc * T + h * TH: oc * T + h * TH + TH],
                                op=ALU.mult)
                            st2["ready"][(oc, h, "s")] = (po_c, cv)
                            st2["ready"][(oc, h, "q")] = (po_c, sq)
                if stats:
                    emit_stats_mms(st2, Bt_cast[:, oc * T:(oc + 1) * T],
                                   SQB[:, (oc % 2) * T:(oc % 2 + 1) * T],
                                   first=(oc == 0), last=(oc == DC - 1), c=oc)
                stW["pedone"][oc] = lastmm
                if ffn_serial:
                    p.barrier()
                stream_weights(stW, oc + 2)
            return st2, castc, xw

        with contextlib.ExitStack() as esA:
            XSTG = sb("XSTG", [128, 2 * 1024], F32, esA)
            EVF = sb("EVF", [128, 2 * 512], F32, esA)
            XOUT = sb("XOUT", [128, 2 * 512], F32, esA)
            SQB = sb("SQB", [128, 2 * T], BF16, esA)
            MUBC = sb("MUBC", [128, 2 * T], F32, esA)
            SROW = sb("SROW", [128, 2 * T], F32, esA)
            ms_sr = PO.op("memset", SROW[:], 0.0)

            stat = {"bank": {("s", 0): 4, ("s", 1): 5, ("q", 0): 6, ("q", 1): 7},
                    "ready": {}, "mm": {}}
            cast_done = {}
            slot_free = {}
            for c in range(DC):
                slot = c % 2
                dx = SY.dma(qsy, XSTG[:, slot * T:(slot + 1) * T],
                            xin3[:, c, :],
                            waits=[(po_c, slot_free.get(c - 2, 0))])
                cast_done[c] = PO.op(
                    "tensor_copy", waits=[(qsy, dx)], out=b1c(c),
                    in_=XSTG[:, slot * T:(slot + 1) * T])
                sq = PO.op("tensor_tensor",
                           waits=[(po_c, cast_done[c]),
                                  (pe_c, stat["mm"].get((c - 2, 1, "q"), 0))],
                           out=SQB[:, (c % 2) * T:(c % 2 + 1) * T],
                           in0=b1c(c), in1=b1c(c), op=ALU.mult)
                for h in (0, 1):
                    stat["ready"][(c, h, "s")] = (po_c, cast_done[c])
                    stat["ready"][(c, h, "q")] = (po_c, sq)
                emit_stats_mms(stat, b1c(c), SQB[:, (c % 2) * T:(c % 2 + 1) * T],
                               first=(c == 0), last=(c == DC - 1), c=c)
                slot_free[c] = cast_done[c]
            if "t_b1raw" in tapd:
                SE.dma(qsc, tapd["t_b1raw"], B1[:],
                       waits=[(po_c, max(cast_done.values()))])
            bc_ready = finalize_ln(stat, MUBC, SROW, sums_tap=tapd.get("t_sums"))
            if "t_mu" in tapd:
                SE.dma(qsc, tapd["t_mu"], MUBC[:], waits=[(se_c, bc_ready)])
            ln1_done = apply_ln(B1, MUBC, bc_ready,
                                {c: (po_c, n) for c, n in cast_done.items()})
            if "t_ln1" in tapd:
                SE.dma(qsc, tapd["t_ln1"], B1[:],
                       waits=[(po_c, max(n for _, n in ln1_done.values()))])

            st2, castc_b2, xw1 = emit_ffn(B1, B2, wfv, BFS, 0.5, True, xin3, x1s3,
                                          True, ln1_done, XSTG, EVF, XOUT, SQB)
            if "t_x1" in tapd:
                SE.dma(qsc, tapd["t_x1"], x1s, waits=[(qsc, max(xw1.values()))])
            if "t_u" in tapd:
                SE.dma(qsc, tapd["t_u"], U1S[:])
            bc2 = finalize_ln(st2, MUBC, SROW)
            ln2_done = apply_ln(B2, MUBC, bc2,
                                {c: (po_c, n) for c, n in castc_b2.items()})
            if "t_ln2" in tapd:
                SE.dma(qsc, tapd["t_ln2"], B2[:],
                       waits=[(po_c, max(n for _, n in ln2_done.values()))])

        with contextlib.ExitStack() as esB:
            RSC = sb("RSC", [128, T], F32, esB)
            ROTB = sb("ROTB", [128, T], BF16, esB)
            CQ = sb("CQ", [128, T], F32, esB)
            SQ_ = sb("SQ_", [128, T], F32, esB)
            BVBC = sb("BVBC", [128, 1024], F32, esB)
            KOUT = sb("KOUT", [128, 2 * T], BF16, esB)
            VOWN = sb("VOWN", [128, 8 * T], BF16, esB)

            bar = p.barrier()
            stW = new_wstream(wqv, QKV_OC)
            stream_weights(stW, 0)
            stream_weights(stW, 1)
            d_cq = SY.dma(qsy, CQ[:], cosq, waits=[(ve_c, bar["ve"]), (se_c, bar["se"])])
            d_sq = SY.dma(qsy, SQ_[:], sinq)
            ms_bv = PO.op("memset", BVBC[:], 0.0)
            d_bv = SY.dma(qsy, BVBC[0:1, 0:1024], bv_r, waits=[(po_c, ms_bv)])
            bvb = []
            for i in (0, 1):
                mm = PE.op("matmul", waits=[(qsy, d_bv)], out=pb[6 + i][:, 0:512],
                           lhsT=BCON[:, 0:128], rhs=BVBC[:, i * 512:(i + 1) * 512],
                           start=True, stop=True)
                bvb.append(SE.op("copy", waits=[(pe_c, mm)],
                                 out=BVBC[:, i * 512:(i + 1) * 512],
                                 in_=pb[6 + i][:, 0:512]))
            bvbc_ready = max(bvb)
            ccdma_last = 0
            bank_free_se = {}
            rsc_free_evt = (po_c, 0)
            kout_free = {}
            vevac = {}
            qn2red = {}
            qn2mm = {}
            qrope = {}
            for oc in range(QKV_OC):
                if OC_V0 <= oc < OC_Q0:
                    kvv = oc - OC_V0
                    lastmm = 0
                    for tt in range(8):
                        bk = 4 + (tt % 2)
                        w = [(po_c, stW["cast"][(oc, 3)]),
                             (ve_c, vevac.get((oc, tt - 2), vevac.get((oc - 1, tt + 6), 0)))]
                        for dc in range(DC):
                            w2 = w + [ln2_done[dc]]
                            lastmm = PE.op("matmul", waits=w2, out=pb[bk][:, 0:128],
                                           lhsT=B2[:, dc * T + tt * 128: dc * T + (tt + 1) * 128],
                                           rhs=wtile(oc, dc),
                                           start=(dc == 0), stop=(dc == DC - 1),
                                           inc=(dc == DC - 1))
                            w = []
                        vevac[(oc, tt)] = VE.op(
                            "tensor_tensor",
                            waits=[(pe_c, lastmm), (se_c, bvbc_ready)],
                            out=VOWN[:, tt * T + kvv * 128: tt * T + (kvv + 1) * 128],
                            in0=pb[bk][:, 0:128],
                            in1=BVBC[:, kvv * 128:(kvv + 1) * 128], op=ALU.add)
                    stW["pedone"][oc] = lastmm
                    if oc == OC_V0 + 7:
                        wv_all = max(vevac[(o2, t2)] for o2 in range(OC_V0, OC_V0 + 8)
                                     for t2 in range(8))
                        for tt in range(8):
                            ccdma_last = PO.dma(
                                qgp, ccin[1024 + tt * 128:1024 + (tt + 1) * 128, :],
                                VOWN[:, tt * T:(tt + 1) * T], waits=[(ve_c, wv_all)])
                        cc_done = PO.collective(
                            cc_c, "AllGather", ALU.bypass,
                            [[0, 1], [2, 3], [4, 5], [6, 7]],
                            [ccin.ap().opt()], [ccout.ap().opt()],
                            waits=[(qgp, ccdma_last)])
                else:
                    is_q = oc >= OC_Q0
                    hh = oc - OC_Q0 if is_q else None
                    lastmm = 0
                    for h in (0, 1):
                        bk = (2 * oc + h) % 4
                        w = [(po_c, stW["cast"][(oc, 3)]),
                             (se_c, bank_free_se.get(bk, 0))]
                        for dc in range(DC):
                            w2 = w + [ln2_done[dc]]
                            lastmm = PE.op("matmul", waits=w2, out=pb[bk][:, 0:512],
                                           lhsT=wtile(oc, dc),
                                           rhs=B2[:, dc * T + h * TH: dc * T + h * TH + TH],
                                           start=(dc == 0), stop=(dc == DC - 1),
                                           inc=(dc == DC - 1))
                            w = []
                    ev = []
                    for h in (0, 1):
                        bk = (2 * oc + h) % 4
                        ev.append(SE.op("activation",
                                        waits=[(pe_c, lastmm - (1 - h)),
                                               rsc_free_evt, (ve_c, bqsk_ready)],
                                        out=RSC[:, h * TH:(h + 1) * TH],
                                        in_=pb[bk][:, 0:512], func=AF.Identity,
                                        bias=(BQS[:, oc:oc + 1] if is_q
                                              else BQSK[:, oc:oc + 1]),
                                        scale=(1.0 if is_q else 128.0)))
                        bank_free_se[bk] = ev[h]
                    r1 = SE.op("activation", waits=[(se_c, ev[1])], out=ROTB[0:64, :],
                               in_=RSC[64:128, 0:T], func=AF.Copy, bias=0.0, scale=-1.0)
                    r2 = SE.op("activation", out=ROTB[64:128, :],
                               in_=RSC[0:64, 0:T], func=AF.Copy, bias=0.0, scale=1.0)
                    if is_q:
                        dst = b1c(hh)
                        v1 = VE.op("tensor_tensor",
                                   waits=[(se_c, max(r1, r2)), (qsy, d_sq)],
                                   out=RSC[:, 0:T], in0=RSC[:, 0:T], in1=CQ[:], op=ALU.mult)
                        v2 = VE.op("tensor_tensor", waits=[(ve_c, v1)], out=ROTB[:],
                                   in0=ROTB[:], in1=SQ_[:], op=ALU.mult)
                        v2 = VE.op("tensor_copy", out=FSCR[:, 0:4],
                                   in_=ROTB[:, T - 4:T])
                        v3 = PO.op("tensor_tensor", waits=[(ve_c, v2)], out=dst,
                                   in0=RSC[:, 0:T], in1=ROTB[:], op=ALU.add)
                        qrope[hh] = v3
                        rsc_free_evt = (po_c, v3)
                        sq_ = PO.op("tensor_tensor",
                                    waits=[(po_c, v3), (qgp, kout_free.get(7, 0)),
                                           (pe_c, qn2mm.get(hh - 2, 0))],
                                    out=KOUT[:, (hh % 2) * T:(hh % 2 + 1) * T],
                                    in0=dst, in1=dst, op=ALU.mult)
                        mms = []
                        for h in (0, 1):
                            w = [(po_c, sq_), (ve_c, qn2red.get(hh - 2, 0))]
                            mms.append(PE.op("matmul", waits=w,
                                             out=pb[4 + h][:, 0:512],
                                             lhsT=CSTB[:, 0:128],
                                             rhs=KOUT[:, (hh % 2) * T + h * TH:
                                                     (hh % 2) * T + h * TH + TH],
                                             start=True, stop=True))
                        qn2mm[hh] = mms[1]
                        rd1 = VE.op("reduce_max", waits=[(pe_c, mms[0])],
                                    out=QN2M[0:1, hh * 8:hh * 8 + 4],
                                    in_=pb[4][0:1, 0:512]
                                    .rearrange("a (m t) -> a m t", t=128), axis=AX.X)
                        qn2red[hh] = VE.op("reduce_max", waits=[(pe_c, mms[1])],
                                           out=QN2M[0:1, hh * 8 + 4:hh * 8 + 8],
                                           in_=pb[5][0:1, 0:512]
                                           .rearrange("a (m t) -> a m t", t=128), axis=AX.X)
                    else:
                        slot = oc % 2
                        dst = KOUT[:, slot * T:(slot + 1) * T]
                        w_t = [(se_c, max(r1, r2)), (qsy, d_sq),
                               (qgp, kout_free.get(oc - 2, 0))]
                        v1 = VE.op("tensor_tensor", waits=w_t, out=RSC[:, 0:T],
                                   in0=RSC[:, 0:T], in1=CQ[:], op=ALU.mult)
                        v2 = VE.op("tensor_tensor", waits=[(ve_c, v1)], out=ROTB[:],
                                   in0=ROTB[:], in1=SQ_[:], op=ALU.mult)
                        v2 = VE.op("tensor_copy", out=FSCR[:, 0:4],
                                   in_=ROTB[:, T - 4:T])
                        v3 = PO.op("tensor_tensor", waits=[(ve_c, v2)], out=dst,
                                   in0=RSC[:, 0:T], in1=ROTB[:], op=ALU.add)
                        rsc_free_evt = (po_c, v3)
                        kout_free[oc] = PO.dma(qgp, ccin[oc * 128:(oc + 1) * 128, :], dst,
                                               waits=[(po_c, v3)])
                    stW["pedone"][oc] = lastmm
                stream_weights(stW, oc + 2)
            qkv_po_last = max(qrope.values())
            qkv_ve_last = max(qn2red.values())
            if "t_q" in tapd:
                SE.dma(qsc, tapd["t_q"], B1[:], waits=[(po_c, qkv_po_last)])

        with contextlib.ExitStack() as esC:
            KKV = sb("KKV", [128, 2 * 2048], BF16, esC)
            VKV = sb("VKV", [128, 2 * 2048], BF16, esC)
            PTS = sb("PTS", [128, 4 * 512], BF16, esC)
            MSK = sb("MSK", [128, max(MASK_COLS, 128)], BF16, esC)
            K2S = sb("K2S", [128, 2048], BF16, esC)
            RSB = sb("RSB", [128, 2 * 512], F32, esC)
            NEGC = sb("NEGC", [128, 64], F32, esC)
            RROW = sb("RROW", [128, 512], F32, esC)

            bar = p.barrier()
            ms_rr = PO.op("memset", RROW[:], 0.0)
            stW_d = new_wstream(wdv, DC)
            stream_weights(stW_d, 0)
            stream_weights(stW_d, 1)
            d_msk = SY.dma(qsy, MSK[:, 0:max(MASK_COLS, 128)], masks,
                           waits=[(ve_c, qkv_ve_last), (po_c, max(bar["po"], qkv_po_last)),
                                  (qgp, ccdma_last)])
            cco3 = ccout.ap().rearrange("(r p) t -> p r t", p=128)
            kvdma = {}
            pe_done_kv = {}
            ctx_free = {}
            sums_free = {}
            exp_by_stb = {}
            pts_war = {}

            kv_ldr = {}

            def issue_readback(kv):
                slot = kv % 2
                w = [(cc_c, 1), (ve_c, qkv_ve_last), (po_c, qkv_po_last),
                     (qgp, ccdma_last),
                     (pe_c, pe_done_kv.get(kv - 2, 0)),
                     (se_c, exp_by_stb.get(0, 0)), (se_c, exp_by_stb.get(1, 0))]
                SE.dma(qsc, KKV[:, slot * 2048:slot * 2048 + 1024],
                       cco3[:, kv:kv + 1, :].rearrange("p a t -> p (a t)"), waits=w)
                SE.dma(qsc, KKV[:, slot * 2048 + 1024:slot * 2048 + 2048],
                       cco3[:, 16 + kv:17 + kv, :].rearrange("p a t -> p (a t)"))
                SE.dma(qsc, VKV[:, slot * 2048:slot * 2048 + 1024]
                       .rearrange("p (c n) -> p c n", n=128),
                       cco3[:, 8:16, kv * 128:(kv + 1) * 128])
                last = SE.dma(qsc, VKV[:, slot * 2048 + 1024:slot * 2048 + 2048]
                              .rearrange("p (c n) -> p c n", n=128),
                              cco3[:, 24:32, kv * 128:(kv + 1) * 128])
                l1 = PO.op("tensor_copy", waits=[(qsc, last)],
                           out=KKV[:, slot * 2048:(slot + 1) * 2048],
                           in_=KKV[:, slot * 2048:(slot + 1) * 2048])
                kv_ldr[kv] = PO.op("tensor_copy",
                                   out=VKV[:, slot * 2048:(slot + 1) * 2048],
                                   in_=VKV[:, slot * 2048:(slot + 1) * 2048])
                return last

            kvdma[0] = issue_readback(0)
            kvdma[1] = issue_readback(1)
            kn2_mm_last = 0
            rsb_free = {}
            mmn_last = 0
            cpb_last = 0
            negc_ready = {}
            ctx_evac_all = []
            for kv in range(NKV):
                slot = kv % 2
                Kk = KKV[:, slot * 2048:(slot + 1) * 2048]
                Vk = VKV[:, slot * 2048:(slot + 1) * 2048]
                k2 = PO.op("tensor_tensor",
                           waits=[(po_c, kv_ldr[kv]), (pe_c, kn2_mm_last)],
                           out=K2S[:], in0=Kk, in1=Kk, op=ALU.mult)
                piece_cnt = {m: 0 for m in range(8)}
                vlast = 0
                for j in range(4):
                    bk = 6 + (j % 2)
                    w = [(po_c, k2), (ve_c, vlast), (se_c, cpb_last), (pe_c, mmn_last)]
                    mm = PE.op("matmul", waits=w, out=pb[bk][:, 0:512],
                               lhsT=CSTB[:, 0:128], rhs=K2S[:, j * 512:(j + 1) * 512],
                               start=True, stop=True)
                    kn2_mm_last = mm
                    for m in range(8):
                        for (q, lo, hi) in VIS_RANGES[m]:
                            if q != j:
                                continue
                            if piece_cnt[m] == 0:
                                vlast = VE.op("reduce_max", waits=[(pe_c, mm)],
                                              out=TINY[0:1, 16 + m:17 + m],
                                              in_=pb[bk][0:1, lo:hi], axis=AX.X)
                            else:
                                vlast = VE.op("reduce_max", waits=[(pe_c, mm)],
                                              out=TINY[0:1, 32:33],
                                              in_=pb[bk][0:1, lo:hi], axis=AX.X)
                                vlast = VE.op("tensor_tensor", waits=[(ve_c, vlast)],
                                              out=TINY[0:1, 16 + m:17 + m],
                                              in0=TINY[0:1, 16 + m:17 + m],
                                              in1=TINY[0:1, 32:33], op=ALU.max)
                            piece_cnt[m] += 1
                qv = QN2M[0:1, 0:256].rearrange("a (h m) -> a m h", m=8)
                for m in range(8):
                    qg = VE.op("reduce_max", waits=[(ve_c, qn2red[4 * kv + 3])],
                               out=TINY[0:1, 33:34],
                               in_=qv[0:1, m, 4 * kv:4 * kv + 4], axis=AX.X)
                    vlast = VE.op("tensor_tensor", waits=[(ve_c, max(qg, vlast))],
                                  out=TINY[0:1, 24 + m:25 + m],
                                  in0=TINY[0:1, 16 + m:17 + m],
                                  in1=TINY[0:1, 33:34], op=ALU.mult)
                c2v = VE.op("tensor_scalar", waits=[(ve_c, vlast)], out=TINY[0:1, 24:32],
                            in0=TINY[0:1, 24:32], scalar1=1e-18, scalar2=None,
                            op0=ALU.add)
                s1 = SE.op("activation", waits=[(ve_c, c2v)], out=TINY[0:1, 24:32],
                           in_=TINY[0:1, 24:32], func=AF.Ln, scale=1.0)
                s2 = SE.op("activation", waits=[(se_c, s1)], out=TINY[0:1, 24:32],
                           in_=TINY[0:1, 24:32], func=AF.Exp, scale=0.5)
                mmn = PE.op("matmul", waits=[(se_c, s2)],
                            out=pb[7][:, 0:8], lhsT=BCON[:, 0:128],
                            rhs=TINY[:, 24:32], start=True, stop=True)
                mmn_last = mmn
                negc_ready[kv] = SE.op("activation", waits=[(pe_c, mmn)],
                                       out=NEGC[:, kv * 8:(kv + 1) * 8],
                                       in_=pb[7][:, 0:8], func=AF.Copy,
                                       bias=0.0, scale=-1.0)
                for m in range(8):
                    nu = len(UNITS[m])
                    sums_bk = 2 + (m % 2)
                    ctx_bk = 4 + (m % 2)
                    last_sum = last_pv = 0
                    for ui, (side, s0) in enumerate(UNITS[m]):
                        stb = ui % 2
                        pslot = ui % 4
                        kcol = side * 1024 + s0 * 64
                        vchunk = side * 8 + s0 // 2
                        w = [(po_c, kv_ldr[kv]), (se_c, exp_by_stb.get(stb, 0))]
                        mm_qk = PE.op("matmul", waits=w, out=pb[stb][:, 0:512],
                                      lhsT=Kk[:, kcol:kcol + 128],
                                      rhs=b1v[:, 4 * kv:4 * kv + 4, m * 128:(m + 1) * 128],
                                      start=True, stop=True)
                        w = [(pe_c, max(mm_qk, pts_war.get(pslot, 0))),
                             (se_c, negc_ready[kv]), (qsy, d_msk)]
                        e = SE.op("activation", waits=w,
                                  out=PTS[:, pslot * 512:(pslot + 1) * 512],
                                  in_=pb[stb][:, 0:512], func=AF.Exp,
                                  bias=NEGC[:, kv * 8 + m:kv * 8 + m + 1], scale=1.0)
                        exp_by_stb[stb] = e
                        gate = (se_c, e)
                        if (m, ui) in MASK_FLAG_SET:
                            off = MASK_OFF[(m, ui)]
                            pm = 0
                            for hq in range(4):
                                pm = PO.op("tensor_tensor", waits=[(se_c, e)],
                                           out=PTS[:, pslot * 512 + hq * 128:
                                                   pslot * 512 + (hq + 1) * 128],
                                           in0=PTS[:, pslot * 512 + hq * 128:
                                                   pslot * 512 + (hq + 1) * 128],
                                           in1=MSK[:, off:off + 128], op=ALU.mult)
                            gate = (po_c, pm)
                        w = [gate]
                        if ui == 0:
                            w.append((se_c, sums_free.get(sums_bk, 0)))
                        last_sum = PE.op("matmul", waits=w, out=pb[sums_bk][:, 0:512],
                                         lhsT=CSTB[:, 0:128],
                                         rhs=PTS[:, pslot * 512:(pslot + 1) * 512],
                                         start=(ui == 0), stop=(ui == nu - 1))
                        w = [gate]
                        if ui == 0:
                            w.append((ve_c, ctx_free.get(ctx_bk, 0)))
                        last_pv = PE.op("matmul", waits=w, out=pb[ctx_bk][:, 0:512],
                                        lhsT=Vk[:, vchunk * 128:(vchunk + 1) * 128],
                                        rhs=PTS[:, pslot * 512:(pslot + 1) * 512],
                                        start=(ui == 0), stop=(ui == nu - 1))
                        pts_war[pslot] = last_pv
                        if attn_serial >= 2:
                            p.barrier()
                    t1 = SE.op("activation", waits=[(pe_c, last_sum), (po_c, ms_rr)],
                               out=RROW[0:1, 0:512], in_=pb[sums_bk][0:1, 0:512],
                               func=AF.Ln, scale=2.0 ** 40)
                    sums_free[sums_bk] = t1
                    t2 = SE.op("activation", waits=[(se_c, t1)], out=RROW[0:1, 0:512],
                               in_=RROW[0:1, 0:512], func=AF.Exp, scale=-1.0)
                    t3v = VE.op("tensor_scalar", waits=[(se_c, t2)],
                                out=RROW[0:1, 0:512], in0=RROW[0:1, 0:512],
                                scalar1=1e25, scalar2=None, op0=ALU.min)
                    t3 = PO.op("tensor_copy", waits=[(ve_c, t3v)],
                               out=RROW[0:1, 0:512], in_=RROW[0:1, 0:512])
                    mmb = PE.op("matmul",
                                waits=[(se_c, cpb_last), (po_c, t3)],
                                out=pb[6][:, 0:512], lhsT=BCON[:, 128:256],
                                rhs=RROW[:, 0:512], start=True, stop=True)
                    cpb_last = SE.op("copy",
                                     waits=[(pe_c, mmb),
                                            (ve_c, rsb_free.get(m % 2, 0))],
                                     out=RSB[:, (m % 2) * 512:(m % 2 + 1) * 512],
                                     in_=pb[6][:, 0:512])
                    ev = VE.op("tensor_tensor", waits=[(se_c, cpb_last), (pe_c, last_pv)],
                               out=b2v[:, 4 * kv:4 * kv + 4, m * 128:(m + 1) * 128],
                               in0=pb[ctx_bk][:, 0:512],
                               in1=RSB[:, (m % 2) * 512:(m % 2 + 1) * 512], op=ALU.mult)
                    ctx_free[ctx_bk] = ev
                    rsb_free[m % 2] = ev
                    ctx_evac_all.append(ev)
                    if attn_serial >= 1:
                        p.barrier()
                pe_done_kv[kv] = last_pv
                if kv + 2 < NKV:
                    kvdma[kv + 2] = issue_readback(kv + 2)
            ctx_done = max(ctx_evac_all)
            if "t_kv" in tapd:
                tkv = tapd["t_kv"].rearrange("p (a n) -> p a n", a=2)
                SE.dma(qsc, tkv[:, 0:1, :].rearrange("p a n -> p (a n)"),
                       KKV[:, 0:2048], waits=[(qsc, kvdma[0]), (pe_c, pe_done_kv[1])])
                SE.dma(qsc, tkv[:, 1:2, :].rearrange("p a n -> p (a n)"),
                       VKV[:, 0:2048])
            if "t_negc" in tapd:
                SE.dma(qsc, tapd["t_negc"], NEGC[:],
                       waits=[(se_c, max(negc_ready.values()))])
            if "t_ctx" in tapd:
                SE.dma(qsc, tapd["t_ctx"], B2[:], waits=[(ve_c, ctx_done)])

        with contextlib.ExitStack() as esD:
            XSTG = sb("XSTG2", [128, 2 * 1024], F32, esD)
            EVF = sb("EVF2", [128, 2 * 512], F32, esD)
            XOUT = sb("XOUT2", [128, 2 * 512], F32, esD)
            SQB = sb("SQB2", [128, 2 * T], BF16, esD)
            MUBC = sb("MUBC2", [128, 2 * T], F32, esD)
            SROW = sb("SROW2", [128, 2 * T], F32, esD)
            ms_sr2 = PO.op("memset", SROW[:], 0.0, waits=[(ve_c, ctx_done)])

            b2_po = PO.op("tensor_copy", waits=[(ve_c, ctx_done)],
                          out=B2[:], in_=B2[:])
            ctx_ready = {c: (po_c, b2_po) for c in range(DC)}
            st3, castc_b1, xw2 = emit_ffn(B2, B1, wdv, BDS, 1.0, False, x1s3, x2s3,
                                          True, ctx_ready, XSTG, EVF, XOUT, SQB)
            if "t_x2" in tapd:
                SE.dma(qsc, tapd["t_x2"], x2s, waits=[(qsc, max(xw2.values()))])
            bc3 = finalize_ln(st3, MUBC, SROW)
            ln3_done = apply_ln(B1, MUBC, bc3,
                                {c: (po_c, n) for c, n in castc_b1.items()})
            _, _, xw_out = emit_ffn(B1, None, wfv, BFS, 0.5, True, x2s3, out3,
                                    False, ln3_done, XSTG, EVF, XOUT, SQB)
            final_q = max(xw_out.values())

        for n in ("pe", "ve", "se", "po", "qsy", "qsc", "qgp", "cc"):
            p.ctr[n].h = es.enter_context(nc.semaphore(n))
        with nc.Block() as block:
            @block.tensor
            def _(eng):
                p.pe.replay(eng)

            @block.vector
            def _(eng):
                p.ve.replay(eng)

            @block.scalar
            def _(eng):
                p.se.replay(eng)

            @block.gpsimd
            def _(eng):
                p.po.replay(eng)

            @block.sync
            def _(eng):
                p.sy.replay(eng)
                eng.wait_ge(p.ctr["qsc"].h, final_q)
    return nc, tapd


def _host_prep(inputs):
    x = np.asarray(inputs["L0_attn_hidden_states"])
    w_qkv = np.asarray(inputs["L0_attn_w_qkv"])
    b_qkv = np.asarray(inputs["L0_attn_b_qkv"])
    w_dense = np.asarray(inputs["L0_attn_w_dense"])
    b_dense = np.asarray(inputs["L0_attn_b_dense"])
    w_ffn = np.asarray(inputs["L0_ffn_weight"])
    b_ffn = np.asarray(inputs["L0_ffn_bias"])
    lora_A = np.asarray(inputs["L0_ffn_lora_A"])
    lora_B = np.asarray(inputs["L0_ffn_lora_B"])
    lsi = np.asarray(inputs["L0_ffn_lora_start_indices"]).astype(np.int32)

    perm = []
    for kv in range(8):
        perm.append(np.arange((kv * 6 + 4) * 128, (kv * 6 + 5) * 128))
    for kv in range(8):
        perm.append(np.arange((kv * 6 + 5) * 128, (kv * 6 + 6) * 128))
    for kv in range(8):
        for j in range(4):
            perm.append(np.arange((kv * 6 + j) * 128, (kv * 6 + j + 1) * 128))
    perm = np.concatenate(perm)

    def wstream(Wp, nog):
        return np.ascontiguousarray(
            Wp.reshape(nog, 128, DC, 128).transpose(0, 3, 2, 1)).reshape(-1)

    wq_s = wstream(w_qkv[perm], QKV_OC)
    wd_s = wstream(w_dense, DC)
    wf_s = wstream(w_ffn, DC)
    lA_s = np.ascontiguousarray(lora_A.transpose(2, 0, 1).reshape(D, 48))
    lB_s = np.ascontiguousarray(lora_B.transpose(0, 2, 1).reshape(48, D))
    bq_pk = np.ascontiguousarray(b_qkv[perm].reshape(QKV_OC, 128).T)
    bv_row = np.ascontiguousarray(b_qkv[perm][8 * 128:16 * 128].reshape(1, 1024))
    bd_pk = np.ascontiguousarray(b_dense.reshape(DC, 128).T)
    bf_pk = np.ascontiguousarray(b_ffn.reshape(DC, 128).T)
    rowc = np.concatenate([np.ones((1, 128), np.float32),
                           -np.ones((1, 128), np.float32),
                           np.full((1, 128), 2.0 ** 40, np.float32)], axis=1)
    iotab = (np.arange(48, dtype=np.float32) // 16).reshape(48, 1)
    cstb = np.zeros((128, 130), np.float32)
    cstb[:, 0] = 1.0
    cstb[:, 1] = 1.0 / 4096
    cstb = cstb.astype(ml_dtypes.bfloat16)
    bcon = np.zeros((128, 256), np.float32)
    bcon[0, 0:128] = 1.0
    bcon[0, 128:256] = 2.0 ** 40
    inv = 1.0 / (1e6 ** (np.arange(0, 128, 2, dtype=np.float64) / 128))

    shared = dict(wq=wq_s, wd=wd_s, wf=wf_s, lA=lA_s, lB=lB_s, bq=bq_pk,
                  bv_r=bv_row, bd=bd_pk, bf=bf_pk, rowc=rowc, iotab=iotab,
                  cstb=cstb, bcon=bcon,
                  lsi=np.ascontiguousarray(lsi.reshape(1, 4)))
    per_core, tok_idx_all = [], []
    for c in range(8):
        b = c // 2
        half = c % 2
        blocks = [2 * i + half for i in range(16)]
        tok = np.concatenate([np.arange(bb * 64, (bb + 1) * 64) for bb in blocks])
        tok_idx_all.append(tok)
        xin = np.ascontiguousarray(x[b][tok].T)
        f = np.outer(inv, tok.astype(np.float64))
        cq, sq = np.cos(f), np.sin(f)
        cosq = (np.vstack([cq, cq]) / 128.0).astype(np.float32)
        sinq = (np.vstack([sq, sq]) / 128.0).astype(np.float32)
        mk = np.zeros((128, max(MASK_COLS, 128)), dtype=np.float32)
        for (m, ui) in MASK_FLAGGED:
            side, s0 = UNITS[m][ui]
            mk[:, MASK_OFF[(m, ui)]:MASK_OFF[(m, ui)] + 128] = \
                _unit_mask(half, m, side, s0)
        im = dict(shared)
        im.update(xin=xin, cosq=cosq, sinq=sinq,
                  masks=np.ascontiguousarray(mk.astype(ml_dtypes.bfloat16)),
                  bval=np.array([[float(b)]], dtype=np.float32))
        per_core.append(im)
    return per_core, tok_idx_all


_NC_CACHE = {}


def kernel(**inputs):
    per_core, tok_idx = _host_prep(inputs)
    if "plain" not in _NC_CACHE:
        _NC_CACHE["plain"] = build(())[0]
    nc = _NC_CACHE["plain"]
    res = run_bass_kernel_spmd(nc, per_core, core_ids=list(range(8)))
    x = np.asarray(inputs["L0_attn_hidden_states"])
    full = np.empty_like(x)
    for c in range(8):
        full[c // 2][tok_idx[c]] = res.results[c]["out"].T
    return full


# version 08
# speedup vs baseline: 1.0890x; 1.0890x over previous
import contextlib
import numpy as np
import ml_dtypes
import concourse.bass as bass
import concourse.mybir as mybir
from concourse.bass_utils import run_bass_kernel_spmd

F32 = mybir.dt.float32
BF16 = mybir.dt.bfloat16
I32 = mybir.dt.int32
AF = mybir.ActivationFunctionType
ALU = mybir.AluOpType
AX = mybir.AxisListType

D = 4096
DC = 32
T = 1024
TH = 512
NKV = 8
NQH = 32
NB = 32
EPS = 1e-5
QKV_OC = 48
OC_K0, OC_V0, OC_Q0 = 0, 8, 16

_ALLOWED = [{j for j in range(NB) if j <= i and (i - j < 16 or (j + 1) % 8 == 0)}
            for i in range(NB)]

def _attn_units():
    units_per_m = []
    for m in range(8):
        U = set()
        for h in (0, 1):
            for i in (4 * m + h, 4 * m + 2 + h):
                U |= _ALLOWED[i]
        units = []
        for side in (0, 1):
            slots = sorted({j // 2 for j in U if j % 2 == side})
            runs = []
            for s in slots:
                if runs and s == runs[-1][1] + 1:
                    runs[-1][1] = s
                else:
                    runs.append([s, s])
            ext = []
            for a, b in runs:
                a -= a % 2
                b += (1 - b % 2)
                if ext and a <= ext[-1][1] + 1:
                    ext[-1][1] = max(ext[-1][1], b)
                else:
                    ext.append([a, b])
            for a, b in ext:
                for s in range(a, b + 1, 2):
                    units.append((side, s))
        units_per_m.append(units)
    return units_per_m

UNITS = _attn_units()

def _unit_mask(half, m, side, s0):
    msk = np.zeros((128, 128), dtype=np.float32)
    for kk in (0, 1):
        j = 2 * (s0 + kk) + side
        for t in (0, 1):
            qb = 4 * m + 2 * t + half
            blk = msk[kk * 64:(kk + 1) * 64, t * 64:(t + 1) * 64]
            if j == qb:
                blk[:] = (np.arange(64)[:, None] <= np.arange(64)[None, :])
            elif j in _ALLOWED[qb]:
                blk[:] = 1.0
    return msk

def _mask_plan():
    flagged, off, col = [], {}, 0
    for m in range(8):
        for ui, (side, s0) in enumerate(UNITS[m]):
            m0 = _unit_mask(0, m, side, s0)
            m1 = _unit_mask(1, m, side, s0)
            if not (np.all(m0 == 1.0) and np.all(m1 == 1.0)):
                flagged.append((m, ui))
                off[(m, ui)] = col
                col += 128
    return flagged, off, col

MASK_FLAGGED, MASK_OFF, MASK_COLS = _mask_plan()
MASK_FLAG_SET = set(MASK_FLAGGED)

def _vis_ranges():
    out = []
    for m in range(8):
        cols = []
        for side, s0 in UNITS[m]:
            a = side * 1024 + s0 * 64
            cols.append((a, a + 128))
        cols.sort()
        merged = []
        for a, b in cols:
            if merged and a <= merged[-1][1]:
                merged[-1][1] = max(merged[-1][1], b)
            else:
                merged.append([a, b])
        pieces = []
        for a, b in merged:
            while a < b:
                q = a // 512
                e = min(b, (q + 1) * 512)
                pieces.append((q, a - q * 512, e - q * 512))
                a = e
        out.append(pieces)
    return out

VIS_RANGES = _vis_ranges()

class Ctr:
    def __init__(self, name):
        self.name = name
        self.n = 0
        self.h = None


class Rec:
    def __init__(self, prog, name, ctr):
        self.p = prog
        self.name = name
        self.c = ctr
        self.ops = []

    def _waits(self, waits):
        for (c, n) in waits:
            if c is None or n is None or n <= 0:
                continue
            key = (self.name, c.name)
            if n > self.p.wm.get(key, 0):
                self.p.wm[key] = n
                self.ops.append(("wait", c, n))

    def op(self, fname, *a, waits=(), inc=True, **kw):
        self._waits(waits)
        if inc:
            self.c.n += 1
        self.ops.append(("op", fname, a, kw, self.c if inc else None))
        return self.c.n

    def dma(self, q, out, in_, waits=()):
        self._waits(waits)
        q.n += 16
        self.ops.append(("dma", out, in_, q))
        return q.n

    def collective(self, ccctr, kind, aop, groups, ins, outs, waits=()):
        self._waits(waits)
        ccctr.n += 1
        self.ops.append(("op", "collective_compute", (kind, aop),
                         dict(replica_groups=groups, ins=ins, outs=outs), ccctr))
        return ccctr.n

    def wait_only(self, waits):
        self._waits(waits)

    def replay(self, eng):
        for rec in self.ops:
            if rec[0] == "wait":
                _, c, n = rec
                eng.wait_ge(c.h, n)
            elif rec[0] == "op":
                _, fname, a, kw, ctr = rec
                r = getattr(eng, fname)(*a, **kw)
                if ctr is not None:
                    r.then_inc(ctr.h, 1)
            else:
                _, out, in_, q = rec
                eng.dma_start(out=out, in_=in_).then_inc(q.h, 16)


class Prog:
    def __init__(self):
        self.wm = {}
        self.ctr = {n: Ctr(n) for n in
                    ("pe", "ve", "se", "po", "qsy", "qsc", "qgp", "cc")}
        self.pe = Rec(self, "pe", self.ctr["pe"])
        self.ve = Rec(self, "ve", self.ctr["ve"])
        self.se = Rec(self, "se", self.ctr["se"])
        self.po = Rec(self, "po", self.ctr["po"])
        self.sy = Rec(self, "sy", self.ctr["qsy"])

    def barrier(self):
        recs = [self.pe, self.ve, self.se, self.po]
        counts = [(r.c, r.c.n) for r in recs]
        for r in recs:
            r.wait_only([(c, n) for (c, n) in counts if c is not r.c])
        return {r.name: r.c.n for r in recs}


def build(taps=(), attn_serial=0, ffn_serial=0):
    nc = bass.Bass("TRN2", target_bir_lowering=False)
    dt = nc.dram_tensor
    xin = dt("xin", [D, T], F32, kind="ExternalInput").ap()
    wq = dt("wq", [QKV_OC * 128 * DC * 128], F32, kind="ExternalInput").ap()
    wd = dt("wd", [DC * 128 * DC * 128], F32, kind="ExternalInput").ap()
    wf = dt("wf", [DC * 128 * DC * 128], F32, kind="ExternalInput").ap()
    lA = dt("lA", [DC * 128, 48], F32, kind="ExternalInput").ap()
    lB = dt("lB", [48, D], F32, kind="ExternalInput").ap()
    bq = dt("bq", [128, QKV_OC], F32, kind="ExternalInput").ap()
    bv_r = dt("bv_r", [1, 1024], F32, kind="ExternalInput").ap()
    bd = dt("bd", [128, DC], F32, kind="ExternalInput").ap()
    bf_ = dt("bf", [128, DC], F32, kind="ExternalInput").ap()
    cosq = dt("cosq", [128, T], F32, kind="ExternalInput").ap()
    sinq = dt("sinq", [128, T], F32, kind="ExternalInput").ap()
    masks = dt("masks", [128, max(MASK_COLS, 128)], BF16, kind="ExternalInput").ap()
    rowc = dt("rowc", [1, 384], F32, kind="ExternalInput").ap()
    iotab = dt("iotab", [48, 1], F32, kind="ExternalInput").ap()
    bval = dt("bval", [1, 1], F32, kind="ExternalInput").ap()
    cstb = dt("cstb", [128, 130], BF16, kind="ExternalInput").ap()
    bcon = dt("bcon", [128, 256], F32, kind="ExternalInput").ap()
    lsi = dt("lsi", [1, 4], I32, kind="ExternalInput").ap()
    out = dt("out", [D, T], F32, kind="ExternalOutput").ap()
    x1s = dt("x1s", [D, T], F32).ap()
    x2s = dt("x2s", [D, T], F32).ap()
    ccin = dt("ccin", [2048, 1024], BF16)
    ccout = dt("ccout", [4096, 1024], BF16)

    tapd = {}
    tap_specs = {
        "t_ln1": ([128, DC * T], BF16), "t_x1": ([D, T], F32),
        "t_ln2": ([128, DC * T], BF16), "t_q": ([128, NQH * T], BF16),
        "t_kv": ([128, 4096], BF16), "t_ctx": ([128, NQH * T], BF16),
        "t_x2": ([D, T], F32), "t_sel": ([48, 1], F32),
        "t_negc": ([128, 64], F32), "t_u": ([48, T], BF16),
        "t_mu": ([128, 2 * T], F32), "t_b1raw": ([128, DC * T], BF16),
        "t_sums": ([1, 2 * T], F32),
    }
    for tname in taps:
        shape, tdt = tap_specs[tname]
        tapd[tname] = dt(tname, shape, tdt, kind="ExternalOutput").ap()

    p = Prog()
    pe_c, ve_c, se_c, po_c = (p.ctr[n] for n in ("pe", "ve", "se", "po"))
    qsy, qsc, qgp, cc_c = (p.ctr[n] for n in ("qsy", "qsc", "qgp", "cc"))
    PE, VE, SE, PO, SY = p.pe, p.ve, p.se, p.po, p.sy

    with contextlib.ExitStack() as es:
        def sb(name, shape, dtp, stack=None):
            return (stack or es).enter_context(nc.sbuf_tensor(name, shape, dtp))

        B1 = sb("B1", [128, DC * T], BF16)
        B2 = sb("B2", [128, DC * T], BF16)
        WST = sb("WST", [128, 2 * 1024], F32)
        WBF = sb("WBF", [128, 2 * 4096], BF16)
        LAB = sb("LAB", [128, DC * 48], BF16)
        LBB = sb("LBB", [128, D], BF16)
        BQS = sb("BQS", [128, QKV_OC], F32)
        BDS = sb("BDS", [128, DC], F32)
        BFS = sb("BFS", [128, DC], F32)
        ROWC = sb("ROWC", [1, 384], F32)
        IOTA = sb("IOTA", [48, 1], F32)
        BVAL = sb("BVAL", [1, 1], F32)
        CSTB = sb("CSTB", [128, 130], BF16)
        BCON = sb("BCON", [128, 256], F32)
        LSIS = sb("LSIS", [1, 4], I32)
        SELC = sb("SELC", [48, 1], F32)
        BQSK = sb("BQSK", [128, 8], F32)
        FSCR = sb("FSCR", [128, 8], BF16)
        U1S = sb("U1S", [128, T], BF16)
        TINY = sb("TINY", [128, 48], F32)
        QN2M = sb("QN2M", [128, 256], F32)
        pb = [es.enter_context(nc.psum_tensor(f"pb{i}", [128, 512], F32)) for i in range(8)]

        xin3 = xin.rearrange("(c p) t -> p c t", p=128)
        x1s3 = x1s.rearrange("(c p) t -> p c t", p=128)
        x2s3 = x2s.rearrange("(c p) t -> p c t", p=128)
        out3 = out.rearrange("(c p) t -> p c t", p=128)
        wqv = wq.rearrange("(g p n) -> g p n", g=QKV_OC, p=128)
        wdv = wd.rearrange("(g p n) -> g p n", g=DC, p=128)
        wfv = wf.rearrange("(g p n) -> g p n", g=DC, p=128)
        b1v = B1.rearrange("p (hh t) -> p hh t", t=T)
        b2v = B2.rearrange("p (hh t) -> p hh t", t=T)

        def b1c(c):
            return B1[:, c * T:(c + 1) * T]

        def b2c(c):
            return B2[:, c * T:(c + 1) * T]

        d_rowc = SY.dma(qsy, ROWC[:], rowc)
        d_iota = SY.dma(qsy, IOTA[:], iotab)
        d_bval = SY.dma(qsy, BVAL[:], bval)
        d_cstb = SY.dma(qsy, CSTB[:], cstb)
        d_bcon = SY.dma(qsy, BCON[:], bcon)
        d_lsi = SY.dma(qsy, LSIS[:], lsi)
        SY.dma(qsy, BQS[:], bq)
        SY.dma(qsy, BDS[:], bd)
        d_bias = SY.dma(qsy, BFS[:], bf_)
        d_lA = SY.dma(qsy, WST[:, 0:DC * 48].rearrange("p (c r) -> p c r", r=48),
                      lA.rearrange("(c p) r -> p c r", p=128))
        ms1 = PO.op("memset", TINY[:], 0.0)
        ms2 = PO.op("memset", QN2M[:], 0.0)
        ms3 = PO.op("memset", U1S[:], 0.0)
        c_lA = PO.op("tensor_copy", waits=[(qsy, d_lA)], out=LAB[:], in_=WST[:, 0:DC * 48])
        c_lB = PO.op("memset", LBB[:], 0.0, waits=[(po_c, c_lA)])
        for hb in (0, 1):
            d_lB = SY.dma(qsy, WST[0:48, 0:2048], lB[:, hb * 2048:(hb + 1) * 2048],
                          waits=[(po_c, c_lB)])
            c_lB = PO.op("tensor_copy", waits=[(qsy, d_lB)],
                         out=LBB[0:48, hb * 2048:(hb + 1) * 2048], in_=WST[0:48, 0:2048])
        lora_w_ready = c_lB
        bqsk_ready = VE.op("tensor_scalar", waits=[(qsy, d_bias)], out=BQSK[:],
                           in0=BQS[:, 0:8], scalar1=128.0, scalar2=None, op0=ALU.mult)
        v = VE.op("tensor_copy", waits=[(qsy, d_lsi), (po_c, ms3)],
                  out=TINY[0:1, 0:4], in_=LSIS[:])
        v = VE.op("tensor_scalar", waits=[(ve_c, v), (qsy, d_bval)], out=TINY[0:1, 0:4],
                  in0=TINY[0:1, 0:4], scalar1=BVAL[0:1, 0:1], scalar2=None, op0=ALU.is_le)
        v = VE.op("reduce_sum", waits=[(ve_c, v)], out=TINY[0:1, 4:5],
                  in_=TINY[0:1, 0:4], axis=AX.X)
        v_use = VE.op("tensor_scalar", waits=[(ve_c, v)], out=TINY[0:1, 5:6],
                      in0=TINY[0:1, 4:5], scalar1=2.0, scalar2=None, op0=ALU.is_ge)
        v_a = VE.op("tensor_scalar", waits=[(ve_c, v)], out=TINY[0:1, 6:7],
                    in0=TINY[0:1, 4:5], scalar1=2.0, scalar2=0.0,
                    op0=ALU.subtract, op1=ALU.max)
        v_a = VE.op("tensor_scalar", waits=[(ve_c, v_a)], out=TINY[0:1, 6:7],
                    in0=TINY[0:1, 6:7], scalar1=2.0, scalar2=None, op0=ALU.min)
        sel_po = PO.op("tensor_copy", waits=[(ve_c, max(v_use, v_a))],
                       out=TINY[0:1, 5:7], in_=TINY[0:1, 5:7])
        PE.op("drain", inc=False)
        m_bc = PE.op("matmul", waits=[(po_c, sel_po), (qsy, d_bcon)],
                     out=pb[7][0:48, 0:2], lhsT=BCON[:, 0:48], rhs=TINY[:, 5:7],
                     start=True, stop=True)
        PE.op("drain", inc=False)
        v = VE.op("tensor_tensor", waits=[(pe_c, m_bc), (qsy, d_iota)], out=SELC[:],
                  in0=IOTA[:], in1=pb[7][0:48, 1:2], op=ALU.is_equal)
        sel_ready = VE.op("tensor_tensor", waits=[(ve_c, v)], out=SELC[:],
                          in0=SELC[:], in1=pb[7][0:48, 0:1], op=ALU.mult)
        if "t_sel" in tapd:
            SE.dma(qsc, tapd["t_sel"], SELC[:], waits=[(ve_c, sel_ready)])

        def new_wstream(wv_ap, nog):
            return {"issued": set(), "cast": {}, "pedone": {}, "nog": nog, "wv": wv_ap,
                    "slot_cast": {}, "war_po": max(po_c.n, c_lB), "war_pe": pe_c.n}

        def stream_weights(st, og):
            if og in st["issued"] or og >= st["nog"]:
                return
            st["issued"].add(og)
            wv = st["wv"]
            for q in range(4):
                s_ = q % 2
                w = [(po_c, max(st["slot_cast"].get(s_, 0), st["war_po"]))]
                dq = SY.dma(qsy, WST[:, s_ * 1024:(s_ + 1) * 1024],
                            wv[og, :, q * 1024:(q + 1) * 1024], waits=w)
                w = [(qsy, dq),
                     (pe_c, max(st["pedone"].get(og - 2, 0), st["war_pe"]))]
                PO.op("tensor_copy", waits=w, out=FSCR[:, 0:8], in_=FSCR[:, 0:8])
                st["cast"][(og, q)] = PO.op(
                    "tensor_copy",
                    out=WBF[:, (og % 2) * 4096 + q * 1024:(og % 2) * 4096 + (q + 1) * 1024],
                    in_=WST[:, s_ * 1024:(s_ + 1) * 1024])
                st["slot_cast"][s_] = st["cast"][(og, q)]
            st["cast"][(og, 3)] = PO.op(
                "tensor_copy",
                out=FSCR[:, 0:4],
                in_=WBF[:, (og % 2) * 4096 + 1022:(og % 2) * 4096 + 1026]
                if False else
                WBF.rearrange("p (q n) -> p q n", n=1024)
                [:, (og % 2) * 4 + 3:(og % 2) * 4 + 4, 1020:1024]
                .rearrange("p a n -> p (a n)"))

        def wtile(og, dc):
            return WBF[:, (og % 2) * 4096 + dc * 128:(og % 2) * 4096 + (dc + 1) * 128]

        def emit_stats_mms(stat, rhs_bf, sq_bf, first, last, c):
            for h in (0, 1):
                for which, rhs in (("s", rhs_bf), ("q", sq_bf)):
                    bk = stat["bank"][(which, h)]
                    w = [stat["ready"][(c, h, which)]]
                    stat["mm"][(c, h, which)] = PE.op(
                        "matmul", waits=w, out=pb[bk][:, 0:512],
                        lhsT=CSTB[:, 1:129], rhs=rhs[:, h * TH:(h + 1) * TH],
                        start=first, stop=last)

        def finalize_ln(stat, MUBC, SROW, sums_tap=None):
            lastmm = max(stat["mm"].values())
            po_now = po_c.n
            SE.op("copy", waits=[(pe_c, lastmm), (po_c, po_now)], out=SROW[0:1, 0:TH],
                  in_=pb[stat["bank"][("s", 0)]][0:1, 0:512])
            SE.op("copy", out=SROW[0:1, TH:T],
                  in_=pb[stat["bank"][("s", 1)]][0:1, 0:512])
            SE.op("copy", out=SROW[0:1, T:T + TH],
                  in_=pb[stat["bank"][("q", 0)]][0:1, 0:512])
            v4se = SE.op("copy", out=SROW[0:1, T + TH:2 * T],
                         in_=pb[stat["bank"][("q", 1)]][0:1, 0:512])
            v2 = v4 = 0
            tap_q = 0
            if sums_tap is not None:
                tap_q = SE.dma(qsc, sums_tap, SROW[0:1, 0:2 * T], waits=[(se_c, v4se)])
            bcs = []
            for i in (0, 1):
                bk = stat["bank"][("s", i)]
                mm = PE.op("matmul", waits=[(se_c, v4se)], out=pb[bk][:, 0:512],
                           lhsT=BCON[:, 0:128], rhs=SROW[:, i * TH:(i + 1) * TH],
                           start=True, stop=True)
                bcs.append(SE.op("copy", waits=[(pe_c, mm)],
                                 out=MUBC[:, i * TH:(i + 1) * TH], in_=pb[bk][:, 0:512]))
            mu_bc_mm = bcs[:]
            v5 = VE.op("tensor_tensor",
                       waits=[(se_c, max(bcs)), (qsc, tap_q)],
                       out=SROW[0:1, 0:T],
                       in0=SROW[0:1, 0:T], in1=SROW[0:1, 0:T], op=ALU.mult)
            v6 = VE.op("tensor_tensor", waits=[(ve_c, v5)], out=SROW[0:1, T:2 * T],
                       in0=SROW[0:1, T:2 * T], in1=SROW[0:1, 0:T], op=ALU.subtract)
            v6 = VE.op("tensor_scalar", waits=[(ve_c, v6)], out=SROW[0:1, T:2 * T],
                       in0=SROW[0:1, T:2 * T], scalar1=EPS, scalar2=None, op0=ALU.add)
            s1 = SE.op("activation", waits=[(ve_c, v6)], out=SROW[0:1, 0:T],
                       in_=SROW[0:1, T:2 * T], func=AF.Ln, scale=1.0)
            s2 = SE.op("activation", waits=[(se_c, s1)], out=SROW[0:1, T:2 * T],
                       in_=SROW[0:1, 0:T], func=AF.Exp, scale=-0.5)
            for i in (0, 1):
                bk = stat["bank"][("q", i)]
                mm = PE.op("matmul", waits=[(se_c, s2)], out=pb[bk][:, 0:512],
                           lhsT=BCON[:, 0:128], rhs=SROW[:, T + i * TH:T + (i + 1) * TH],
                           start=True, stop=True)
                bcs.append(SE.op("copy", waits=[(pe_c, mm)],
                                 out=MUBC[:, T + i * TH:T + (i + 1) * TH],
                                 in_=pb[bk][:, 0:512]))
            return max(bcs)

        def apply_ln(Bt, MUBC, bc_ready, cast_events):
            done = {}
            for c in range(DC):
                w = [(se_c, bc_ready), cast_events.get(c, (po_c, 0))]
                v1 = PO.op("tensor_tensor", waits=w, out=Bt[:, c * T:(c + 1) * T],
                           in0=Bt[:, c * T:(c + 1) * T], in1=MUBC[:, 0:T], op=ALU.subtract)
                PO.op("tensor_tensor", waits=[(po_c, v1)],
                      out=Bt[:, c * T:(c + 1) * T],
                      in0=Bt[:, c * T:(c + 1) * T],
                      in1=MUBC[:, T:2 * T], op=ALU.mult)
                done[c] = (po_c, PO.op("tensor_copy", out=FSCR[:, 4:8],
                                       in_=Bt[:, c * T + T - 4:(c + 1) * T]))
            return done

        def emit_ffn(Bt_in, Bt_cast, wv_ap, bias_sb, scale, lora, resid3, write3,
                     stats, ln_ready, XSTG, EVF, XOUT, SQB):
            bar = p.barrier()
            stW = new_wstream(wv_ap, DC)
            stream_weights(stW, 0)
            stream_weights(stW, 1)
            st2 = None
            if stats:
                st2 = {"bank": {("s", 0): 4, ("s", 1): 5, ("q", 0): 6, ("q", 1): 7},
                       "ready": {}, "mm": {}}
            u_evac = 0
            if lora:
                umm = {}
                PE.op("drain", inc=False)
                for h in (0, 1):
                    for dc in range(DC):
                        umm[h] = PE.op("matmul",
                                       waits=[ln_ready[dc], (po_c, lora_w_ready)],
                                       out=pb[h][0:48, 0:512],
                                       lhsT=LAB[:, dc * 48:(dc + 1) * 48],
                                       rhs=Bt_in[:, dc * T + h * TH: dc * T + h * TH + TH],
                                       start=(dc == 0), stop=(dc == DC - 1),
                                       inc=(dc == DC - 1))
                for h in (0, 1):
                    u_evac = VE.op("tensor_scalar",
                                   waits=[(pe_c, umm[h]), (ve_c, sel_ready)],
                                   out=U1S[0:48, h * TH:(h + 1) * TH],
                                   in0=pb[h][0:48, 0:512], scalar1=SELC[0:48, 0:1],
                                   scalar2=None, op0=ALU.mult)
                PE.op("drain", inc=False)
                u_evac_po = PO.op("tensor_copy", waits=[(ve_c, u_evac)],
                                  out=U1S[:], in_=U1S[:])
            evac, stt, xw, castc, xdma, sqd = {}, {}, {}, {}, {}, {}
            xldr = {}
            for oc in range(DC):
                war = max(stt.get((oc - 2, 1), 0), bar["ve"])
                xdma[oc] = SY.dma(qsy, XSTG[:, (oc % 2) * T:(oc % 2 + 1) * T],
                                  resid3[:, oc, :], waits=[(ve_c, war)])
                xldr[oc] = PO.op("tensor_copy", waits=[(qsy, xdma[oc])],
                                 out=XSTG[:, (oc % 2) * T:(oc % 2 + 1) * T],
                                 in_=XSTG[:, (oc % 2) * T:(oc % 2 + 1) * T])
                lastmm = 0
                for h in (0, 1):
                    bk = (2 * oc + h) % 4
                    w = [(po_c, stW["cast"][(oc, 3)]),
                         (se_c, evac.get((oc - 2, h), 0))]
                    if lora and oc < 2:
                        w.append((po_c, u_evac_po))
                    for dc in range(DC):
                        w2 = w + [ln_ready[dc]]
                        last = (dc == DC - 1)
                        lastmm = PE.op("matmul", waits=w2, out=pb[bk][:, 0:512],
                                       lhsT=wtile(oc, dc),
                                       rhs=Bt_in[:, dc * T + h * TH: dc * T + h * TH + TH],
                                       start=(dc == 0), stop=(last and not lora),
                                       inc=(last and not lora))
                        w = []
                    if lora:
                        lastmm = PE.op("matmul", waits=[(po_c, u_evac_po)],
                                       out=pb[bk][:, 0:512],
                                       lhsT=LBB[:, oc * 128:(oc + 1) * 128],
                                       rhs=U1S[:, h * TH:(h + 1) * TH],
                                       start=False, stop=True)
                    eslot = h
                    if ffn_serial >= 2:
                        p.barrier()
                    w = [(pe_c, lastmm), (ve_c, stt.get((oc - 1, h), 0))]
                    evac[(oc, h)] = SE.op(
                        "activation", waits=w,
                        out=EVF[:, eslot * 512:(eslot + 1) * 512],
                        in_=pb[bk][:, 0:512], func=AF.Identity,
                        bias=bias_sb[:, oc:oc + 1], scale=1.0)
                    if ffn_serial >= 2:
                        p.barrier()
                    w = [(se_c, evac[(oc, h)]), (po_c, xldr[oc]),
                         (qsc, xw.get((oc - 1, h), 0)),
                         (po_c, castc.get(oc - 1, 0))]
                    xoslot = h
                    stt[(oc, h)] = VE.op(
                        "scalar_tensor_tensor", waits=w,
                        out=XOUT[:, xoslot * 512:(xoslot + 1) * 512],
                        in0=EVF[:, eslot * 512:(eslot + 1) * 512], scalar=scale,
                        in1=XSTG[:, (oc % 2) * T + h * TH:(oc % 2) * T + h * TH + TH],
                        op0=ALU.mult, op1=ALU.add)
                    stt_f = VE.op("tensor_copy", out=FSCR[:, 0:4],
                                  in_=XOUT[:, xoslot * 512 + 508:xoslot * 512 + 512])
                    stt[(oc, h)] = stt_f
                    if ffn_serial >= 2:
                        p.barrier()
                    xw[(oc, h)] = SE.dma(
                        qsc, write3[:, oc, h * TH:(h + 1) * TH],
                        XOUT[:, xoslot * 512:(xoslot + 1) * 512],
                        waits=[(ve_c, stt_f)])
                    if Bt_cast is not None:
                        cv = PO.op("tensor_copy", waits=[(ve_c, stt[(oc, h)])],
                                   out=Bt_cast[:, oc * T + h * TH: oc * T + h * TH + TH],
                                   in_=XOUT[:, xoslot * 512:(xoslot + 1) * 512])
                        castc[oc] = cv
                        if stats:
                            sq = PO.op(
                                "tensor_tensor",
                                waits=[(po_c, cv),
                                       (pe_c, st2["mm"].get((oc - 2, h, "q"), 0))],
                                out=SQB[:, (oc % 2) * T + h * TH:(oc % 2) * T + h * TH + TH],
                                in0=Bt_cast[:, oc * T + h * TH: oc * T + h * TH + TH],
                                in1=Bt_cast[:, oc * T + h * TH: oc * T + h * TH + TH],
                                op=ALU.mult)
                            st2["ready"][(oc, h, "s")] = (po_c, cv)
                            st2["ready"][(oc, h, "q")] = (po_c, sq)
                if stats:
                    emit_stats_mms(st2, Bt_cast[:, oc * T:(oc + 1) * T],
                                   SQB[:, (oc % 2) * T:(oc % 2 + 1) * T],
                                   first=(oc == 0), last=(oc == DC - 1), c=oc)
                stW["pedone"][oc] = lastmm
                if ffn_serial:
                    p.barrier()
                stream_weights(stW, oc + 2)
            return st2, castc, xw

        with contextlib.ExitStack() as esA:
            XSTG = sb("XSTG", [128, 2 * 1024], F32, esA)
            EVF = sb("EVF", [128, 2 * 512], F32, esA)
            XOUT = sb("XOUT", [128, 2 * 512], F32, esA)
            SQB = sb("SQB", [128, 2 * T], BF16, esA)
            MUBC = sb("MUBC", [128, 2 * T], F32, esA)
            SROW = sb("SROW", [128, 2 * T], F32, esA)
            ms_sr = PO.op("memset", SROW[:], 0.0)

            stat = {"bank": {("s", 0): 4, ("s", 1): 5, ("q", 0): 6, ("q", 1): 7},
                    "ready": {}, "mm": {}}
            cast_done = {}
            slot_free = {}
            for c in range(DC):
                slot = c % 2
                dx = SY.dma(qsy, XSTG[:, slot * T:(slot + 1) * T],
                            xin3[:, c, :],
                            waits=[(po_c, slot_free.get(c - 2, 0))])
                cast_done[c] = PO.op(
                    "tensor_copy", waits=[(qsy, dx)], out=b1c(c),
                    in_=XSTG[:, slot * T:(slot + 1) * T])
                sq = PO.op("tensor_tensor",
                           waits=[(po_c, cast_done[c]),
                                  (pe_c, stat["mm"].get((c - 2, 1, "q"), 0))],
                           out=SQB[:, (c % 2) * T:(c % 2 + 1) * T],
                           in0=b1c(c), in1=b1c(c), op=ALU.mult)
                for h in (0, 1):
                    stat["ready"][(c, h, "s")] = (po_c, cast_done[c])
                    stat["ready"][(c, h, "q")] = (po_c, sq)
                emit_stats_mms(stat, b1c(c), SQB[:, (c % 2) * T:(c % 2 + 1) * T],
                               first=(c == 0), last=(c == DC - 1), c=c)
                slot_free[c] = cast_done[c]
            if "t_b1raw" in tapd:
                SE.dma(qsc, tapd["t_b1raw"], B1[:],
                       waits=[(po_c, max(cast_done.values()))])
            bc_ready = finalize_ln(stat, MUBC, SROW, sums_tap=tapd.get("t_sums"))
            if "t_mu" in tapd:
                SE.dma(qsc, tapd["t_mu"], MUBC[:], waits=[(se_c, bc_ready)])
            ln1_done = apply_ln(B1, MUBC, bc_ready,
                                {c: (po_c, n) for c, n in cast_done.items()})
            if "t_ln1" in tapd:
                SE.dma(qsc, tapd["t_ln1"], B1[:],
                       waits=[(po_c, max(n for _, n in ln1_done.values()))])

            st2, castc_b2, xw1 = emit_ffn(B1, B2, wfv, BFS, 0.5, True, xin3, x1s3,
                                          True, ln1_done, XSTG, EVF, XOUT, SQB)
            if "t_x1" in tapd:
                SE.dma(qsc, tapd["t_x1"], x1s, waits=[(qsc, max(xw1.values()))])
            if "t_u" in tapd:
                SE.dma(qsc, tapd["t_u"], U1S[:])
            bc2 = finalize_ln(st2, MUBC, SROW)
            ln2_done = apply_ln(B2, MUBC, bc2,
                                {c: (po_c, n) for c, n in castc_b2.items()})
            if "t_ln2" in tapd:
                SE.dma(qsc, tapd["t_ln2"], B2[:],
                       waits=[(po_c, max(n for _, n in ln2_done.values()))])

        with contextlib.ExitStack() as esB:
            RSC = sb("RSC", [128, T], F32, esB)
            ROTB = sb("ROTB", [128, T], BF16, esB)
            CQ = sb("CQ", [128, T], F32, esB)
            SQ_ = sb("SQ_", [128, T], F32, esB)
            BVBC = sb("BVBC", [128, 1024], F32, esB)
            KOUT = sb("KOUT", [128, 2 * T], BF16, esB)
            VOWN = sb("VOWN", [128, 8 * T], BF16, esB)

            bar = p.barrier()
            stW = new_wstream(wqv, QKV_OC)
            stream_weights(stW, 0)
            stream_weights(stW, 1)
            d_cq = SY.dma(qsy, CQ[:], cosq, waits=[(ve_c, bar["ve"]), (se_c, bar["se"])])
            d_sq = SY.dma(qsy, SQ_[:], sinq)
            ms_bv = PO.op("memset", BVBC[:], 0.0)
            d_bv = SY.dma(qsy, BVBC[0:1, 0:1024], bv_r, waits=[(po_c, ms_bv)])
            bvb = []
            for i in (0, 1):
                mm = PE.op("matmul", waits=[(qsy, d_bv)], out=pb[6 + i][:, 0:512],
                           lhsT=BCON[:, 0:128], rhs=BVBC[:, i * 512:(i + 1) * 512],
                           start=True, stop=True)
                bvb.append(SE.op("copy", waits=[(pe_c, mm)],
                                 out=BVBC[:, i * 512:(i + 1) * 512],
                                 in_=pb[6 + i][:, 0:512]))
            bvbc_ready = max(bvb)
            ccdma_last = 0
            bank_free_se = {}
            rsc_free_evt = (po_c, 0)
            kout_free = {}
            vevac = {}
            qn2red = {}
            qn2mm = {}
            qrope = {}
            for oc in range(QKV_OC):
                if OC_V0 <= oc < OC_Q0:
                    kvv = oc - OC_V0
                    lastmm = 0
                    for tt in range(8):
                        bk = 4 + (tt % 2)
                        w = [(po_c, stW["cast"][(oc, 3)]),
                             (ve_c, vevac.get((oc, tt - 2), vevac.get((oc - 1, tt + 6), 0)))]
                        for dc in range(DC):
                            w2 = w + [ln2_done[dc]]
                            lastmm = PE.op("matmul", waits=w2, out=pb[bk][:, 0:128],
                                           lhsT=B2[:, dc * T + tt * 128: dc * T + (tt + 1) * 128],
                                           rhs=wtile(oc, dc),
                                           start=(dc == 0), stop=(dc == DC - 1),
                                           inc=(dc == DC - 1))
                            w = []
                        vevac[(oc, tt)] = VE.op(
                            "tensor_tensor",
                            waits=[(pe_c, lastmm), (se_c, bvbc_ready)],
                            out=VOWN[:, tt * T + kvv * 128: tt * T + (kvv + 1) * 128],
                            in0=pb[bk][:, 0:128],
                            in1=BVBC[:, kvv * 128:(kvv + 1) * 128], op=ALU.add)
                    stW["pedone"][oc] = lastmm
                    if oc == OC_V0 + 7:
                        wv_all = max(vevac[(o2, t2)] for o2 in range(OC_V0, OC_V0 + 8)
                                     for t2 in range(8))
                        for tt in range(8):
                            ccdma_last = PO.dma(
                                qgp, ccin[1024 + tt * 128:1024 + (tt + 1) * 128, :],
                                VOWN[:, tt * T:(tt + 1) * T], waits=[(ve_c, wv_all)])
                        cc_done = PO.collective(
                            cc_c, "AllGather", ALU.bypass,
                            [[0, 1], [2, 3], [4, 5], [6, 7]],
                            [ccin.ap().opt()], [ccout.ap().opt()],
                            waits=[(qgp, ccdma_last)])
                else:
                    is_q = oc >= OC_Q0
                    hh = oc - OC_Q0 if is_q else None
                    lastmm = 0
                    for h in (0, 1):
                        bk = (2 * oc + h) % 4
                        w = [(po_c, stW["cast"][(oc, 3)]),
                             (se_c, bank_free_se.get(bk, 0))]
                        for dc in range(DC):
                            w2 = w + [ln2_done[dc]]
                            lastmm = PE.op("matmul", waits=w2, out=pb[bk][:, 0:512],
                                           lhsT=wtile(oc, dc),
                                           rhs=B2[:, dc * T + h * TH: dc * T + h * TH + TH],
                                           start=(dc == 0), stop=(dc == DC - 1),
                                           inc=(dc == DC - 1))
                            w = []
                    ev = []
                    for h in (0, 1):
                        bk = (2 * oc + h) % 4
                        ev.append(SE.op("activation",
                                        waits=[(pe_c, lastmm - (1 - h)),
                                               rsc_free_evt, (ve_c, bqsk_ready)],
                                        out=RSC[:, h * TH:(h + 1) * TH],
                                        in_=pb[bk][:, 0:512], func=AF.Identity,
                                        bias=(BQS[:, oc:oc + 1] if is_q
                                              else BQSK[:, oc:oc + 1]),
                                        scale=(1.0 if is_q else 128.0)))
                        bank_free_se[bk] = ev[h]
                    r1 = SE.op("activation", waits=[(se_c, ev[1])], out=ROTB[0:64, :],
                               in_=RSC[64:128, 0:T], func=AF.Copy, bias=0.0, scale=-1.0)
                    r2 = SE.op("activation", out=ROTB[64:128, :],
                               in_=RSC[0:64, 0:T], func=AF.Copy, bias=0.0, scale=1.0)
                    if is_q:
                        dst = b1c(hh)
                        v1 = VE.op("tensor_tensor",
                                   waits=[(se_c, max(r1, r2)), (qsy, d_sq)],
                                   out=RSC[:, 0:T], in0=RSC[:, 0:T], in1=CQ[:], op=ALU.mult)
                        v2 = VE.op("tensor_tensor", waits=[(ve_c, v1)], out=ROTB[:],
                                   in0=ROTB[:], in1=SQ_[:], op=ALU.mult)
                        v2 = VE.op("tensor_copy", out=FSCR[:, 0:4],
                                   in_=ROTB[:, T - 4:T])
                        v3 = PO.op("tensor_tensor", waits=[(ve_c, v2)], out=dst,
                                   in0=RSC[:, 0:T], in1=ROTB[:], op=ALU.add)
                        qrope[hh] = v3
                        rsc_free_evt = (po_c, v3)
                        sq_ = PO.op("tensor_tensor",
                                    waits=[(po_c, v3), (qgp, kout_free.get(7, 0)),
                                           (pe_c, qn2mm.get(hh - 2, 0))],
                                    out=KOUT[:, (hh % 2) * T:(hh % 2 + 1) * T],
                                    in0=dst, in1=dst, op=ALU.mult)
                        mms = []
                        for h in (0, 1):
                            w = [(po_c, sq_), (ve_c, qn2red.get(hh - 2, 0))]
                            mms.append(PE.op("matmul", waits=w,
                                             out=pb[4 + h][:, 0:512],
                                             lhsT=CSTB[:, 0:128],
                                             rhs=KOUT[:, (hh % 2) * T + h * TH:
                                                     (hh % 2) * T + h * TH + TH],
                                             start=True, stop=True))
                        qn2mm[hh] = mms[1]
                        rd1 = VE.op("reduce_max", waits=[(pe_c, mms[0])],
                                    out=QN2M[0:1, hh * 8:hh * 8 + 4],
                                    in_=pb[4][0:1, 0:512]
                                    .rearrange("a (m t) -> a m t", t=128), axis=AX.X)
                        qn2red[hh] = VE.op("reduce_max", waits=[(pe_c, mms[1])],
                                           out=QN2M[0:1, hh * 8 + 4:hh * 8 + 8],
                                           in_=pb[5][0:1, 0:512]
                                           .rearrange("a (m t) -> a m t", t=128), axis=AX.X)
                    else:
                        slot = oc % 2
                        dst = KOUT[:, slot * T:(slot + 1) * T]
                        w_t = [(se_c, max(r1, r2)), (qsy, d_sq),
                               (qgp, kout_free.get(oc - 2, 0))]
                        v1 = VE.op("tensor_tensor", waits=w_t, out=RSC[:, 0:T],
                                   in0=RSC[:, 0:T], in1=CQ[:], op=ALU.mult)
                        v2 = VE.op("tensor_tensor", waits=[(ve_c, v1)], out=ROTB[:],
                                   in0=ROTB[:], in1=SQ_[:], op=ALU.mult)
                        v2 = VE.op("tensor_copy", out=FSCR[:, 0:4],
                                   in_=ROTB[:, T - 4:T])
                        v3 = PO.op("tensor_tensor", waits=[(ve_c, v2)], out=dst,
                                   in0=RSC[:, 0:T], in1=ROTB[:], op=ALU.add)
                        rsc_free_evt = (po_c, v3)
                        kout_free[oc] = PO.dma(qgp, ccin[oc * 128:(oc + 1) * 128, :], dst,
                                               waits=[(po_c, v3)])
                    stW["pedone"][oc] = lastmm
                stream_weights(stW, oc + 2)
            qkv_po_last = max(qrope.values())
            qkv_ve_last = max(qn2red.values())
            if "t_q" in tapd:
                SE.dma(qsc, tapd["t_q"], B1[:], waits=[(po_c, qkv_po_last)])

        with contextlib.ExitStack() as esC:
            KKV = sb("KKV", [128, 2 * 2048], BF16, esC)
            VKV = sb("VKV", [128, 2 * 2048], BF16, esC)
            PTS = sb("PTS", [128, 4 * 512], BF16, esC)
            MSK = sb("MSK", [128, max(MASK_COLS, 128)], BF16, esC)
            K2S = sb("K2S", [128, 2048], BF16, esC)
            RSB = sb("RSB", [128, 2 * 512], F32, esC)
            NEGC = sb("NEGC", [128, 64], F32, esC)
            RROW = sb("RROW", [128, 512], F32, esC)

            bar = p.barrier()
            ms_rr = PO.op("memset", RROW[:], 0.0)
            stW_d = new_wstream(wdv, DC)
            stream_weights(stW_d, 0)
            stream_weights(stW_d, 1)
            d_msk = SY.dma(qsy, MSK[:, 0:max(MASK_COLS, 128)], masks,
                           waits=[(ve_c, qkv_ve_last), (po_c, max(bar["po"], qkv_po_last)),
                                  (qgp, ccdma_last)])
            cco3 = ccout.ap().rearrange("(r p) t -> p r t", p=128)
            kvdma = {}
            pe_done_kv = {}
            ctx_free = {}
            sums_free = {}
            exp_by_stb = {}
            pts_war = {}

            kv_ldr = {}

            def issue_readback(kv):
                slot = kv % 2
                w = [(cc_c, 1), (ve_c, qkv_ve_last), (po_c, qkv_po_last),
                     (qgp, ccdma_last),
                     (pe_c, pe_done_kv.get(kv - 2, 0)),
                     (se_c, exp_by_stb.get(0, 0)), (se_c, exp_by_stb.get(1, 0))]
                SE.dma(qsc, KKV[:, slot * 2048:slot * 2048 + 1024],
                       cco3[:, kv:kv + 1, :].rearrange("p a t -> p (a t)"), waits=w)
                SE.dma(qsc, KKV[:, slot * 2048 + 1024:slot * 2048 + 2048],
                       cco3[:, 16 + kv:17 + kv, :].rearrange("p a t -> p (a t)"))
                SE.dma(qsc, VKV[:, slot * 2048:slot * 2048 + 1024]
                       .rearrange("p (c n) -> p c n", n=128),
                       cco3[:, 8:16, kv * 128:(kv + 1) * 128])
                last = SE.dma(qsc, VKV[:, slot * 2048 + 1024:slot * 2048 + 2048]
                              .rearrange("p (c n) -> p c n", n=128),
                              cco3[:, 24:32, kv * 128:(kv + 1) * 128])
                l1 = PO.op("tensor_copy", waits=[(qsc, last)],
                           out=KKV[:, slot * 2048:(slot + 1) * 2048],
                           in_=KKV[:, slot * 2048:(slot + 1) * 2048])
                kv_ldr[kv] = PO.op("tensor_copy",
                                   out=VKV[:, slot * 2048:(slot + 1) * 2048],
                                   in_=VKV[:, slot * 2048:(slot + 1) * 2048])
                return last

            kvdma[0] = issue_readback(0)
            kvdma[1] = issue_readback(1)
            kn2_mm_last = 0
            rsb_free = {}
            mmn_last = 0
            cpb_last = 0
            negc_ready = {}
            ctx_evac_all = []
            for kv in range(NKV):
                slot = kv % 2
                Kk = KKV[:, slot * 2048:(slot + 1) * 2048]
                Vk = VKV[:, slot * 2048:(slot + 1) * 2048]
                k2 = PO.op("tensor_tensor",
                           waits=[(po_c, kv_ldr[kv]), (pe_c, kn2_mm_last)],
                           out=K2S[:], in0=Kk, in1=Kk, op=ALU.mult)
                piece_cnt = {m: 0 for m in range(8)}
                vlast = 0
                for j in range(4):
                    bk = 6 + (j % 2)
                    w = [(po_c, k2), (ve_c, vlast), (se_c, cpb_last), (pe_c, mmn_last)]
                    mm = PE.op("matmul", waits=w, out=pb[bk][:, 0:512],
                               lhsT=CSTB[:, 0:128], rhs=K2S[:, j * 512:(j + 1) * 512],
                               start=True, stop=True)
                    kn2_mm_last = mm
                    for m in range(8):
                        for (q, lo, hi) in VIS_RANGES[m]:
                            if q != j:
                                continue
                            if piece_cnt[m] == 0:
                                vlast = VE.op("reduce_max", waits=[(pe_c, mm)],
                                              out=TINY[0:1, 16 + m:17 + m],
                                              in_=pb[bk][0:1, lo:hi], axis=AX.X)
                            else:
                                vlast = VE.op("reduce_max", waits=[(pe_c, mm)],
                                              out=TINY[0:1, 32:33],
                                              in_=pb[bk][0:1, lo:hi], axis=AX.X)
                                vlast = VE.op("tensor_tensor", waits=[(ve_c, vlast)],
                                              out=TINY[0:1, 16 + m:17 + m],
                                              in0=TINY[0:1, 16 + m:17 + m],
                                              in1=TINY[0:1, 32:33], op=ALU.max)
                            piece_cnt[m] += 1
                qv = QN2M[0:1, 0:256].rearrange("a (h m) -> a m h", m=8)
                for m in range(8):
                    qg = VE.op("reduce_max", waits=[(ve_c, qn2red[4 * kv + 3])],
                               out=TINY[0:1, 33:34],
                               in_=qv[0:1, m, 4 * kv:4 * kv + 4], axis=AX.X)
                    vlast = VE.op("tensor_tensor", waits=[(ve_c, max(qg, vlast))],
                                  out=TINY[0:1, 24 + m:25 + m],
                                  in0=TINY[0:1, 16 + m:17 + m],
                                  in1=TINY[0:1, 33:34], op=ALU.mult)
                c2v = VE.op("tensor_scalar", waits=[(ve_c, vlast)], out=TINY[0:1, 24:32],
                            in0=TINY[0:1, 24:32], scalar1=1e-18, scalar2=None,
                            op0=ALU.add)
                s1 = SE.op("activation", waits=[(ve_c, c2v)], out=TINY[0:1, 24:32],
                           in_=TINY[0:1, 24:32], func=AF.Ln, scale=1.0)
                s2 = SE.op("activation", waits=[(se_c, s1)], out=TINY[0:1, 24:32],
                           in_=TINY[0:1, 24:32], func=AF.Exp, scale=0.5)
                mmn = PE.op("matmul", waits=[(se_c, s2)],
                            out=pb[7][:, 0:8], lhsT=BCON[:, 0:128],
                            rhs=TINY[:, 24:32], start=True, stop=True)
                mmn_last = mmn
                negc_ready[kv] = SE.op("activation", waits=[(pe_c, mmn)],
                                       out=NEGC[:, kv * 8:(kv + 1) * 8],
                                       in_=pb[7][:, 0:8], func=AF.Copy,
                                       bias=0.0, scale=-1.0)
                for m in range(8):
                    nu = len(UNITS[m])
                    sums_bk = 2 + (m % 2)
                    ctx_bk = 4 + (m % 2)
                    last_sum = last_pv = 0
                    for ui, (side, s0) in enumerate(UNITS[m]):
                        stb = ui % 2
                        pslot = ui % 4
                        kcol = side * 1024 + s0 * 64
                        vchunk = side * 8 + s0 // 2
                        w = [(po_c, kv_ldr[kv]), (se_c, exp_by_stb.get(stb, 0))]
                        mm_qk = PE.op("matmul", waits=w, out=pb[stb][:, 0:512],
                                      lhsT=Kk[:, kcol:kcol + 128],
                                      rhs=b1v[:, 4 * kv:4 * kv + 4, m * 128:(m + 1) * 128],
                                      start=True, stop=True)
                        w = [(pe_c, max(mm_qk, pts_war.get(pslot, 0))),
                             (se_c, negc_ready[kv]), (qsy, d_msk)]
                        e = SE.op("activation", waits=w,
                                  out=PTS[:, pslot * 512:(pslot + 1) * 512],
                                  in_=pb[stb][:, 0:512], func=AF.Exp,
                                  bias=NEGC[:, kv * 8 + m:kv * 8 + m + 1], scale=1.0)
                        exp_by_stb[stb] = e
                        gate = (se_c, e)
                        if (m, ui) in MASK_FLAG_SET:
                            off = MASK_OFF[(m, ui)]
                            pm = 0
                            for hq in range(4):
                                pm = PO.op("tensor_tensor", waits=[(se_c, e)],
                                           out=PTS[:, pslot * 512 + hq * 128:
                                                   pslot * 512 + (hq + 1) * 128],
                                           in0=PTS[:, pslot * 512 + hq * 128:
                                                   pslot * 512 + (hq + 1) * 128],
                                           in1=MSK[:, off:off + 128], op=ALU.mult)
                            gate = (po_c, pm)
                        w = [gate]
                        if ui == 0:
                            w.append((se_c, sums_free.get(sums_bk, 0)))
                        last_sum = PE.op("matmul", waits=w, out=pb[sums_bk][:, 0:512],
                                         lhsT=CSTB[:, 0:128],
                                         rhs=PTS[:, pslot * 512:(pslot + 1) * 512],
                                         start=(ui == 0), stop=(ui == nu - 1))
                        w = [gate]
                        if ui == 0:
                            w.append((ve_c, ctx_free.get(ctx_bk, 0)))
                        last_pv = PE.op("matmul", waits=w, out=pb[ctx_bk][:, 0:512],
                                        lhsT=Vk[:, vchunk * 128:(vchunk + 1) * 128],
                                        rhs=PTS[:, pslot * 512:(pslot + 1) * 512],
                                        start=(ui == 0), stop=(ui == nu - 1))
                        pts_war[pslot] = last_pv
                        if attn_serial >= 2:
                            p.barrier()
                    t1 = SE.op("activation", waits=[(pe_c, last_sum), (po_c, ms_rr)],
                               out=RROW[0:1, 0:512], in_=pb[sums_bk][0:1, 0:512],
                               func=AF.Ln, scale=2.0 ** 40)
                    sums_free[sums_bk] = t1
                    t2 = SE.op("activation", waits=[(se_c, t1)], out=RROW[0:1, 0:512],
                               in_=RROW[0:1, 0:512], func=AF.Exp, scale=-1.0)
                    mmb = PE.op("matmul",
                                waits=[(se_c, max(cpb_last, t2))],
                                out=pb[6][:, 0:512], lhsT=BCON[:, 128:256],
                                rhs=RROW[:, 0:512], start=True, stop=True)
                    cpb_last = SE.op("copy",
                                     waits=[(pe_c, mmb),
                                            (ve_c, rsb_free.get(m % 2, 0))],
                                     out=RSB[:, (m % 2) * 512:(m % 2 + 1) * 512],
                                     in_=pb[6][:, 0:512])
                    ev = VE.op("tensor_tensor", waits=[(se_c, cpb_last), (pe_c, last_pv)],
                               out=b2v[:, 4 * kv:4 * kv + 4, m * 128:(m + 1) * 128],
                               in0=pb[ctx_bk][:, 0:512],
                               in1=RSB[:, (m % 2) * 512:(m % 2 + 1) * 512], op=ALU.mult)
                    ctx_free[ctx_bk] = ev
                    rsb_free[m % 2] = ev
                    ctx_evac_all.append(ev)
                    if attn_serial >= 1:
                        p.barrier()
                pe_done_kv[kv] = last_pv
                if kv + 2 < NKV:
                    kvdma[kv + 2] = issue_readback(kv + 2)
            ctx_done = max(ctx_evac_all)
            if "t_kv" in tapd:
                tkv = tapd["t_kv"].rearrange("p (a n) -> p a n", a=2)
                SE.dma(qsc, tkv[:, 0:1, :].rearrange("p a n -> p (a n)"),
                       KKV[:, 0:2048], waits=[(qsc, kvdma[0]), (pe_c, pe_done_kv[1])])
                SE.dma(qsc, tkv[:, 1:2, :].rearrange("p a n -> p (a n)"),
                       VKV[:, 0:2048])
            if "t_negc" in tapd:
                SE.dma(qsc, tapd["t_negc"], NEGC[:],
                       waits=[(se_c, max(negc_ready.values()))])
            if "t_ctx" in tapd:
                SE.dma(qsc, tapd["t_ctx"], B2[:], waits=[(ve_c, ctx_done)])

        with contextlib.ExitStack() as esD:
            XSTG = sb("XSTG2", [128, 2 * 1024], F32, esD)
            EVF = sb("EVF2", [128, 2 * 512], F32, esD)
            XOUT = sb("XOUT2", [128, 2 * 512], F32, esD)
            SQB = sb("SQB2", [128, 2 * T], BF16, esD)
            MUBC = sb("MUBC2", [128, 2 * T], F32, esD)
            SROW = sb("SROW2", [128, 2 * T], F32, esD)
            ms_sr2 = PO.op("memset", SROW[:], 0.0, waits=[(ve_c, ctx_done)])

            b2_po = PO.op("tensor_copy", waits=[(ve_c, ctx_done)],
                          out=B2[:], in_=B2[:])
            ctx_ready = {c: (po_c, b2_po) for c in range(DC)}
            st3, castc_b1, xw2 = emit_ffn(B2, B1, wdv, BDS, 1.0, False, x1s3, x2s3,
                                          True, ctx_ready, XSTG, EVF, XOUT, SQB)
            if "t_x2" in tapd:
                SE.dma(qsc, tapd["t_x2"], x2s, waits=[(qsc, max(xw2.values()))])
            bc3 = finalize_ln(st3, MUBC, SROW)
            ln3_done = apply_ln(B1, MUBC, bc3,
                                {c: (po_c, n) for c, n in castc_b1.items()})
            _, _, xw_out = emit_ffn(B1, None, wfv, BFS, 0.5, True, x2s3, out3,
                                    False, ln3_done, XSTG, EVF, XOUT, SQB)
            final_q = max(xw_out.values())

        for n in ("pe", "ve", "se", "po", "qsy", "qsc", "qgp", "cc"):
            p.ctr[n].h = es.enter_context(nc.semaphore(n))
        with nc.Block() as block:
            @block.tensor
            def _(eng):
                p.pe.replay(eng)

            @block.vector
            def _(eng):
                p.ve.replay(eng)

            @block.scalar
            def _(eng):
                p.se.replay(eng)

            @block.gpsimd
            def _(eng):
                p.po.replay(eng)

            @block.sync
            def _(eng):
                p.sy.replay(eng)
                eng.wait_ge(p.ctr["qsc"].h, final_q)
    return nc, tapd


def _host_prep(inputs):
    x = np.asarray(inputs["L0_attn_hidden_states"])
    w_qkv = np.asarray(inputs["L0_attn_w_qkv"])
    b_qkv = np.asarray(inputs["L0_attn_b_qkv"])
    w_dense = np.asarray(inputs["L0_attn_w_dense"])
    b_dense = np.asarray(inputs["L0_attn_b_dense"])
    w_ffn = np.asarray(inputs["L0_ffn_weight"])
    b_ffn = np.asarray(inputs["L0_ffn_bias"])
    lora_A = np.asarray(inputs["L0_ffn_lora_A"])
    lora_B = np.asarray(inputs["L0_ffn_lora_B"])
    lsi = np.asarray(inputs["L0_ffn_lora_start_indices"]).astype(np.int32)

    perm = []
    for kv in range(8):
        perm.append(np.arange((kv * 6 + 4) * 128, (kv * 6 + 5) * 128))
    for kv in range(8):
        perm.append(np.arange((kv * 6 + 5) * 128, (kv * 6 + 6) * 128))
    for kv in range(8):
        for j in range(4):
            perm.append(np.arange((kv * 6 + j) * 128, (kv * 6 + j + 1) * 128))
    perm = np.concatenate(perm)

    def wstream(Wp, nog):
        return np.ascontiguousarray(
            Wp.reshape(nog, 128, DC, 128).transpose(0, 3, 2, 1)).reshape(-1)

    wq_s = wstream(w_qkv[perm], QKV_OC)
    wd_s = wstream(w_dense, DC)
    wf_s = wstream(w_ffn, DC)
    lA_s = np.ascontiguousarray(lora_A.transpose(2, 0, 1).reshape(D, 48))
    lB_s = np.ascontiguousarray(lora_B.transpose(0, 2, 1).reshape(48, D))
    bq_pk = np.ascontiguousarray(b_qkv[perm].reshape(QKV_OC, 128).T)
    bv_row = np.ascontiguousarray(b_qkv[perm][8 * 128:16 * 128].reshape(1, 1024))
    bd_pk = np.ascontiguousarray(b_dense.reshape(DC, 128).T)
    bf_pk = np.ascontiguousarray(b_ffn.reshape(DC, 128).T)
    rowc = np.concatenate([np.ones((1, 128), np.float32),
                           -np.ones((1, 128), np.float32),
                           np.full((1, 128), 2.0 ** 40, np.float32)], axis=1)
    iotab = (np.arange(48, dtype=np.float32) // 16).reshape(48, 1)
    cstb = np.zeros((128, 130), np.float32)
    cstb[:, 0] = 1.0
    cstb[:, 1] = 1.0 / 4096
    cstb = cstb.astype(ml_dtypes.bfloat16)
    bcon = np.zeros((128, 256), np.float32)
    bcon[0, 0:128] = 1.0
    bcon[0, 128:256] = 2.0 ** 40
    inv = 1.0 / (1e6 ** (np.arange(0, 128, 2, dtype=np.float64) / 128))

    shared = dict(wq=wq_s, wd=wd_s, wf=wf_s, lA=lA_s, lB=lB_s, bq=bq_pk,
                  bv_r=bv_row, bd=bd_pk, bf=bf_pk, rowc=rowc, iotab=iotab,
                  cstb=cstb, bcon=bcon,
                  lsi=np.ascontiguousarray(lsi.reshape(1, 4)))
    per_core, tok_idx_all = [], []
    for c in range(8):
        b = c // 2
        half = c % 2
        blocks = [2 * i + half for i in range(16)]
        tok = np.concatenate([np.arange(bb * 64, (bb + 1) * 64) for bb in blocks])
        tok_idx_all.append(tok)
        xin = np.ascontiguousarray(x[b][tok].T)
        f = np.outer(inv, tok.astype(np.float64))
        cq, sq = np.cos(f), np.sin(f)
        cosq = (np.vstack([cq, cq]) / 128.0).astype(np.float32)
        sinq = (np.vstack([sq, sq]) / 128.0).astype(np.float32)
        mk = np.zeros((128, max(MASK_COLS, 128)), dtype=np.float32)
        for (m, ui) in MASK_FLAGGED:
            side, s0 = UNITS[m][ui]
            mk[:, MASK_OFF[(m, ui)]:MASK_OFF[(m, ui)] + 128] = \
                _unit_mask(half, m, side, s0)
        im = dict(shared)
        im.update(xin=xin, cosq=cosq, sinq=sinq,
                  masks=np.ascontiguousarray(mk.astype(ml_dtypes.bfloat16)),
                  bval=np.array([[float(b)]], dtype=np.float32))
        per_core.append(im)
    return per_core, tok_idx_all


_NC_CACHE = {}


def kernel(**inputs):
    per_core, tok_idx = _host_prep(inputs)
    if "plain" not in _NC_CACHE:
        _NC_CACHE["plain"] = build(())[0]
    nc = _NC_CACHE["plain"]
    res = run_bass_kernel_spmd(nc, per_core, core_ids=list(range(8)))
    x = np.asarray(inputs["L0_attn_hidden_states"])
    full = np.empty_like(x)
    for c in range(8):
        full[c // 2][tok_idx[c]] = res.results[c]["out"].T
    return full


# version 09
# speedup vs baseline: 1.1138x; 1.0227x over previous
import contextlib
import numpy as np
import ml_dtypes
import concourse.bass as bass
import concourse.mybir as mybir
from concourse.bass_utils import run_bass_kernel_spmd

F32 = mybir.dt.float32
BF16 = mybir.dt.bfloat16
I32 = mybir.dt.int32
AF = mybir.ActivationFunctionType
ALU = mybir.AluOpType
AX = mybir.AxisListType

D = 4096
DC = 32
T = 1024
TH = 512
NKV = 8
NQH = 32
NB = 32
EPS = 1e-5
QKV_OC = 48
OC_K0, OC_V0, OC_Q0 = 0, 8, 16

_ALLOWED = [{j for j in range(NB) if j <= i and (i - j < 16 or (j + 1) % 8 == 0)}
            for i in range(NB)]

def _attn_units():
    units_per_m = []
    for m in range(8):
        U = set()
        for h in (0, 1):
            for i in (4 * m + h, 4 * m + 2 + h):
                U |= _ALLOWED[i]
        units = []
        for side in (0, 1):
            slots = sorted({j // 2 for j in U if j % 2 == side})
            runs = []
            for s in slots:
                if runs and s == runs[-1][1] + 1:
                    runs[-1][1] = s
                else:
                    runs.append([s, s])
            ext = []
            for a, b in runs:
                a -= a % 2
                b += (1 - b % 2)
                if ext and a <= ext[-1][1] + 1:
                    ext[-1][1] = max(ext[-1][1], b)
                else:
                    ext.append([a, b])
            for a, b in ext:
                for s in range(a, b + 1, 2):
                    units.append((side, s))
        units_per_m.append(units)
    return units_per_m

UNITS = _attn_units()

def _unit_mask(half, m, side, s0):
    msk = np.zeros((128, 128), dtype=np.float32)
    for kk in (0, 1):
        j = 2 * (s0 + kk) + side
        for t in (0, 1):
            qb = 4 * m + 2 * t + half
            blk = msk[kk * 64:(kk + 1) * 64, t * 64:(t + 1) * 64]
            if j == qb:
                blk[:] = (np.arange(64)[:, None] <= np.arange(64)[None, :])
            elif j in _ALLOWED[qb]:
                blk[:] = 1.0
    return msk

def _mask_plan():
    flagged, off, col = [], {}, 0
    for m in range(8):
        for ui, (side, s0) in enumerate(UNITS[m]):
            m0 = _unit_mask(0, m, side, s0)
            m1 = _unit_mask(1, m, side, s0)
            if not (np.all(m0 == 1.0) and np.all(m1 == 1.0)):
                flagged.append((m, ui))
                off[(m, ui)] = col
                col += 128
    return flagged, off, col

MASK_FLAGGED, MASK_OFF, MASK_COLS = _mask_plan()
MASK_FLAG_SET = set(MASK_FLAGGED)

def _vis_ranges():
    out = []
    for m in range(8):
        cols = []
        for side, s0 in UNITS[m]:
            a = side * 1024 + s0 * 64
            cols.append((a, a + 128))
        cols.sort()
        merged = []
        for a, b in cols:
            if merged and a <= merged[-1][1]:
                merged[-1][1] = max(merged[-1][1], b)
            else:
                merged.append([a, b])
        pieces = []
        for a, b in merged:
            while a < b:
                q = a // 512
                e = min(b, (q + 1) * 512)
                pieces.append((q, a - q * 512, e - q * 512))
                a = e
        out.append(pieces)
    return out

VIS_RANGES = _vis_ranges()

class Ctr:
    def __init__(self, name):
        self.name = name
        self.n = 0
        self.h = None


class Rec:
    def __init__(self, prog, name, ctr):
        self.p = prog
        self.name = name
        self.c = ctr
        self.ops = []

    def _waits(self, waits):
        for (c, n) in waits:
            if c is None or n is None or n <= 0:
                continue
            key = (self.name, c.name)
            if n > self.p.wm.get(key, 0):
                self.p.wm[key] = n
                self.ops.append(("wait", c, n))

    def op(self, fname, *a, waits=(), inc=True, **kw):
        self._waits(waits)
        if inc:
            self.c.n += 1
        self.ops.append(("op", fname, a, kw, self.c if inc else None))
        return self.c.n

    def dma(self, q, out, in_, waits=()):
        self._waits(waits)
        q.n += 16
        self.ops.append(("dma", out, in_, q))
        return q.n

    def collective(self, ccctr, kind, aop, groups, ins, outs, waits=()):
        self._waits(waits)
        ccctr.n += 1
        self.ops.append(("op", "collective_compute", (kind, aop),
                         dict(replica_groups=groups, ins=ins, outs=outs), ccctr))
        return ccctr.n

    def wait_only(self, waits):
        self._waits(waits)

    def replay(self, eng):
        for rec in self.ops:
            if rec[0] == "wait":
                _, c, n = rec
                eng.wait_ge(c.h, n)
            elif rec[0] == "op":
                _, fname, a, kw, ctr = rec
                r = getattr(eng, fname)(*a, **kw)
                if ctr is not None:
                    r.then_inc(ctr.h, 1)
            else:
                _, out, in_, q = rec
                eng.dma_start(out=out, in_=in_).then_inc(q.h, 16)


class Prog:
    def __init__(self):
        self.wm = {}
        self.ctr = {n: Ctr(n) for n in
                    ("pe", "ve", "se", "po", "qsy", "qsc", "qgp", "cc")}
        self.pe = Rec(self, "pe", self.ctr["pe"])
        self.ve = Rec(self, "ve", self.ctr["ve"])
        self.se = Rec(self, "se", self.ctr["se"])
        self.po = Rec(self, "po", self.ctr["po"])
        self.sy = Rec(self, "sy", self.ctr["qsy"])

    def barrier(self):
        recs = [self.pe, self.ve, self.se, self.po]
        counts = [(r.c, r.c.n) for r in recs]
        for r in recs:
            r.wait_only([(c, n) for (c, n) in counts if c is not r.c])
        return {r.name: r.c.n for r in recs}


def build(taps=(), attn_serial=0, ffn_serial=0):
    nc = bass.Bass("TRN2", target_bir_lowering=False)
    dt = nc.dram_tensor
    xin = dt("xin", [D, T], F32, kind="ExternalInput").ap()
    wq = dt("wq", [QKV_OC * 128 * DC * 128], F32, kind="ExternalInput").ap()
    wd = dt("wd", [DC * 128 * DC * 128], F32, kind="ExternalInput").ap()
    wf = dt("wf", [DC * 128 * DC * 128], F32, kind="ExternalInput").ap()
    lA = dt("lA", [DC * 128, 48], F32, kind="ExternalInput").ap()
    lB = dt("lB", [48, D], F32, kind="ExternalInput").ap()
    bq = dt("bq", [128, QKV_OC], F32, kind="ExternalInput").ap()
    bv_r = dt("bv_r", [1, 1024], F32, kind="ExternalInput").ap()
    bd = dt("bd", [128, DC], F32, kind="ExternalInput").ap()
    bf_ = dt("bf", [128, DC], F32, kind="ExternalInput").ap()
    cosq = dt("cosq", [128, T], F32, kind="ExternalInput").ap()
    sinq = dt("sinq", [128, T], F32, kind="ExternalInput").ap()
    masks = dt("masks", [128, max(MASK_COLS, 128)], BF16, kind="ExternalInput").ap()
    rowc = dt("rowc", [1, 384], F32, kind="ExternalInput").ap()
    iotab = dt("iotab", [48, 1], F32, kind="ExternalInput").ap()
    bval = dt("bval", [1, 1], F32, kind="ExternalInput").ap()
    cstb = dt("cstb", [128, 130], BF16, kind="ExternalInput").ap()
    bcon = dt("bcon", [128, 256], F32, kind="ExternalInput").ap()
    lsi = dt("lsi", [1, 4], I32, kind="ExternalInput").ap()
    out = dt("out", [D, T], F32, kind="ExternalOutput").ap()
    x1s = dt("x1s", [D, T], F32).ap()
    x2s = dt("x2s", [D, T], F32).ap()
    ccin = dt("ccin", [2048, 1024], BF16)
    ccout = dt("ccout", [4096, 1024], BF16)

    tapd = {}
    tap_specs = {
        "t_ln1": ([128, DC * T], BF16), "t_x1": ([D, T], F32),
        "t_ln2": ([128, DC * T], BF16), "t_q": ([128, NQH * T], BF16),
        "t_kv": ([128, 4096], BF16), "t_ctx": ([128, NQH * T], BF16),
        "t_x2": ([D, T], F32), "t_sel": ([48, 1], F32),
        "t_negc": ([128, 64], F32), "t_u": ([48, T], BF16),
        "t_mu": ([128, 2 * T], F32), "t_b1raw": ([128, DC * T], BF16),
        "t_sums": ([1, 2 * T], F32),
    }
    for tname in taps:
        shape, tdt = tap_specs[tname]
        tapd[tname] = dt(tname, shape, tdt, kind="ExternalOutput").ap()

    p = Prog()
    pe_c, ve_c, se_c, po_c = (p.ctr[n] for n in ("pe", "ve", "se", "po"))
    qsy, qsc, qgp, cc_c = (p.ctr[n] for n in ("qsy", "qsc", "qgp", "cc"))
    PE, VE, SE, PO, SY = p.pe, p.ve, p.se, p.po, p.sy

    with contextlib.ExitStack() as es:
        def sb(name, shape, dtp, stack=None):
            return (stack or es).enter_context(nc.sbuf_tensor(name, shape, dtp))

        B1 = sb("B1", [128, DC * T], BF16)
        B2 = sb("B2", [128, DC * T], BF16)
        WST = sb("WST", [128, 2 * 1024], F32)
        WBF = sb("WBF", [128, 2 * 4096], BF16)
        LAB = sb("LAB", [128, DC * 48], BF16)
        LBB = sb("LBB", [128, D], BF16)
        BQS = sb("BQS", [128, QKV_OC], F32)
        BDS = sb("BDS", [128, DC], F32)
        BFS = sb("BFS", [128, DC], F32)
        ROWC = sb("ROWC", [1, 384], F32)
        IOTA = sb("IOTA", [48, 1], F32)
        BVAL = sb("BVAL", [1, 1], F32)
        CSTB = sb("CSTB", [128, 130], BF16)
        BCON = sb("BCON", [128, 256], F32)
        LSIS = sb("LSIS", [1, 4], I32)
        SELC = sb("SELC", [48, 1], F32)
        BQSK = sb("BQSK", [128, 8], F32)
        FSCR = sb("FSCR", [128, 8], BF16)
        U1S = sb("U1S", [128, T], BF16)
        TINY = sb("TINY", [128, 48], F32)
        QN2M = sb("QN2M", [128, 256], F32)
        pb = [es.enter_context(nc.psum_tensor(f"pb{i}", [128, 512], F32)) for i in range(8)]

        xin3 = xin.rearrange("(c p) t -> p c t", p=128)
        x1s3 = x1s.rearrange("(c p) t -> p c t", p=128)
        x2s3 = x2s.rearrange("(c p) t -> p c t", p=128)
        out3 = out.rearrange("(c p) t -> p c t", p=128)
        wqv = wq.rearrange("(g p n) -> g p n", g=QKV_OC, p=128)
        wdv = wd.rearrange("(g p n) -> g p n", g=DC, p=128)
        wfv = wf.rearrange("(g p n) -> g p n", g=DC, p=128)
        b1v = B1.rearrange("p (hh t) -> p hh t", t=T)
        b2v = B2.rearrange("p (hh t) -> p hh t", t=T)

        def b1c(c):
            return B1[:, c * T:(c + 1) * T]

        def b2c(c):
            return B2[:, c * T:(c + 1) * T]

        d_rowc = SY.dma(qsy, ROWC[:], rowc)
        d_iota = SY.dma(qsy, IOTA[:], iotab)
        d_bval = SY.dma(qsy, BVAL[:], bval)
        d_cstb = SY.dma(qsy, CSTB[:], cstb)
        d_bcon = SY.dma(qsy, BCON[:], bcon)
        d_lsi = SY.dma(qsy, LSIS[:], lsi)
        SY.dma(qsy, BQS[:], bq)
        SY.dma(qsy, BDS[:], bd)
        d_bias = SY.dma(qsy, BFS[:], bf_)
        d_lA = SY.dma(qsy, WST[:, 0:DC * 48].rearrange("p (c r) -> p c r", r=48),
                      lA.rearrange("(c p) r -> p c r", p=128))
        ms1 = PO.op("memset", TINY[:], 0.0)
        ms2 = PO.op("memset", QN2M[:], 0.0)
        ms3 = PO.op("memset", U1S[:], 0.0)
        c_lA = PO.op("tensor_copy", waits=[(qsy, d_lA)], out=LAB[:], in_=WST[:, 0:DC * 48])
        c_lB = PO.op("memset", LBB[:], 0.0, waits=[(po_c, c_lA)])
        for hb in (0, 1):
            d_lB = SY.dma(qsy, WST[0:48, 0:2048], lB[:, hb * 2048:(hb + 1) * 2048],
                          waits=[(po_c, c_lB)])
            c_lB = PO.op("tensor_copy", waits=[(qsy, d_lB)],
                         out=LBB[0:48, hb * 2048:(hb + 1) * 2048], in_=WST[0:48, 0:2048])
        lora_w_ready = c_lB
        bqsk_ready = VE.op("tensor_scalar", waits=[(qsy, d_bias)], out=BQSK[:],
                           in0=BQS[:, 0:8], scalar1=128.0, scalar2=None, op0=ALU.mult)
        v = VE.op("tensor_copy", waits=[(qsy, d_lsi), (po_c, ms3)],
                  out=TINY[0:1, 0:4], in_=LSIS[:])
        v = VE.op("tensor_scalar", waits=[(ve_c, v), (qsy, d_bval)], out=TINY[0:1, 0:4],
                  in0=TINY[0:1, 0:4], scalar1=BVAL[0:1, 0:1], scalar2=None, op0=ALU.is_le)
        v = VE.op("reduce_sum", waits=[(ve_c, v)], out=TINY[0:1, 4:5],
                  in_=TINY[0:1, 0:4], axis=AX.X)
        v_use = VE.op("tensor_scalar", waits=[(ve_c, v)], out=TINY[0:1, 5:6],
                      in0=TINY[0:1, 4:5], scalar1=2.0, scalar2=None, op0=ALU.is_ge)
        v_a = VE.op("tensor_scalar", waits=[(ve_c, v)], out=TINY[0:1, 6:7],
                    in0=TINY[0:1, 4:5], scalar1=2.0, scalar2=0.0,
                    op0=ALU.subtract, op1=ALU.max)
        v_a = VE.op("tensor_scalar", waits=[(ve_c, v_a)], out=TINY[0:1, 6:7],
                    in0=TINY[0:1, 6:7], scalar1=2.0, scalar2=None, op0=ALU.min)
        sel_po = PO.op("tensor_copy", waits=[(ve_c, max(v_use, v_a))],
                       out=TINY[0:1, 5:7], in_=TINY[0:1, 5:7])
        PE.op("drain", inc=False)
        m_bc = PE.op("matmul", waits=[(po_c, sel_po), (qsy, d_bcon)],
                     out=pb[7][0:48, 0:2], lhsT=BCON[:, 0:48], rhs=TINY[:, 5:7],
                     start=True, stop=True)
        PE.op("drain", inc=False)
        v = VE.op("tensor_tensor", waits=[(pe_c, m_bc), (qsy, d_iota)], out=SELC[:],
                  in0=IOTA[:], in1=pb[7][0:48, 1:2], op=ALU.is_equal)
        sel_ready = VE.op("tensor_tensor", waits=[(ve_c, v)], out=SELC[:],
                          in0=SELC[:], in1=pb[7][0:48, 0:1], op=ALU.mult)
        if "t_sel" in tapd:
            SE.dma(qsc, tapd["t_sel"], SELC[:], waits=[(ve_c, sel_ready)])

        def new_wstream(wv_ap, nog):
            return {"issued": set(), "cast": {}, "pedone": {}, "nog": nog, "wv": wv_ap,
                    "slot_cast": {}, "war_po": max(po_c.n, c_lB), "war_pe": pe_c.n}

        def stream_weights(st, og):
            if og in st["issued"] or og >= st["nog"]:
                return
            st["issued"].add(og)
            wv = st["wv"]
            for q in range(4):
                s_ = q % 2
                w = [(po_c, max(st["slot_cast"].get(s_, 0), st["war_po"]))]
                dq = SY.dma(qsy, WST[:, s_ * 1024:(s_ + 1) * 1024],
                            wv[og, :, q * 1024:(q + 1) * 1024], waits=w)
                w = [(qsy, dq),
                     (pe_c, max(st["pedone"].get(og - 2, 0), st["war_pe"]))]
                PO.op("tensor_copy", waits=w, out=FSCR[:, 0:8], in_=FSCR[:, 0:8])
                st["cast"][(og, q)] = PO.op(
                    "tensor_copy",
                    out=WBF[:, (og % 2) * 4096 + q * 1024:(og % 2) * 4096 + (q + 1) * 1024],
                    in_=WST[:, s_ * 1024:(s_ + 1) * 1024])
                st["slot_cast"][s_] = st["cast"][(og, q)]
            st["cast"][(og, 3)] = PO.op(
                "tensor_copy",
                out=FSCR[:, 0:4],
                in_=WBF[:, (og % 2) * 4096 + 1022:(og % 2) * 4096 + 1026]
                if False else
                WBF.rearrange("p (q n) -> p q n", n=1024)
                [:, (og % 2) * 4 + 3:(og % 2) * 4 + 4, 1020:1024]
                .rearrange("p a n -> p (a n)"))

        def wtile(og, dc):
            return WBF[:, (og % 2) * 4096 + dc * 128:(og % 2) * 4096 + (dc + 1) * 128]

        def emit_stats_mms(stat, rhs_bf, sq_bf, first, last, c):
            for h in (0, 1):
                for which, rhs in (("s", rhs_bf), ("q", sq_bf)):
                    bk = stat["bank"][(which, h)]
                    w = [stat["ready"][(c, h, which)]]
                    stat["mm"][(c, h, which)] = PE.op(
                        "matmul", waits=w, out=pb[bk][:, 0:512],
                        lhsT=CSTB[:, 1:129], rhs=rhs[:, h * TH:(h + 1) * TH],
                        start=first, stop=last)

        def finalize_ln(stat, MUBC, SROW, sums_tap=None):
            lastmm = max(stat["mm"].values())
            po_now = po_c.n
            SE.op("copy", waits=[(pe_c, lastmm), (po_c, po_now)], out=SROW[0:1, 0:TH],
                  in_=pb[stat["bank"][("s", 0)]][0:1, 0:512])
            SE.op("copy", out=SROW[0:1, TH:T],
                  in_=pb[stat["bank"][("s", 1)]][0:1, 0:512])
            SE.op("copy", out=SROW[0:1, T:T + TH],
                  in_=pb[stat["bank"][("q", 0)]][0:1, 0:512])
            v4se = SE.op("copy", out=SROW[0:1, T + TH:2 * T],
                         in_=pb[stat["bank"][("q", 1)]][0:1, 0:512])
            v2 = v4 = 0
            tap_q = 0
            if sums_tap is not None:
                tap_q = SE.dma(qsc, sums_tap, SROW[0:1, 0:2 * T], waits=[(se_c, v4se)])
            bcs = []
            for i in (0, 1):
                bk = stat["bank"][("s", i)]
                mm = PE.op("matmul", waits=[(se_c, v4se)], out=pb[bk][:, 0:512],
                           lhsT=BCON[:, 0:128], rhs=SROW[:, i * TH:(i + 1) * TH],
                           start=True, stop=True)
                bcs.append(SE.op("copy", waits=[(pe_c, mm)],
                                 out=MUBC[:, i * TH:(i + 1) * TH], in_=pb[bk][:, 0:512]))
            mu_bc_mm = bcs[:]
            v5 = VE.op("tensor_tensor",
                       waits=[(se_c, max(bcs)), (qsc, tap_q)],
                       out=SROW[0:1, 0:T],
                       in0=SROW[0:1, 0:T], in1=SROW[0:1, 0:T], op=ALU.mult)
            v6 = VE.op("tensor_tensor", waits=[(ve_c, v5)], out=SROW[0:1, T:2 * T],
                       in0=SROW[0:1, T:2 * T], in1=SROW[0:1, 0:T], op=ALU.subtract)
            v6 = VE.op("tensor_scalar", waits=[(ve_c, v6)], out=SROW[0:1, T:2 * T],
                       in0=SROW[0:1, T:2 * T], scalar1=EPS, scalar2=None, op0=ALU.add)
            s1 = SE.op("activation", waits=[(ve_c, v6)], out=SROW[0:1, 0:T],
                       in_=SROW[0:1, T:2 * T], func=AF.Ln, scale=1.0)
            s2 = SE.op("activation", waits=[(se_c, s1)], out=SROW[0:1, T:2 * T],
                       in_=SROW[0:1, 0:T], func=AF.Exp, scale=-0.5)
            for i in (0, 1):
                bk = stat["bank"][("q", i)]
                mm = PE.op("matmul", waits=[(se_c, s2)], out=pb[bk][:, 0:512],
                           lhsT=BCON[:, 0:128], rhs=SROW[:, T + i * TH:T + (i + 1) * TH],
                           start=True, stop=True)
                bcs.append(SE.op("copy", waits=[(pe_c, mm)],
                                 out=MUBC[:, T + i * TH:T + (i + 1) * TH],
                                 in_=pb[bk][:, 0:512]))
            return max(bcs)

        def apply_ln(Bt, MUBC, bc_ready, cast_events):
            done = {}
            for c in range(DC):
                w = [(se_c, bc_ready), cast_events.get(c, (po_c, 0))]
                v1 = PO.op("tensor_tensor", waits=w, out=Bt[:, c * T:(c + 1) * T],
                           in0=Bt[:, c * T:(c + 1) * T], in1=MUBC[:, 0:T], op=ALU.subtract)
                PO.op("tensor_tensor", waits=[(po_c, v1)],
                      out=Bt[:, c * T:(c + 1) * T],
                      in0=Bt[:, c * T:(c + 1) * T],
                      in1=MUBC[:, T:2 * T], op=ALU.mult)
                done[c] = (po_c, PO.op("tensor_copy", out=FSCR[:, 4:8],
                                       in_=Bt[:, c * T + T - 4:(c + 1) * T]))
            return done

        def emit_ffn(Bt_in, Bt_cast, wv_ap, bias_sb, scale, lora, resid3, write3,
                     stats, ln_ready, XSTG, EVF, XOUT, SQB):
            bar = p.barrier()
            stW = new_wstream(wv_ap, DC)
            stream_weights(stW, 0)
            stream_weights(stW, 1)
            st2 = None
            if stats:
                st2 = {"bank": {("s", 0): 4, ("s", 1): 5, ("q", 0): 6, ("q", 1): 7},
                       "ready": {}, "mm": {}}
            u_evac = 0
            if lora:
                umm = {}
                PE.op("drain", inc=False)
                for h in (0, 1):
                    for dc in range(DC):
                        umm[h] = PE.op("matmul",
                                       waits=[ln_ready[dc], (po_c, lora_w_ready)],
                                       out=pb[h][0:48, 0:512],
                                       lhsT=LAB[:, dc * 48:(dc + 1) * 48],
                                       rhs=Bt_in[:, dc * T + h * TH: dc * T + h * TH + TH],
                                       start=(dc == 0), stop=(dc == DC - 1),
                                       inc=(dc == DC - 1))
                for h in (0, 1):
                    u_evac = VE.op("tensor_scalar",
                                   waits=[(pe_c, umm[h]), (ve_c, sel_ready)],
                                   out=U1S[0:48, h * TH:(h + 1) * TH],
                                   in0=pb[h][0:48, 0:512], scalar1=SELC[0:48, 0:1],
                                   scalar2=None, op0=ALU.mult)
                PE.op("drain", inc=False)
                u_evac_po = PO.op("tensor_copy", waits=[(ve_c, u_evac)],
                                  out=U1S[:], in_=U1S[:])
            evac, stt, xw, castc, xdma, sqd = {}, {}, {}, {}, {}, {}
            xldr = {}
            for oc in range(DC):
                war = max(stt.get((oc - 2, 1), 0), bar["ve"])
                xdma[oc] = SY.dma(qsy, XSTG[:, (oc % 2) * T:(oc % 2 + 1) * T],
                                  resid3[:, oc, :], waits=[(ve_c, war)])
                xldr[oc] = PO.op("tensor_copy", waits=[(qsy, xdma[oc])],
                                 out=XSTG[:, (oc % 2) * T:(oc % 2 + 1) * T],
                                 in_=XSTG[:, (oc % 2) * T:(oc % 2 + 1) * T])
                lastmm = 0
                for h in (0, 1):
                    bk = (2 * oc + h) % 4
                    w = [(po_c, stW["cast"][(oc, 3)]),
                         (se_c, evac.get((oc - 2, h), 0))]
                    if lora and oc < 2:
                        w.append((po_c, u_evac_po))
                    for dc in range(DC):
                        w2 = w + [ln_ready[dc]]
                        last = (dc == DC - 1)
                        lastmm = PE.op("matmul", waits=w2, out=pb[bk][:, 0:512],
                                       lhsT=wtile(oc, dc),
                                       rhs=Bt_in[:, dc * T + h * TH: dc * T + h * TH + TH],
                                       start=(dc == 0), stop=(last and not lora),
                                       inc=(last and not lora))
                        w = []
                    if lora:
                        lastmm = PE.op("matmul", waits=[(po_c, u_evac_po)],
                                       out=pb[bk][:, 0:512],
                                       lhsT=LBB[:, oc * 128:(oc + 1) * 128],
                                       rhs=U1S[:, h * TH:(h + 1) * TH],
                                       start=False, stop=True)
                    eslot = h
                    if ffn_serial >= 2:
                        p.barrier()
                    w = [(pe_c, lastmm), (ve_c, stt.get((oc - 1, h), 0))]
                    evac[(oc, h)] = SE.op(
                        "activation", waits=w,
                        out=EVF[:, eslot * 512:(eslot + 1) * 512],
                        in_=pb[bk][:, 0:512], func=AF.Identity,
                        bias=bias_sb[:, oc:oc + 1], scale=1.0)
                    if ffn_serial >= 2:
                        p.barrier()
                    w = [(se_c, evac[(oc, h)]), (po_c, xldr[oc]),
                         (qsc, xw.get((oc - 1, h), 0)),
                         (po_c, castc.get(oc - 1, 0))]
                    xoslot = h
                    stt[(oc, h)] = VE.op(
                        "scalar_tensor_tensor", waits=w,
                        out=XOUT[:, xoslot * 512:(xoslot + 1) * 512],
                        in0=EVF[:, eslot * 512:(eslot + 1) * 512], scalar=scale,
                        in1=XSTG[:, (oc % 2) * T + h * TH:(oc % 2) * T + h * TH + TH],
                        op0=ALU.mult, op1=ALU.add)
                    stt_f = VE.op("tensor_copy", out=FSCR[:, 0:4],
                                  in_=XOUT[:, xoslot * 512 + 508:xoslot * 512 + 512])
                    stt[(oc, h)] = stt_f
                    if ffn_serial >= 2:
                        p.barrier()
                    xw[(oc, h)] = SE.dma(
                        qsc, write3[:, oc, h * TH:(h + 1) * TH],
                        XOUT[:, xoslot * 512:(xoslot + 1) * 512],
                        waits=[(ve_c, stt_f)])
                    if Bt_cast is not None:
                        cv = PO.op("tensor_copy", waits=[(ve_c, stt[(oc, h)])],
                                   out=Bt_cast[:, oc * T + h * TH: oc * T + h * TH + TH],
                                   in_=XOUT[:, xoslot * 512:(xoslot + 1) * 512])
                        castc[oc] = cv
                        if stats:
                            sq = PO.op(
                                "tensor_tensor",
                                waits=[(po_c, cv),
                                       (pe_c, st2["mm"].get((oc - 2, h, "q"), 0))],
                                out=SQB[:, (oc % 2) * T + h * TH:(oc % 2) * T + h * TH + TH],
                                in0=Bt_cast[:, oc * T + h * TH: oc * T + h * TH + TH],
                                in1=Bt_cast[:, oc * T + h * TH: oc * T + h * TH + TH],
                                op=ALU.mult)
                            st2["ready"][(oc, h, "s")] = (po_c, cv)
                            st2["ready"][(oc, h, "q")] = (po_c, sq)
                if stats and oc > 0:
                    oc_ = oc - 1
                    emit_stats_mms(st2, Bt_cast[:, oc_ * T:(oc_ + 1) * T],
                                   SQB[:, (oc_ % 2) * T:(oc_ % 2 + 1) * T],
                                   first=(oc_ == 0), last=False, c=oc_)
                stW["pedone"][oc] = lastmm
                if ffn_serial:
                    p.barrier()
                stream_weights(stW, oc + 2)
            if stats:
                oc_ = DC - 1
                emit_stats_mms(st2, Bt_cast[:, oc_ * T:(oc_ + 1) * T],
                               SQB[:, (oc_ % 2) * T:(oc_ % 2 + 1) * T],
                               first=False, last=True, c=oc_)
            return st2, castc, xw

        with contextlib.ExitStack() as esA:
            XSTG = sb("XSTG", [128, 2 * 1024], F32, esA)
            EVF = sb("EVF", [128, 2 * 512], F32, esA)
            XOUT = sb("XOUT", [128, 2 * 512], F32, esA)
            SQB = sb("SQB", [128, 2 * T], BF16, esA)
            MUBC = sb("MUBC", [128, 2 * T], F32, esA)
            SROW = sb("SROW", [128, 2 * T], F32, esA)
            ms_sr = PO.op("memset", SROW[:], 0.0)

            stat = {"bank": {("s", 0): 4, ("s", 1): 5, ("q", 0): 6, ("q", 1): 7},
                    "ready": {}, "mm": {}}
            cast_done = {}
            slot_free = {}
            for c in range(DC):
                slot = c % 2
                dx = SY.dma(qsy, XSTG[:, slot * T:(slot + 1) * T],
                            xin3[:, c, :],
                            waits=[(po_c, slot_free.get(c - 2, 0))])
                cast_done[c] = PO.op(
                    "tensor_copy", waits=[(qsy, dx)], out=b1c(c),
                    in_=XSTG[:, slot * T:(slot + 1) * T])
                sq = PO.op("tensor_tensor",
                           waits=[(po_c, cast_done[c]),
                                  (pe_c, stat["mm"].get((c - 2, 1, "q"), 0))],
                           out=SQB[:, (c % 2) * T:(c % 2 + 1) * T],
                           in0=b1c(c), in1=b1c(c), op=ALU.mult)
                for h in (0, 1):
                    stat["ready"][(c, h, "s")] = (po_c, cast_done[c])
                    stat["ready"][(c, h, "q")] = (po_c, sq)
                emit_stats_mms(stat, b1c(c), SQB[:, (c % 2) * T:(c % 2 + 1) * T],
                               first=(c == 0), last=(c == DC - 1), c=c)
                slot_free[c] = cast_done[c]
            if "t_b1raw" in tapd:
                SE.dma(qsc, tapd["t_b1raw"], B1[:],
                       waits=[(po_c, max(cast_done.values()))])
            bc_ready = finalize_ln(stat, MUBC, SROW, sums_tap=tapd.get("t_sums"))
            if "t_mu" in tapd:
                SE.dma(qsc, tapd["t_mu"], MUBC[:], waits=[(se_c, bc_ready)])
            ln1_done = apply_ln(B1, MUBC, bc_ready,
                                {c: (po_c, n) for c, n in cast_done.items()})
            if "t_ln1" in tapd:
                SE.dma(qsc, tapd["t_ln1"], B1[:],
                       waits=[(po_c, max(n for _, n in ln1_done.values()))])

            st2, castc_b2, xw1 = emit_ffn(B1, B2, wfv, BFS, 0.5, True, xin3, x1s3,
                                          True, ln1_done, XSTG, EVF, XOUT, SQB)
            if "t_x1" in tapd:
                SE.dma(qsc, tapd["t_x1"], x1s, waits=[(qsc, max(xw1.values()))])
            if "t_u" in tapd:
                SE.dma(qsc, tapd["t_u"], U1S[:])
            bc2 = finalize_ln(st2, MUBC, SROW)
            ln2_done = apply_ln(B2, MUBC, bc2,
                                {c: (po_c, n) for c, n in castc_b2.items()})
            if "t_ln2" in tapd:
                SE.dma(qsc, tapd["t_ln2"], B2[:],
                       waits=[(po_c, max(n for _, n in ln2_done.values()))])

        with contextlib.ExitStack() as esB:
            RSC = sb("RSC", [128, T], F32, esB)
            ROTB = sb("ROTB", [128, T], BF16, esB)
            CQ = sb("CQ", [128, T], F32, esB)
            SQ_ = sb("SQ_", [128, T], F32, esB)
            BVBC = sb("BVBC", [128, 1024], F32, esB)
            KOUT = sb("KOUT", [128, 2 * T], BF16, esB)
            VOWN = sb("VOWN", [128, 8 * T], BF16, esB)

            bar = p.barrier()
            stW = new_wstream(wqv, QKV_OC)
            stream_weights(stW, 0)
            stream_weights(stW, 1)
            d_cq = SY.dma(qsy, CQ[:], cosq, waits=[(ve_c, bar["ve"]), (se_c, bar["se"])])
            d_sq = SY.dma(qsy, SQ_[:], sinq)
            ms_bv = PO.op("memset", BVBC[:], 0.0)
            d_bv = SY.dma(qsy, BVBC[0:1, 0:1024], bv_r, waits=[(po_c, ms_bv)])
            bvb = []
            for i in (0, 1):
                mm = PE.op("matmul", waits=[(qsy, d_bv)], out=pb[6 + i][:, 0:512],
                           lhsT=BCON[:, 0:128], rhs=BVBC[:, i * 512:(i + 1) * 512],
                           start=True, stop=True)
                bvb.append(SE.op("copy", waits=[(pe_c, mm)],
                                 out=BVBC[:, i * 512:(i + 1) * 512],
                                 in_=pb[6 + i][:, 0:512]))
            bvbc_ready = max(bvb)
            ccdma_last = 0
            pend_qn2 = []

            def flush_qn2():
                while pend_qn2:
                    hh_, sq_ev, dst_ = pend_qn2.pop(0)
                    mms = []
                    for h in (0, 1):
                        w = [(po_c, sq_ev), (ve_c, qn2red.get(hh_ - 2, 0))]
                        mms.append(PE.op("matmul", waits=w,
                                         out=pb[4 + h][:, 0:512],
                                         lhsT=CSTB[:, 0:128],
                                         rhs=KOUT[:, (hh_ % 2) * T + h * TH:
                                                 (hh_ % 2) * T + h * TH + TH],
                                         start=True, stop=True))
                    qn2mm[hh_] = mms[1]
                    rd1 = VE.op("reduce_max", waits=[(pe_c, mms[0])],
                                out=QN2M[0:1, hh_ * 8:hh_ * 8 + 4],
                                in_=pb[4][0:1, 0:512]
                                .rearrange("a (m t) -> a m t", t=128), axis=AX.X)
                    qn2red[hh_] = VE.op("reduce_max", waits=[(pe_c, mms[1])],
                                        out=QN2M[0:1, hh_ * 8 + 4:hh_ * 8 + 8],
                                        in_=pb[5][0:1, 0:512]
                                        .rearrange("a (m t) -> a m t", t=128),
                                        axis=AX.X)
            bank_free_se = {}
            rsc_free_evt = (po_c, 0)
            kout_free = {}
            vevac = {}
            qn2red = {}
            qn2mm = {}
            qrope = {}
            for oc in range(QKV_OC):
                if OC_V0 <= oc < OC_Q0:
                    kvv = oc - OC_V0
                    lastmm = 0
                    for tt in range(8):
                        bk = 4 + (tt % 2)
                        w = [(po_c, stW["cast"][(oc, 3)]),
                             (ve_c, vevac.get((oc, tt - 2), vevac.get((oc - 1, tt + 6), 0)))]
                        for dc in range(DC):
                            w2 = w + [ln2_done[dc]]
                            lastmm = PE.op("matmul", waits=w2, out=pb[bk][:, 0:128],
                                           lhsT=B2[:, dc * T + tt * 128: dc * T + (tt + 1) * 128],
                                           rhs=wtile(oc, dc),
                                           start=(dc == 0), stop=(dc == DC - 1),
                                           inc=(dc == DC - 1))
                            w = []
                        vevac[(oc, tt)] = VE.op(
                            "tensor_tensor",
                            waits=[(pe_c, lastmm), (se_c, bvbc_ready)],
                            out=VOWN[:, tt * T + kvv * 128: tt * T + (kvv + 1) * 128],
                            in0=pb[bk][:, 0:128],
                            in1=BVBC[:, kvv * 128:(kvv + 1) * 128], op=ALU.add)
                    stW["pedone"][oc] = lastmm
                    if oc == OC_V0 + 7:
                        wv_all = max(vevac[(o2, t2)] for o2 in range(OC_V0, OC_V0 + 8)
                                     for t2 in range(8))
                        for tt in range(8):
                            ccdma_last = PO.dma(
                                qgp, ccin[1024 + tt * 128:1024 + (tt + 1) * 128, :],
                                VOWN[:, tt * T:(tt + 1) * T], waits=[(ve_c, wv_all)])
                        cc_done = PO.collective(
                            cc_c, "AllGather", ALU.bypass,
                            [[0, 1], [2, 3], [4, 5], [6, 7]],
                            [ccin.ap().opt()], [ccout.ap().opt()],
                            waits=[(qgp, ccdma_last)])
                else:
                    is_q = oc >= OC_Q0
                    hh = oc - OC_Q0 if is_q else None
                    lastmm = 0
                    for h in (0, 1):
                        bk = (2 * oc + h) % 4
                        w = [(po_c, stW["cast"][(oc, 3)]),
                             (se_c, bank_free_se.get(bk, 0))]
                        for dc in range(DC):
                            w2 = w + [ln2_done[dc]]
                            lastmm = PE.op("matmul", waits=w2, out=pb[bk][:, 0:512],
                                           lhsT=wtile(oc, dc),
                                           rhs=B2[:, dc * T + h * TH: dc * T + h * TH + TH],
                                           start=(dc == 0), stop=(dc == DC - 1),
                                           inc=(dc == DC - 1))
                            w = []
                    flush_qn2()
                    ev = []
                    for h in (0, 1):
                        bk = (2 * oc + h) % 4
                        ev.append(SE.op("activation",
                                        waits=[(pe_c, lastmm - (1 - h)),
                                               rsc_free_evt, (ve_c, bqsk_ready)],
                                        out=RSC[:, h * TH:(h + 1) * TH],
                                        in_=pb[bk][:, 0:512], func=AF.Identity,
                                        bias=(BQS[:, oc:oc + 1] if is_q
                                              else BQSK[:, oc:oc + 1]),
                                        scale=(1.0 if is_q else 128.0)))
                        bank_free_se[bk] = ev[h]
                    r1 = SE.op("activation", waits=[(se_c, ev[1])], out=ROTB[0:64, :],
                               in_=RSC[64:128, 0:T], func=AF.Copy, bias=0.0, scale=-1.0)
                    r2 = SE.op("activation", out=ROTB[64:128, :],
                               in_=RSC[0:64, 0:T], func=AF.Copy, bias=0.0, scale=1.0)
                    if is_q:
                        dst = b1c(hh)
                        v1 = VE.op("tensor_tensor",
                                   waits=[(se_c, max(r1, r2)), (qsy, d_sq)],
                                   out=RSC[:, 0:T], in0=RSC[:, 0:T], in1=CQ[:], op=ALU.mult)
                        v2 = VE.op("tensor_tensor", waits=[(ve_c, v1)], out=ROTB[:],
                                   in0=ROTB[:], in1=SQ_[:], op=ALU.mult)
                        v2 = VE.op("tensor_copy", out=FSCR[:, 0:4],
                                   in_=ROTB[:, T - 4:T])
                        v3 = PO.op("tensor_tensor", waits=[(ve_c, v2)], out=dst,
                                   in0=RSC[:, 0:T], in1=ROTB[:], op=ALU.add)
                        qrope[hh] = v3
                        rsc_free_evt = (po_c, v3)
                        sq_ = PO.op("tensor_tensor",
                                    waits=[(po_c, v3), (qgp, kout_free.get(7, 0)),
                                           (pe_c, qn2mm.get(hh - 2, 0))],
                                    out=KOUT[:, (hh % 2) * T:(hh % 2 + 1) * T],
                                    in0=dst, in1=dst, op=ALU.mult)
                        pend_qn2.append((hh, sq_, dst))
                    else:
                        slot = oc % 2
                        dst = KOUT[:, slot * T:(slot + 1) * T]
                        w_t = [(se_c, max(r1, r2)), (qsy, d_sq),
                               (qgp, kout_free.get(oc - 2, 0))]
                        v1 = VE.op("tensor_tensor", waits=w_t, out=RSC[:, 0:T],
                                   in0=RSC[:, 0:T], in1=CQ[:], op=ALU.mult)
                        v2 = VE.op("tensor_tensor", waits=[(ve_c, v1)], out=ROTB[:],
                                   in0=ROTB[:], in1=SQ_[:], op=ALU.mult)
                        v2 = VE.op("tensor_copy", out=FSCR[:, 0:4],
                                   in_=ROTB[:, T - 4:T])
                        v3 = PO.op("tensor_tensor", waits=[(ve_c, v2)], out=dst,
                                   in0=RSC[:, 0:T], in1=ROTB[:], op=ALU.add)
                        rsc_free_evt = (po_c, v3)
                        kout_free[oc] = PO.dma(qgp, ccin[oc * 128:(oc + 1) * 128, :], dst,
                                               waits=[(po_c, v3)])
                    stW["pedone"][oc] = lastmm
                stream_weights(stW, oc + 2)
            flush_qn2()
            qkv_po_last = max(qrope.values())
            qkv_ve_last = max(qn2red.values())
            if "t_q" in tapd:
                SE.dma(qsc, tapd["t_q"], B1[:], waits=[(po_c, qkv_po_last)])

        with contextlib.ExitStack() as esC:
            KKV = sb("KKV", [128, 2 * 2048], BF16, esC)
            VKV = sb("VKV", [128, 2 * 2048], BF16, esC)
            PTS = sb("PTS", [128, 4 * 512], BF16, esC)
            MSK = sb("MSK", [128, max(MASK_COLS, 128)], BF16, esC)
            K2S = sb("K2S", [128, 2048], BF16, esC)
            RSB = sb("RSB", [128, 2 * 512], F32, esC)
            NEGC = sb("NEGC", [128, 64], F32, esC)
            RROW = sb("RROW", [128, 512], F32, esC)

            bar = p.barrier()
            ms_rr = PO.op("memset", RROW[:], 0.0)
            stW_d = new_wstream(wdv, DC)
            stream_weights(stW_d, 0)
            stream_weights(stW_d, 1)
            d_msk = SY.dma(qsy, MSK[:, 0:max(MASK_COLS, 128)], masks,
                           waits=[(ve_c, qkv_ve_last), (po_c, max(bar["po"], qkv_po_last)),
                                  (qgp, ccdma_last)])
            cco3 = ccout.ap().rearrange("(r p) t -> p r t", p=128)
            kvdma = {}
            pe_done_kv = {}
            ctx_free = {}
            sums_free = {}
            exp_by_stb = {}
            pts_war = {}

            kv_ldr = {}

            def issue_readback(kv):
                slot = kv % 2
                w = [(cc_c, 1), (ve_c, qkv_ve_last), (po_c, qkv_po_last),
                     (qgp, ccdma_last),
                     (pe_c, pe_done_kv.get(kv - 2, 0)),
                     (se_c, exp_by_stb.get(0, 0)), (se_c, exp_by_stb.get(1, 0))]
                SE.dma(qsc, KKV[:, slot * 2048:slot * 2048 + 1024],
                       cco3[:, kv:kv + 1, :].rearrange("p a t -> p (a t)"), waits=w)
                SE.dma(qsc, KKV[:, slot * 2048 + 1024:slot * 2048 + 2048],
                       cco3[:, 16 + kv:17 + kv, :].rearrange("p a t -> p (a t)"))
                SE.dma(qsc, VKV[:, slot * 2048:slot * 2048 + 1024]
                       .rearrange("p (c n) -> p c n", n=128),
                       cco3[:, 8:16, kv * 128:(kv + 1) * 128])
                last = SE.dma(qsc, VKV[:, slot * 2048 + 1024:slot * 2048 + 2048]
                              .rearrange("p (c n) -> p c n", n=128),
                              cco3[:, 24:32, kv * 128:(kv + 1) * 128])
                l1 = PO.op("tensor_copy", waits=[(qsc, last)],
                           out=KKV[:, slot * 2048:(slot + 1) * 2048],
                           in_=KKV[:, slot * 2048:(slot + 1) * 2048])
                kv_ldr[kv] = PO.op("tensor_copy",
                                   out=VKV[:, slot * 2048:(slot + 1) * 2048],
                                   in_=VKV[:, slot * 2048:(slot + 1) * 2048])
                return last

            kvdma[0] = issue_readback(0)
            kvdma[1] = issue_readback(1)
            kn2_mm_last = 0
            rsb_free = {}
            mmn_last = 0
            cpb_last = 0
            negc_ready = {}
            ctx_evac_all = []
            for kv in range(NKV):
                slot = kv % 2
                Kk = KKV[:, slot * 2048:(slot + 1) * 2048]
                Vk = VKV[:, slot * 2048:(slot + 1) * 2048]
                k2 = PO.op("tensor_tensor",
                           waits=[(po_c, kv_ldr[kv]), (pe_c, kn2_mm_last)],
                           out=K2S[:], in0=Kk, in1=Kk, op=ALU.mult)
                piece_cnt = {m: 0 for m in range(8)}
                vlast = 0
                for j in range(4):
                    bk = 6 + (j % 2)
                    w = [(po_c, k2), (ve_c, vlast), (se_c, cpb_last), (pe_c, mmn_last)]
                    mm = PE.op("matmul", waits=w, out=pb[bk][:, 0:512],
                               lhsT=CSTB[:, 0:128], rhs=K2S[:, j * 512:(j + 1) * 512],
                               start=True, stop=True)
                    kn2_mm_last = mm
                    for m in range(8):
                        for (q, lo, hi) in VIS_RANGES[m]:
                            if q != j:
                                continue
                            if piece_cnt[m] == 0:
                                vlast = VE.op("reduce_max", waits=[(pe_c, mm)],
                                              out=TINY[0:1, 16 + m:17 + m],
                                              in_=pb[bk][0:1, lo:hi], axis=AX.X)
                            else:
                                vlast = VE.op("reduce_max", waits=[(pe_c, mm)],
                                              out=TINY[0:1, 32:33],
                                              in_=pb[bk][0:1, lo:hi], axis=AX.X)
                                vlast = VE.op("tensor_tensor", waits=[(ve_c, vlast)],
                                              out=TINY[0:1, 16 + m:17 + m],
                                              in0=TINY[0:1, 16 + m:17 + m],
                                              in1=TINY[0:1, 32:33], op=ALU.max)
                            piece_cnt[m] += 1
                qv = QN2M[0:1, 0:256].rearrange("a (h m) -> a m h", m=8)
                for m in range(8):
                    qg = VE.op("reduce_max", waits=[(ve_c, qn2red[4 * kv + 3])],
                               out=TINY[0:1, 33:34],
                               in_=qv[0:1, m, 4 * kv:4 * kv + 4], axis=AX.X)
                    vlast = VE.op("tensor_tensor", waits=[(ve_c, max(qg, vlast))],
                                  out=TINY[0:1, 24 + m:25 + m],
                                  in0=TINY[0:1, 16 + m:17 + m],
                                  in1=TINY[0:1, 33:34], op=ALU.mult)
                c2v = VE.op("tensor_scalar", waits=[(ve_c, vlast)], out=TINY[0:1, 24:32],
                            in0=TINY[0:1, 24:32], scalar1=1e-18, scalar2=None,
                            op0=ALU.add)
                s1 = SE.op("activation", waits=[(ve_c, c2v)], out=TINY[0:1, 24:32],
                           in_=TINY[0:1, 24:32], func=AF.Ln, scale=1.0)
                s2 = SE.op("activation", waits=[(se_c, s1)], out=TINY[0:1, 24:32],
                           in_=TINY[0:1, 24:32], func=AF.Exp, scale=0.5)
                mmn = PE.op("matmul", waits=[(se_c, s2)],
                            out=pb[7][:, 0:8], lhsT=BCON[:, 0:128],
                            rhs=TINY[:, 24:32], start=True, stop=True)
                mmn_last = mmn
                negc_ready[kv] = SE.op("activation", waits=[(pe_c, mmn)],
                                       out=NEGC[:, kv * 8:(kv + 1) * 8],
                                       in_=pb[7][:, 0:8], func=AF.Copy,
                                       bias=0.0, scale=-1.0)
                for m in range(8):
                    nu = len(UNITS[m])
                    sums_bk = 2 + (m % 2)
                    ctx_bk = 4 + (m % 2)
                    last_sum = last_pv = 0
                    for ui, (side, s0) in enumerate(UNITS[m]):
                        stb = ui % 2
                        pslot = ui % 4
                        kcol = side * 1024 + s0 * 64
                        vchunk = side * 8 + s0 // 2
                        w = [(po_c, kv_ldr[kv]), (se_c, exp_by_stb.get(stb, 0))]
                        mm_qk = PE.op("matmul", waits=w, out=pb[stb][:, 0:512],
                                      lhsT=Kk[:, kcol:kcol + 128],
                                      rhs=b1v[:, 4 * kv:4 * kv + 4, m * 128:(m + 1) * 128],
                                      start=True, stop=True)
                        w = [(pe_c, max(mm_qk, pts_war.get(pslot, 0))),
                             (se_c, negc_ready[kv]), (qsy, d_msk)]
                        e = SE.op("activation", waits=w,
                                  out=PTS[:, pslot * 512:(pslot + 1) * 512],
                                  in_=pb[stb][:, 0:512], func=AF.Exp,
                                  bias=NEGC[:, kv * 8 + m:kv * 8 + m + 1], scale=1.0)
                        exp_by_stb[stb] = e
                        gate = (se_c, e)
                        if (m, ui) in MASK_FLAG_SET:
                            off = MASK_OFF[(m, ui)]
                            pm = 0
                            for hq in range(4):
                                pm = PO.op("tensor_tensor", waits=[(se_c, e)],
                                           out=PTS[:, pslot * 512 + hq * 128:
                                                   pslot * 512 + (hq + 1) * 128],
                                           in0=PTS[:, pslot * 512 + hq * 128:
                                                   pslot * 512 + (hq + 1) * 128],
                                           in1=MSK[:, off:off + 128], op=ALU.mult)
                            gate = (po_c, pm)
                        w = [gate]
                        if ui == 0:
                            w.append((se_c, sums_free.get(sums_bk, 0)))
                        last_sum = PE.op("matmul", waits=w, out=pb[sums_bk][:, 0:512],
                                         lhsT=CSTB[:, 0:128],
                                         rhs=PTS[:, pslot * 512:(pslot + 1) * 512],
                                         start=(ui == 0), stop=(ui == nu - 1))
                        w = [gate]
                        if ui == 0:
                            w.append((ve_c, ctx_free.get(ctx_bk, 0)))
                        last_pv = PE.op("matmul", waits=w, out=pb[ctx_bk][:, 0:512],
                                        lhsT=Vk[:, vchunk * 128:(vchunk + 1) * 128],
                                        rhs=PTS[:, pslot * 512:(pslot + 1) * 512],
                                        start=(ui == 0), stop=(ui == nu - 1))
                        pts_war[pslot] = last_pv
                        if attn_serial >= 2:
                            p.barrier()
                    t1 = SE.op("activation", waits=[(pe_c, last_sum), (po_c, ms_rr)],
                               out=RROW[0:1, 0:512], in_=pb[sums_bk][0:1, 0:512],
                               func=AF.Ln, scale=2.0 ** 40)
                    sums_free[sums_bk] = t1
                    t2 = SE.op("activation", waits=[(se_c, t1)], out=RROW[0:1, 0:512],
                               in_=RROW[0:1, 0:512], func=AF.Exp, scale=-1.0)
                    mmb = PE.op("matmul",
                                waits=[(se_c, max(cpb_last, t2))],
                                out=pb[6][:, 0:512], lhsT=BCON[:, 128:256],
                                rhs=RROW[:, 0:512], start=True, stop=True)
                    cpb_last = SE.op("copy",
                                     waits=[(pe_c, mmb),
                                            (ve_c, rsb_free.get(m % 2, 0))],
                                     out=RSB[:, (m % 2) * 512:(m % 2 + 1) * 512],
                                     in_=pb[6][:, 0:512])
                    ev = VE.op("tensor_tensor", waits=[(se_c, cpb_last), (pe_c, last_pv)],
                               out=b2v[:, 4 * kv:4 * kv + 4, m * 128:(m + 1) * 128],
                               in0=pb[ctx_bk][:, 0:512],
                               in1=RSB[:, (m % 2) * 512:(m % 2 + 1) * 512], op=ALU.mult)
                    ctx_free[ctx_bk] = ev
                    rsb_free[m % 2] = ev
                    ctx_evac_all.append(ev)
                    if attn_serial >= 1:
                        p.barrier()
                pe_done_kv[kv] = last_pv
                if kv + 2 < NKV:
                    kvdma[kv + 2] = issue_readback(kv + 2)
            ctx_done = max(ctx_evac_all)
            if "t_kv" in tapd:
                tkv = tapd["t_kv"].rearrange("p (a n) -> p a n", a=2)
                SE.dma(qsc, tkv[:, 0:1, :].rearrange("p a n -> p (a n)"),
                       KKV[:, 0:2048], waits=[(qsc, kvdma[0]), (pe_c, pe_done_kv[1])])
                SE.dma(qsc, tkv[:, 1:2, :].rearrange("p a n -> p (a n)"),
                       VKV[:, 0:2048])
            if "t_negc" in tapd:
                SE.dma(qsc, tapd["t_negc"], NEGC[:],
                       waits=[(se_c, max(negc_ready.values()))])
            if "t_ctx" in tapd:
                SE.dma(qsc, tapd["t_ctx"], B2[:], waits=[(ve_c, ctx_done)])

        with contextlib.ExitStack() as esD:
            XSTG = sb("XSTG2", [128, 2 * 1024], F32, esD)
            EVF = sb("EVF2", [128, 2 * 512], F32, esD)
            XOUT = sb("XOUT2", [128, 2 * 512], F32, esD)
            SQB = sb("SQB2", [128, 2 * T], BF16, esD)
            MUBC = sb("MUBC2", [128, 2 * T], F32, esD)
            SROW = sb("SROW2", [128, 2 * T], F32, esD)
            ms_sr2 = PO.op("memset", SROW[:], 0.0, waits=[(ve_c, ctx_done)])

            b2_po = PO.op("tensor_copy", waits=[(ve_c, ctx_done)],
                          out=B2[:], in_=B2[:])
            ctx_ready = {c: (po_c, b2_po) for c in range(DC)}
            st3, castc_b1, xw2 = emit_ffn(B2, B1, wdv, BDS, 1.0, False, x1s3, x2s3,
                                          True, ctx_ready, XSTG, EVF, XOUT, SQB)
            if "t_x2" in tapd:
                SE.dma(qsc, tapd["t_x2"], x2s, waits=[(qsc, max(xw2.values()))])
            bc3 = finalize_ln(st3, MUBC, SROW)
            ln3_done = apply_ln(B1, MUBC, bc3,
                                {c: (po_c, n) for c, n in castc_b1.items()})
            _, _, xw_out = emit_ffn(B1, None, wfv, BFS, 0.5, True, x2s3, out3,
                                    False, ln3_done, XSTG, EVF, XOUT, SQB)
            final_q = max(xw_out.values())

        for n in ("pe", "ve", "se", "po", "qsy", "qsc", "qgp", "cc"):
            p.ctr[n].h = es.enter_context(nc.semaphore(n))
        with nc.Block() as block:
            @block.tensor
            def _(eng):
                p.pe.replay(eng)

            @block.vector
            def _(eng):
                p.ve.replay(eng)

            @block.scalar
            def _(eng):
                p.se.replay(eng)

            @block.gpsimd
            def _(eng):
                p.po.replay(eng)

            @block.sync
            def _(eng):
                p.sy.replay(eng)
                eng.wait_ge(p.ctr["qsc"].h, final_q)
    return nc, tapd


def _host_prep(inputs):
    x = np.asarray(inputs["L0_attn_hidden_states"])
    w_qkv = np.asarray(inputs["L0_attn_w_qkv"])
    b_qkv = np.asarray(inputs["L0_attn_b_qkv"])
    w_dense = np.asarray(inputs["L0_attn_w_dense"])
    b_dense = np.asarray(inputs["L0_attn_b_dense"])
    w_ffn = np.asarray(inputs["L0_ffn_weight"])
    b_ffn = np.asarray(inputs["L0_ffn_bias"])
    lora_A = np.asarray(inputs["L0_ffn_lora_A"])
    lora_B = np.asarray(inputs["L0_ffn_lora_B"])
    lsi = np.asarray(inputs["L0_ffn_lora_start_indices"]).astype(np.int32)

    perm = []
    for kv in range(8):
        perm.append(np.arange((kv * 6 + 4) * 128, (kv * 6 + 5) * 128))
    for kv in range(8):
        perm.append(np.arange((kv * 6 + 5) * 128, (kv * 6 + 6) * 128))
    for kv in range(8):
        for j in range(4):
            perm.append(np.arange((kv * 6 + j) * 128, (kv * 6 + j + 1) * 128))
    perm = np.concatenate(perm)

    def wstream(Wp, nog):
        return np.ascontiguousarray(
            Wp.reshape(nog, 128, DC, 128).transpose(0, 3, 2, 1)).reshape(-1)

    wq_s = wstream(w_qkv[perm], QKV_OC)
    wd_s = wstream(w_dense, DC)
    wf_s = wstream(w_ffn, DC)
    lA_s = np.ascontiguousarray(lora_A.transpose(2, 0, 1).reshape(D, 48))
    lB_s = np.ascontiguousarray(lora_B.transpose(0, 2, 1).reshape(48, D))
    bq_pk = np.ascontiguousarray(b_qkv[perm].reshape(QKV_OC, 128).T)
    bv_row = np.ascontiguousarray(b_qkv[perm][8 * 128:16 * 128].reshape(1, 1024))
    bd_pk = np.ascontiguousarray(b_dense.reshape(DC, 128).T)
    bf_pk = np.ascontiguousarray(b_ffn.reshape(DC, 128).T)
    rowc = np.concatenate([np.ones((1, 128), np.float32),
                           -np.ones((1, 128), np.float32),
                           np.full((1, 128), 2.0 ** 40, np.float32)], axis=1)
    iotab = (np.arange(48, dtype=np.float32) // 16).reshape(48, 1)
    cstb = np.zeros((128, 130), np.float32)
    cstb[:, 0] = 1.0
    cstb[:, 1] = 1.0 / 4096
    cstb = cstb.astype(ml_dtypes.bfloat16)
    bcon = np.zeros((128, 256), np.float32)
    bcon[0, 0:128] = 1.0
    bcon[0, 128:256] = 2.0 ** 40
    inv = 1.0 / (1e6 ** (np.arange(0, 128, 2, dtype=np.float64) / 128))

    shared = dict(wq=wq_s, wd=wd_s, wf=wf_s, lA=lA_s, lB=lB_s, bq=bq_pk,
                  bv_r=bv_row, bd=bd_pk, bf=bf_pk, rowc=rowc, iotab=iotab,
                  cstb=cstb, bcon=bcon,
                  lsi=np.ascontiguousarray(lsi.reshape(1, 4)))
    per_core, tok_idx_all = [], []
    for c in range(8):
        b = c // 2
        half = c % 2
        blocks = [2 * i + half for i in range(16)]
        tok = np.concatenate([np.arange(bb * 64, (bb + 1) * 64) for bb in blocks])
        tok_idx_all.append(tok)
        xin = np.ascontiguousarray(x[b][tok].T)
        f = np.outer(inv, tok.astype(np.float64))
        cq, sq = np.cos(f), np.sin(f)
        cosq = (np.vstack([cq, cq]) / 128.0).astype(np.float32)
        sinq = (np.vstack([sq, sq]) / 128.0).astype(np.float32)
        mk = np.zeros((128, max(MASK_COLS, 128)), dtype=np.float32)
        for (m, ui) in MASK_FLAGGED:
            side, s0 = UNITS[m][ui]
            mk[:, MASK_OFF[(m, ui)]:MASK_OFF[(m, ui)] + 128] = \
                _unit_mask(half, m, side, s0)
        im = dict(shared)
        im.update(xin=xin, cosq=cosq, sinq=sinq,
                  masks=np.ascontiguousarray(mk.astype(ml_dtypes.bfloat16)),
                  bval=np.array([[float(b)]], dtype=np.float32))
        per_core.append(im)
    return per_core, tok_idx_all


_NC_CACHE = {}


def kernel(**inputs):
    per_core, tok_idx = _host_prep(inputs)
    if "plain" not in _NC_CACHE:
        _NC_CACHE["plain"] = build(())[0]
    nc = _NC_CACHE["plain"]
    res = run_bass_kernel_spmd(nc, per_core, core_ids=list(range(8)))
    x = np.asarray(inputs["L0_attn_hidden_states"])
    full = np.empty_like(x)
    for c in range(8):
        full[c // 2][tok_idx[c]] = res.results[c]["out"].T
    return full


# version 10
# speedup vs baseline: 1.1403x; 1.0238x over previous
import contextlib
import numpy as np
import ml_dtypes
import concourse.bass as bass
import concourse.mybir as mybir
from concourse.bass_utils import run_bass_kernel_spmd

F32 = mybir.dt.float32
BF16 = mybir.dt.bfloat16
I32 = mybir.dt.int32
AF = mybir.ActivationFunctionType
ALU = mybir.AluOpType
AX = mybir.AxisListType

D = 4096
DC = 32
T = 1024
TH = 512
NKV = 8
NQH = 32
NB = 32
EPS = 1e-5
QKV_OC = 48
OC_K0, OC_V0, OC_Q0 = 0, 8, 16

_ALLOWED = [{j for j in range(NB) if j <= i and (i - j < 16 or (j + 1) % 8 == 0)}
            for i in range(NB)]

def _attn_units():
    units_per_m = []
    for m in range(8):
        U = set()
        for h in (0, 1):
            for i in (4 * m + h, 4 * m + 2 + h):
                U |= _ALLOWED[i]
        units = []
        for side in (0, 1):
            slots = sorted({j // 2 for j in U if j % 2 == side})
            runs = []
            for s in slots:
                if runs and s == runs[-1][1] + 1:
                    runs[-1][1] = s
                else:
                    runs.append([s, s])
            ext = []
            for a, b in runs:
                a -= a % 2
                b += (1 - b % 2)
                if ext and a <= ext[-1][1] + 1:
                    ext[-1][1] = max(ext[-1][1], b)
                else:
                    ext.append([a, b])
            for a, b in ext:
                for s in range(a, b + 1, 2):
                    units.append((side, s))
        units_per_m.append(units)
    return units_per_m

UNITS = _attn_units()

def _unit_mask(half, m, side, s0):
    msk = np.zeros((128, 128), dtype=np.float32)
    for kk in (0, 1):
        j = 2 * (s0 + kk) + side
        for t in (0, 1):
            qb = 4 * m + 2 * t + half
            blk = msk[kk * 64:(kk + 1) * 64, t * 64:(t + 1) * 64]
            if j == qb:
                blk[:] = (np.arange(64)[:, None] <= np.arange(64)[None, :])
            elif j in _ALLOWED[qb]:
                blk[:] = 1.0
    return msk

def _mask_plan():
    flagged, off, col = [], {}, 0
    for m in range(8):
        for ui, (side, s0) in enumerate(UNITS[m]):
            m0 = _unit_mask(0, m, side, s0)
            m1 = _unit_mask(1, m, side, s0)
            if not (np.all(m0 == 1.0) and np.all(m1 == 1.0)):
                flagged.append((m, ui))
                off[(m, ui)] = col
                col += 128
    return flagged, off, col

MASK_FLAGGED, MASK_OFF, MASK_COLS = _mask_plan()
MASK_FLAG_SET = set(MASK_FLAGGED)

def _vis_ranges():
    out = []
    for m in range(8):
        cols = []
        for side, s0 in UNITS[m]:
            a = side * 1024 + s0 * 64
            cols.append((a, a + 128))
        cols.sort()
        merged = []
        for a, b in cols:
            if merged and a <= merged[-1][1]:
                merged[-1][1] = max(merged[-1][1], b)
            else:
                merged.append([a, b])
        pieces = []
        for a, b in merged:
            while a < b:
                q = a // 512
                e = min(b, (q + 1) * 512)
                pieces.append((q, a - q * 512, e - q * 512))
                a = e
        out.append(pieces)
    return out

VIS_RANGES = _vis_ranges()

class Ctr:
    def __init__(self, name):
        self.name = name
        self.n = 0
        self.h = None


class Rec:
    def __init__(self, prog, name, ctr):
        self.p = prog
        self.name = name
        self.c = ctr
        self.ops = []

    def _waits(self, waits):
        for (c, n) in waits:
            if c is None or n is None or n <= 0:
                continue
            key = (self.name, c.name)
            if n > self.p.wm.get(key, 0):
                self.p.wm[key] = n
                self.ops.append(("wait", c, n))

    def op(self, fname, *a, waits=(), inc=True, **kw):
        self._waits(waits)
        if inc:
            self.c.n += 1
        self.ops.append(("op", fname, a, kw, self.c if inc else None))
        return self.c.n

    def dma(self, q, out, in_, waits=()):
        self._waits(waits)
        q.n += 16
        self.ops.append(("dma", out, in_, q))
        return q.n

    def collective(self, ccctr, kind, aop, groups, ins, outs, waits=()):
        self._waits(waits)
        ccctr.n += 1
        self.ops.append(("op", "collective_compute", (kind, aop),
                         dict(replica_groups=groups, ins=ins, outs=outs), ccctr))
        return ccctr.n

    def wait_only(self, waits):
        self._waits(waits)

    def replay(self, eng):
        for rec in self.ops:
            if rec[0] == "wait":
                _, c, n = rec
                eng.wait_ge(c.h, n)
            elif rec[0] == "op":
                _, fname, a, kw, ctr = rec
                r = getattr(eng, fname)(*a, **kw)
                if ctr is not None:
                    r.then_inc(ctr.h, 1)
            else:
                _, out, in_, q = rec
                eng.dma_start(out=out, in_=in_).then_inc(q.h, 16)


class Prog:
    def __init__(self):
        self.wm = {}
        self.ctr = {n: Ctr(n) for n in
                    ("pe", "ve", "se", "po", "qsy", "qsc", "qgp", "cc")}
        self.pe = Rec(self, "pe", self.ctr["pe"])
        self.ve = Rec(self, "ve", self.ctr["ve"])
        self.se = Rec(self, "se", self.ctr["se"])
        self.po = Rec(self, "po", self.ctr["po"])
        self.sy = Rec(self, "sy", self.ctr["qsy"])

    def barrier(self):
        recs = [self.pe, self.ve, self.se, self.po]
        counts = [(r.c, r.c.n) for r in recs]
        for r in recs:
            r.wait_only([(c, n) for (c, n) in counts if c is not r.c])
        return {r.name: r.c.n for r in recs}


def build(taps=(), attn_serial=0, ffn_serial=0):
    nc = bass.Bass("TRN2", target_bir_lowering=False)
    dt = nc.dram_tensor
    xin = dt("xin", [D, T], F32, kind="ExternalInput").ap()
    wq = dt("wq", [QKV_OC * 128 * DC * 128], F32, kind="ExternalInput").ap()
    wd = dt("wd", [DC * 128 * DC * 128], F32, kind="ExternalInput").ap()
    wf = dt("wf", [DC * 128 * DC * 128], F32, kind="ExternalInput").ap()
    lA = dt("lA", [DC * 128, 48], F32, kind="ExternalInput").ap()
    lB = dt("lB", [48, D], F32, kind="ExternalInput").ap()
    bq = dt("bq", [128, QKV_OC], F32, kind="ExternalInput").ap()
    bv_r = dt("bv_r", [1, 1024], F32, kind="ExternalInput").ap()
    bd = dt("bd", [128, DC], F32, kind="ExternalInput").ap()
    bf_ = dt("bf", [128, DC], F32, kind="ExternalInput").ap()
    cosq = dt("cosq", [128, T], F32, kind="ExternalInput").ap()
    sinq = dt("sinq", [128, T], F32, kind="ExternalInput").ap()
    masks = dt("masks", [128, max(MASK_COLS, 128)], BF16, kind="ExternalInput").ap()
    rowc = dt("rowc", [1, 384], F32, kind="ExternalInput").ap()
    iotab = dt("iotab", [48, 1], F32, kind="ExternalInput").ap()
    bval = dt("bval", [1, 1], F32, kind="ExternalInput").ap()
    cstb = dt("cstb", [128, 130], BF16, kind="ExternalInput").ap()
    bcon = dt("bcon", [128, 256], F32, kind="ExternalInput").ap()
    lsi = dt("lsi", [1, 4], I32, kind="ExternalInput").ap()
    out = dt("out", [D, T], F32, kind="ExternalOutput").ap()
    x1s = dt("x1s", [D, T], F32).ap()
    x2s = dt("x2s", [D, T], F32).ap()
    ccin = dt("ccin", [2048, 1024], BF16)
    ccout = dt("ccout", [4096, 1024], BF16)

    tapd = {}
    tap_specs = {
        "t_ln1": ([128, DC * T], BF16), "t_x1": ([D, T], F32),
        "t_ln2": ([128, DC * T], BF16), "t_q": ([128, NQH * T], BF16),
        "t_kv": ([128, 4096], BF16), "t_ctx": ([128, NQH * T], BF16),
        "t_x2": ([D, T], F32), "t_sel": ([48, 1], F32),
        "t_negc": ([128, 64], F32), "t_u": ([48, T], BF16),
        "t_mu": ([128, 2 * T], F32), "t_b1raw": ([128, DC * T], BF16),
        "t_sums": ([1, 2 * T], F32),
    }
    for tname in taps:
        shape, tdt = tap_specs[tname]
        tapd[tname] = dt(tname, shape, tdt, kind="ExternalOutput").ap()

    p = Prog()
    pe_c, ve_c, se_c, po_c = (p.ctr[n] for n in ("pe", "ve", "se", "po"))
    qsy, qsc, qgp, cc_c = (p.ctr[n] for n in ("qsy", "qsc", "qgp", "cc"))
    PE, VE, SE, PO, SY = p.pe, p.ve, p.se, p.po, p.sy

    with contextlib.ExitStack() as es:
        def sb(name, shape, dtp, stack=None):
            return (stack or es).enter_context(nc.sbuf_tensor(name, shape, dtp))

        B1 = sb("B1", [128, DC * T], BF16)
        B2 = sb("B2", [128, DC * T], BF16)
        WST = sb("WST", [128, 2 * 1024], F32)
        WBF = sb("WBF", [128, 2 * 4096], BF16)
        LAB = sb("LAB", [128, DC * 48], BF16)
        LBB = sb("LBB", [128, D], BF16)
        BQS = sb("BQS", [128, QKV_OC], F32)
        BDS = sb("BDS", [128, DC], F32)
        BFS = sb("BFS", [128, DC], F32)
        ROWC = sb("ROWC", [1, 384], F32)
        IOTA = sb("IOTA", [48, 1], F32)
        BVAL = sb("BVAL", [1, 1], F32)
        CSTB = sb("CSTB", [128, 130], BF16)
        BCON = sb("BCON", [128, 256], F32)
        LSIS = sb("LSIS", [1, 4], I32)
        SELC = sb("SELC", [48, 1], F32)
        BQSK = sb("BQSK", [128, 8], F32)
        FSCR = sb("FSCR", [128, 8], BF16)
        U1S = sb("U1S", [128, T], BF16)
        TINY = sb("TINY", [128, 48], F32)
        QN2M = sb("QN2M", [128, 256], F32)
        pb = [es.enter_context(nc.psum_tensor(f"pb{i}", [128, 512], F32)) for i in range(8)]

        xin3 = xin.rearrange("(c p) t -> p c t", p=128)
        x1s3 = x1s.rearrange("(c p) t -> p c t", p=128)
        x2s3 = x2s.rearrange("(c p) t -> p c t", p=128)
        out3 = out.rearrange("(c p) t -> p c t", p=128)
        wqv = wq.rearrange("(g p n) -> g p n", g=QKV_OC, p=128)
        wdv = wd.rearrange("(g p n) -> g p n", g=DC, p=128)
        wfv = wf.rearrange("(g p n) -> g p n", g=DC, p=128)
        b1v = B1.rearrange("p (hh t) -> p hh t", t=T)
        b2v = B2.rearrange("p (hh t) -> p hh t", t=T)

        def b1c(c):
            return B1[:, c * T:(c + 1) * T]

        def b2c(c):
            return B2[:, c * T:(c + 1) * T]

        d_rowc = SY.dma(qsy, ROWC[:], rowc)
        d_iota = SY.dma(qsy, IOTA[:], iotab)
        d_bval = SY.dma(qsy, BVAL[:], bval)
        d_cstb = SY.dma(qsy, CSTB[:], cstb)
        d_bcon = SY.dma(qsy, BCON[:], bcon)
        d_lsi = SY.dma(qsy, LSIS[:], lsi)
        SY.dma(qsy, BQS[:], bq)
        SY.dma(qsy, BDS[:], bd)
        d_bias = SY.dma(qsy, BFS[:], bf_)
        d_lA = SY.dma(qsy, WST[:, 0:DC * 48].rearrange("p (c r) -> p c r", r=48),
                      lA.rearrange("(c p) r -> p c r", p=128))
        ms1 = PO.op("memset", TINY[:], 0.0)
        ms2 = PO.op("memset", QN2M[:], 0.0)
        ms3 = PO.op("memset", U1S[:], 0.0)
        c_lA = PO.op("tensor_copy", waits=[(qsy, d_lA)], out=LAB[:], in_=WST[:, 0:DC * 48])
        c_lB = PO.op("memset", LBB[:], 0.0, waits=[(po_c, c_lA)])
        for hb in (0, 1):
            d_lB = SY.dma(qsy, WST[0:48, 0:2048], lB[:, hb * 2048:(hb + 1) * 2048],
                          waits=[(po_c, c_lB)])
            c_lB = PO.op("tensor_copy", waits=[(qsy, d_lB)],
                         out=LBB[0:48, hb * 2048:(hb + 1) * 2048], in_=WST[0:48, 0:2048])
        lora_w_ready = c_lB
        bqsk_ready = VE.op("tensor_scalar", waits=[(qsy, d_bias)], out=BQSK[:],
                           in0=BQS[:, 0:8], scalar1=128.0, scalar2=None, op0=ALU.mult)
        v = VE.op("tensor_copy", waits=[(qsy, d_lsi), (po_c, ms3)],
                  out=TINY[0:1, 0:4], in_=LSIS[:])
        v = VE.op("tensor_scalar", waits=[(ve_c, v), (qsy, d_bval)], out=TINY[0:1, 0:4],
                  in0=TINY[0:1, 0:4], scalar1=BVAL[0:1, 0:1], scalar2=None, op0=ALU.is_le)
        v = VE.op("reduce_sum", waits=[(ve_c, v)], out=TINY[0:1, 4:5],
                  in_=TINY[0:1, 0:4], axis=AX.X)
        v_use = VE.op("tensor_scalar", waits=[(ve_c, v)], out=TINY[0:1, 5:6],
                      in0=TINY[0:1, 4:5], scalar1=2.0, scalar2=None, op0=ALU.is_ge)
        v_a = VE.op("tensor_scalar", waits=[(ve_c, v)], out=TINY[0:1, 6:7],
                    in0=TINY[0:1, 4:5], scalar1=2.0, scalar2=0.0,
                    op0=ALU.subtract, op1=ALU.max)
        v_a = VE.op("tensor_scalar", waits=[(ve_c, v_a)], out=TINY[0:1, 6:7],
                    in0=TINY[0:1, 6:7], scalar1=2.0, scalar2=None, op0=ALU.min)
        sel_po = PO.op("tensor_copy", waits=[(ve_c, max(v_use, v_a))],
                       out=TINY[0:1, 5:7], in_=TINY[0:1, 5:7])
        PE.op("drain", inc=False)
        m_bc = PE.op("matmul", waits=[(po_c, sel_po), (qsy, d_bcon)],
                     out=pb[7][0:48, 0:2], lhsT=BCON[:, 0:48], rhs=TINY[:, 5:7],
                     start=True, stop=True)
        PE.op("drain", inc=False)
        v = VE.op("tensor_tensor", waits=[(pe_c, m_bc), (qsy, d_iota)], out=SELC[:],
                  in0=IOTA[:], in1=pb[7][0:48, 1:2], op=ALU.is_equal)
        sel_ready = VE.op("tensor_tensor", waits=[(ve_c, v)], out=SELC[:],
                          in0=SELC[:], in1=pb[7][0:48, 0:1], op=ALU.mult)
        if "t_sel" in tapd:
            SE.dma(qsc, tapd["t_sel"], SELC[:], waits=[(ve_c, sel_ready)])

        def new_wstream(wv_ap, nog):
            return {"issued": set(), "cast": {}, "pedone": {}, "nog": nog, "wv": wv_ap,
                    "slot_cast": {}, "war_po": max(po_c.n, c_lB), "war_pe": pe_c.n}

        def stream_weights(st, og):
            if og in st["issued"] or og >= st["nog"]:
                return
            st["issued"].add(og)
            wv = st["wv"]
            for q in range(4):
                s_ = q % 2
                w = [(po_c, max(st["slot_cast"].get(s_, 0), st["war_po"]))]
                dq = SY.dma(qsy, WST[:, s_ * 1024:(s_ + 1) * 1024],
                            wv[og, :, q * 1024:(q + 1) * 1024], waits=w)
                w = [(qsy, dq),
                     (pe_c, max(st["pedone"].get(og - 2, 0), st["war_pe"]))]
                PO.op("tensor_copy", waits=w, out=FSCR[:, 0:8], in_=FSCR[:, 0:8])
                st["cast"][(og, q)] = PO.op(
                    "tensor_copy",
                    out=WBF[:, (og % 2) * 4096 + q * 1024:(og % 2) * 4096 + (q + 1) * 1024],
                    in_=WST[:, s_ * 1024:(s_ + 1) * 1024])
                st["slot_cast"][s_] = st["cast"][(og, q)]
            st["cast"][(og, 3)] = PO.op(
                "tensor_copy",
                out=FSCR[:, 0:4],
                in_=WBF[:, (og % 2) * 4096 + 1022:(og % 2) * 4096 + 1026]
                if False else
                WBF.rearrange("p (q n) -> p q n", n=1024)
                [:, (og % 2) * 4 + 3:(og % 2) * 4 + 4, 1020:1024]
                .rearrange("p a n -> p (a n)"))

        def wtile(og, dc):
            return WBF[:, (og % 2) * 4096 + dc * 128:(og % 2) * 4096 + (dc + 1) * 128]

        def emit_stats_mms(stat, rhs_bf, sq_bf, first, last, c):
            for h in (0, 1):
                for which, rhs in (("s", rhs_bf), ("q", sq_bf)):
                    bk = stat["bank"][(which, h)]
                    w = [stat["ready"][(c, h, which)]]
                    stat["mm"][(c, h, which)] = PE.op(
                        "matmul", waits=w, out=pb[bk][:, 0:512],
                        lhsT=CSTB[:, 1:129], rhs=rhs[:, h * TH:(h + 1) * TH],
                        start=first, stop=last)

        def finalize_ln(stat, MUBC, SROW, sums_tap=None):
            lastmm = max(stat["mm"].values())
            po_now = po_c.n
            SE.op("copy", waits=[(pe_c, lastmm), (po_c, po_now)], out=SROW[0:1, 0:TH],
                  in_=pb[stat["bank"][("s", 0)]][0:1, 0:512])
            SE.op("copy", out=SROW[0:1, TH:T],
                  in_=pb[stat["bank"][("s", 1)]][0:1, 0:512])
            SE.op("copy", out=SROW[0:1, T:T + TH],
                  in_=pb[stat["bank"][("q", 0)]][0:1, 0:512])
            v4se = SE.op("copy", out=SROW[0:1, T + TH:2 * T],
                         in_=pb[stat["bank"][("q", 1)]][0:1, 0:512])
            v2 = v4 = 0
            tap_q = 0
            if sums_tap is not None:
                tap_q = SE.dma(qsc, sums_tap, SROW[0:1, 0:2 * T], waits=[(se_c, v4se)])
            bcs = []
            for i in (0, 1):
                bk = stat["bank"][("s", i)]
                mm = PE.op("matmul", waits=[(se_c, v4se)], out=pb[bk][:, 0:512],
                           lhsT=BCON[:, 0:128], rhs=SROW[:, i * TH:(i + 1) * TH],
                           start=True, stop=True)
                bcs.append(SE.op("copy", waits=[(pe_c, mm)],
                                 out=MUBC[:, i * TH:(i + 1) * TH], in_=pb[bk][:, 0:512]))
            mu_bc_mm = bcs[:]
            v5 = VE.op("tensor_tensor",
                       waits=[(se_c, max(bcs)), (qsc, tap_q)],
                       out=SROW[0:1, 0:T],
                       in0=SROW[0:1, 0:T], in1=SROW[0:1, 0:T], op=ALU.mult)
            v6 = VE.op("tensor_tensor", waits=[(ve_c, v5)], out=SROW[0:1, T:2 * T],
                       in0=SROW[0:1, T:2 * T], in1=SROW[0:1, 0:T], op=ALU.subtract)
            v6 = VE.op("tensor_scalar", waits=[(ve_c, v6)], out=SROW[0:1, T:2 * T],
                       in0=SROW[0:1, T:2 * T], scalar1=EPS, scalar2=None, op0=ALU.add)
            s1 = SE.op("activation", waits=[(ve_c, v6)], out=SROW[0:1, 0:T],
                       in_=SROW[0:1, T:2 * T], func=AF.Ln, scale=1.0)
            s2 = SE.op("activation", waits=[(se_c, s1)], out=SROW[0:1, T:2 * T],
                       in_=SROW[0:1, 0:T], func=AF.Exp, scale=-0.5)
            for i in (0, 1):
                bk = stat["bank"][("q", i)]
                mm = PE.op("matmul", waits=[(se_c, s2)], out=pb[bk][:, 0:512],
                           lhsT=BCON[:, 0:128], rhs=SROW[:, T + i * TH:T + (i + 1) * TH],
                           start=True, stop=True)
                bcs.append(SE.op("copy", waits=[(pe_c, mm)],
                                 out=MUBC[:, T + i * TH:T + (i + 1) * TH],
                                 in_=pb[bk][:, 0:512]))
            return max(bcs)

        def apply_ln(Bt, MUBC, bc_ready, cast_events):
            done = {}
            for c in range(DC):
                w = [(se_c, bc_ready), cast_events.get(c, (po_c, 0))]
                v1 = PO.op("tensor_tensor", waits=w, out=Bt[:, c * T:(c + 1) * T],
                           in0=Bt[:, c * T:(c + 1) * T], in1=MUBC[:, 0:T], op=ALU.subtract)
                PO.op("tensor_tensor", waits=[(po_c, v1)],
                      out=Bt[:, c * T:(c + 1) * T],
                      in0=Bt[:, c * T:(c + 1) * T],
                      in1=MUBC[:, T:2 * T], op=ALU.mult)
                done[c] = (po_c, PO.op("tensor_copy", out=FSCR[:, 4:8],
                                       in_=Bt[:, c * T + T - 4:(c + 1) * T]))
            return done

        def emit_ffn(Bt_in, Bt_cast, wv_ap, bias_sb, scale, lora, resid3, write3,
                     stats, ln_ready, XSTG, EVF, XOUT, SQB):
            bar = p.barrier()
            stW = new_wstream(wv_ap, DC)
            stream_weights(stW, 0)
            stream_weights(stW, 1)
            st2 = None
            if stats:
                st2 = {"bank": {("s", 0): 4, ("s", 1): 5, ("q", 0): 6, ("q", 1): 7},
                       "ready": {}, "mm": {}}
            u_evac = 0
            if lora:
                umm = {}
                PE.op("drain", inc=False)
                for h in (0, 1):
                    for dc in range(DC):
                        umm[h] = PE.op("matmul",
                                       waits=[ln_ready[dc], (po_c, lora_w_ready)],
                                       out=pb[h][0:48, 0:512],
                                       lhsT=LAB[:, dc * 48:(dc + 1) * 48],
                                       rhs=Bt_in[:, dc * T + h * TH: dc * T + h * TH + TH],
                                       start=(dc == 0), stop=(dc == DC - 1),
                                       inc=(dc == DC - 1))
                for h in (0, 1):
                    u_evac = VE.op("tensor_scalar",
                                   waits=[(pe_c, umm[h]), (ve_c, sel_ready)],
                                   out=U1S[0:48, h * TH:(h + 1) * TH],
                                   in0=pb[h][0:48, 0:512], scalar1=SELC[0:48, 0:1],
                                   scalar2=None, op0=ALU.mult)
                PE.op("drain", inc=False)
                u_evac_po = PO.op("tensor_copy", waits=[(ve_c, u_evac)],
                                  out=U1S[:], in_=U1S[:])
            evac, stt, xw, castc, xdma, sqd = {}, {}, {}, {}, {}, {}
            xldr = {}
            for oc in range(DC):
                war = max(stt.get((oc - 2, 1), 0), bar["ve"])
                xdma[oc] = SY.dma(qsy, XSTG[:, (oc % 2) * T:(oc % 2 + 1) * T],
                                  resid3[:, oc, :], waits=[(ve_c, war)])
                xldr[oc] = PO.op("tensor_copy", waits=[(qsy, xdma[oc])],
                                 out=XSTG[:, (oc % 2) * T:(oc % 2 + 1) * T],
                                 in_=XSTG[:, (oc % 2) * T:(oc % 2 + 1) * T])
                lastmm = 0
                for h in (0, 1):
                    bk = (2 * oc + h) % 4
                    w = [(po_c, stW["cast"][(oc, 3)]),
                         (se_c, evac.get((oc - 2, h), 0))]
                    if lora and oc < 2:
                        w.append((po_c, u_evac_po))
                    for dc in range(DC):
                        w2 = w + [ln_ready[dc]]
                        last = (dc == DC - 1)
                        lastmm = PE.op("matmul", waits=w2, out=pb[bk][:, 0:512],
                                       lhsT=wtile(oc, dc),
                                       rhs=Bt_in[:, dc * T + h * TH: dc * T + h * TH + TH],
                                       start=(dc == 0), stop=(last and not lora),
                                       inc=(last and not lora))
                        w = []
                    if lora:
                        lastmm = PE.op("matmul", waits=[(po_c, u_evac_po)],
                                       out=pb[bk][:, 0:512],
                                       lhsT=LBB[:, oc * 128:(oc + 1) * 128],
                                       rhs=U1S[:, h * TH:(h + 1) * TH],
                                       start=False, stop=True)
                    eslot = h
                    if ffn_serial >= 2:
                        p.barrier()
                    w = [(pe_c, lastmm), (ve_c, stt.get((oc - 1, h), 0))]
                    evac[(oc, h)] = SE.op(
                        "activation", waits=w,
                        out=EVF[:, eslot * 512:(eslot + 1) * 512],
                        in_=pb[bk][:, 0:512], func=AF.Identity,
                        bias=bias_sb[:, oc:oc + 1], scale=1.0)
                    if ffn_serial >= 2:
                        p.barrier()
                    w = [(se_c, evac[(oc, h)]), (po_c, xldr[oc]),
                         (qsc, xw.get((oc - 1, h), 0)),
                         (po_c, castc.get(oc - 1, 0))]
                    xoslot = h
                    stt[(oc, h)] = VE.op(
                        "scalar_tensor_tensor", waits=w,
                        out=XOUT[:, xoslot * 512:(xoslot + 1) * 512],
                        in0=EVF[:, eslot * 512:(eslot + 1) * 512], scalar=scale,
                        in1=XSTG[:, (oc % 2) * T + h * TH:(oc % 2) * T + h * TH + TH],
                        op0=ALU.mult, op1=ALU.add)
                    stt_f = VE.op("tensor_copy", out=FSCR[:, 0:4],
                                  in_=XOUT[:, xoslot * 512 + 508:xoslot * 512 + 512])
                    stt[(oc, h)] = stt_f
                    if ffn_serial >= 2:
                        p.barrier()
                    xw[(oc, h)] = SE.dma(
                        qsc, write3[:, oc, h * TH:(h + 1) * TH],
                        XOUT[:, xoslot * 512:(xoslot + 1) * 512],
                        waits=[(ve_c, stt_f)])
                    if Bt_cast is not None:
                        cv = PO.op("tensor_copy", waits=[(ve_c, stt[(oc, h)])],
                                   out=Bt_cast[:, oc * T + h * TH: oc * T + h * TH + TH],
                                   in_=XOUT[:, xoslot * 512:(xoslot + 1) * 512])
                        castc[oc] = cv
                        if stats:
                            sq = PO.op(
                                "tensor_tensor",
                                waits=[(po_c, cv),
                                       (pe_c, st2["mm"].get((oc - 2, h, "q"), 0))],
                                out=SQB[:, (oc % 2) * T + h * TH:(oc % 2) * T + h * TH + TH],
                                in0=Bt_cast[:, oc * T + h * TH: oc * T + h * TH + TH],
                                in1=Bt_cast[:, oc * T + h * TH: oc * T + h * TH + TH],
                                op=ALU.mult)
                            st2["ready"][(oc, h, "s")] = (po_c, cv)
                            st2["ready"][(oc, h, "q")] = (po_c, sq)
                if stats and oc > 0:
                    oc_ = oc - 1
                    emit_stats_mms(st2, Bt_cast[:, oc_ * T:(oc_ + 1) * T],
                                   SQB[:, (oc_ % 2) * T:(oc_ % 2 + 1) * T],
                                   first=(oc_ == 0), last=False, c=oc_)
                stW["pedone"][oc] = lastmm
                if ffn_serial:
                    p.barrier()
                stream_weights(stW, oc + 2)
            if stats:
                oc_ = DC - 1
                emit_stats_mms(st2, Bt_cast[:, oc_ * T:(oc_ + 1) * T],
                               SQB[:, (oc_ % 2) * T:(oc_ % 2 + 1) * T],
                               first=False, last=True, c=oc_)
            return st2, castc, xw

        with contextlib.ExitStack() as esA:
            XSTG = sb("XSTG", [128, 2 * 1024], F32, esA)
            EVF = sb("EVF", [128, 2 * 512], F32, esA)
            XOUT = sb("XOUT", [128, 2 * 512], F32, esA)
            SQB = sb("SQB", [128, 2 * T], BF16, esA)
            MUBC = sb("MUBC", [128, 2 * T], F32, esA)
            SROW = sb("SROW", [128, 2 * T], F32, esA)
            ms_sr = PO.op("memset", SROW[:], 0.0)

            stat = {"bank": {("s", 0): 4, ("s", 1): 5, ("q", 0): 6, ("q", 1): 7},
                    "ready": {}, "mm": {}}
            cast_done = {}
            slot_free = {}
            for c in range(DC):
                slot = c % 2
                dx = SY.dma(qsy, XSTG[:, slot * T:(slot + 1) * T],
                            xin3[:, c, :],
                            waits=[(po_c, slot_free.get(c - 2, 0))])
                cast_done[c] = PO.op(
                    "tensor_copy", waits=[(qsy, dx)], out=b1c(c),
                    in_=XSTG[:, slot * T:(slot + 1) * T])
                sq = PO.op("tensor_tensor",
                           waits=[(po_c, cast_done[c]),
                                  (pe_c, stat["mm"].get((c - 2, 1, "q"), 0))],
                           out=SQB[:, (c % 2) * T:(c % 2 + 1) * T],
                           in0=b1c(c), in1=b1c(c), op=ALU.mult)
                for h in (0, 1):
                    stat["ready"][(c, h, "s")] = (po_c, cast_done[c])
                    stat["ready"][(c, h, "q")] = (po_c, sq)
                emit_stats_mms(stat, b1c(c), SQB[:, (c % 2) * T:(c % 2 + 1) * T],
                               first=(c == 0), last=(c == DC - 1), c=c)
                slot_free[c] = cast_done[c]
            if "t_b1raw" in tapd:
                SE.dma(qsc, tapd["t_b1raw"], B1[:],
                       waits=[(po_c, max(cast_done.values()))])
            bc_ready = finalize_ln(stat, MUBC, SROW, sums_tap=tapd.get("t_sums"))
            if "t_mu" in tapd:
                SE.dma(qsc, tapd["t_mu"], MUBC[:], waits=[(se_c, bc_ready)])
            ln1_done = apply_ln(B1, MUBC, bc_ready,
                                {c: (po_c, n) for c, n in cast_done.items()})
            if "t_ln1" in tapd:
                SE.dma(qsc, tapd["t_ln1"], B1[:],
                       waits=[(po_c, max(n for _, n in ln1_done.values()))])

            st2, castc_b2, xw1 = emit_ffn(B1, B2, wfv, BFS, 0.5, True, xin3, x1s3,
                                          True, ln1_done, XSTG, EVF, XOUT, SQB)
            if "t_x1" in tapd:
                SE.dma(qsc, tapd["t_x1"], x1s, waits=[(qsc, max(xw1.values()))])
            if "t_u" in tapd:
                SE.dma(qsc, tapd["t_u"], U1S[:])
            bc2 = finalize_ln(st2, MUBC, SROW)
            ln2_done = apply_ln(B2, MUBC, bc2,
                                {c: (po_c, n) for c, n in castc_b2.items()})
            if "t_ln2" in tapd:
                SE.dma(qsc, tapd["t_ln2"], B2[:],
                       waits=[(po_c, max(n for _, n in ln2_done.values()))])

        with contextlib.ExitStack() as esB:
            RSC = sb("RSC", [128, T], F32, esB)
            ROTB = sb("ROTB", [128, T], BF16, esB)
            CQ = sb("CQ", [128, T], F32, esB)
            SQ_ = sb("SQ_", [128, T], F32, esB)
            BVBC = sb("BVBC", [128, 1024], F32, esB)
            KOUT = sb("KOUT", [128, 2 * T], BF16, esB)
            VOWN = sb("VOWN", [128, 8 * T], BF16, esB)

            bar = p.barrier()
            stW = new_wstream(wqv, QKV_OC)
            stream_weights(stW, 0)
            stream_weights(stW, 1)
            d_cq = SY.dma(qsy, CQ[:], cosq, waits=[(ve_c, bar["ve"]), (se_c, bar["se"])])
            d_sq = SY.dma(qsy, SQ_[:], sinq)
            ms_bv = PO.op("memset", BVBC[:], 0.0)
            d_bv = SY.dma(qsy, BVBC[0:1, 0:1024], bv_r, waits=[(po_c, ms_bv)])
            bvb = []
            for i in (0, 1):
                mm = PE.op("matmul", waits=[(qsy, d_bv)], out=pb[6 + i][:, 0:512],
                           lhsT=BCON[:, 0:128], rhs=BVBC[:, i * 512:(i + 1) * 512],
                           start=True, stop=True)
                bvb.append(SE.op("copy", waits=[(pe_c, mm)],
                                 out=BVBC[:, i * 512:(i + 1) * 512],
                                 in_=pb[6 + i][:, 0:512]))
            bvbc_ready = max(bvb)
            ccdma_last = 0
            pend_qn2 = []

            def flush_qn2():
                while pend_qn2:
                    hh_, sq_ev, dst_ = pend_qn2.pop(0)
                    mms = []
                    for h in (0, 1):
                        w = [(po_c, sq_ev), (ve_c, qn2red.get(hh_ - 2, 0))]
                        mms.append(PE.op("matmul", waits=w,
                                         out=pb[4 + h][:, 0:512],
                                         lhsT=CSTB[:, 0:128],
                                         rhs=KOUT[:, (hh_ % 2) * T + h * TH:
                                                 (hh_ % 2) * T + h * TH + TH],
                                         start=True, stop=True))
                    qn2mm[hh_] = mms[1]
                    rd1 = VE.op("reduce_max", waits=[(pe_c, mms[0])],
                                out=QN2M[0:1, hh_ * 8:hh_ * 8 + 4],
                                in_=pb[4][0:1, 0:512]
                                .rearrange("a (m t) -> a m t", t=128), axis=AX.X)
                    qn2red[hh_] = VE.op("reduce_max", waits=[(pe_c, mms[1])],
                                        out=QN2M[0:1, hh_ * 8 + 4:hh_ * 8 + 8],
                                        in_=pb[5][0:1, 0:512]
                                        .rearrange("a (m t) -> a m t", t=128),
                                        axis=AX.X)
            bank_free_se = {}
            rsc_free_evt = (po_c, 0)
            kout_free = {}
            vevac = {}
            qn2red = {}
            qn2mm = {}
            qrope = {}
            for oc in range(QKV_OC):
                if OC_V0 <= oc < OC_Q0:
                    kvv = oc - OC_V0
                    lastmm = 0
                    for tt in range(8):
                        bk = 4 + (tt % 2)
                        w = [(po_c, stW["cast"][(oc, 3)]),
                             (ve_c, vevac.get((oc, tt - 2), vevac.get((oc - 1, tt + 6), 0)))]
                        for dc in range(DC):
                            w2 = w + [ln2_done[dc]]
                            lastmm = PE.op("matmul", waits=w2, out=pb[bk][:, 0:128],
                                           lhsT=B2[:, dc * T + tt * 128: dc * T + (tt + 1) * 128],
                                           rhs=wtile(oc, dc),
                                           start=(dc == 0), stop=(dc == DC - 1),
                                           inc=(dc == DC - 1))
                            w = []
                        vevac[(oc, tt)] = VE.op(
                            "tensor_tensor",
                            waits=[(pe_c, lastmm), (se_c, bvbc_ready)],
                            out=VOWN[:, tt * T + kvv * 128: tt * T + (kvv + 1) * 128],
                            in0=pb[bk][:, 0:128],
                            in1=BVBC[:, kvv * 128:(kvv + 1) * 128], op=ALU.add)
                    stW["pedone"][oc] = lastmm
                    if oc == OC_V0 + 7:
                        wv_all = max(vevac[(o2, t2)] for o2 in range(OC_V0, OC_V0 + 8)
                                     for t2 in range(8))
                        for tt in range(8):
                            ccdma_last = PO.dma(
                                qgp, ccin[1024 + tt * 128:1024 + (tt + 1) * 128, :],
                                VOWN[:, tt * T:(tt + 1) * T], waits=[(ve_c, wv_all)])
                        cc_done = PO.collective(
                            cc_c, "AllGather", ALU.bypass,
                            [[0, 1], [2, 3], [4, 5], [6, 7]],
                            [ccin.ap().opt()], [ccout.ap().opt()],
                            waits=[(qgp, ccdma_last)])
                else:
                    is_q = oc >= OC_Q0
                    hh = oc - OC_Q0 if is_q else None
                    lastmm = 0
                    for h in (0, 1):
                        bk = (2 * oc + h) % 4
                        w = [(po_c, stW["cast"][(oc, 3)]),
                             (se_c, bank_free_se.get(bk, 0))]
                        for dc in range(DC):
                            w2 = w + [ln2_done[dc]]
                            lastmm = PE.op("matmul", waits=w2, out=pb[bk][:, 0:512],
                                           lhsT=wtile(oc, dc),
                                           rhs=B2[:, dc * T + h * TH: dc * T + h * TH + TH],
                                           start=(dc == 0), stop=(dc == DC - 1),
                                           inc=(dc == DC - 1))
                            w = []
                    flush_qn2()
                    ev = []
                    for h in (0, 1):
                        bk = (2 * oc + h) % 4
                        ev.append(SE.op("activation",
                                        waits=[(pe_c, lastmm - (1 - h)),
                                               rsc_free_evt, (ve_c, bqsk_ready)],
                                        out=RSC[:, h * TH:(h + 1) * TH],
                                        in_=pb[bk][:, 0:512], func=AF.Identity,
                                        bias=(BQS[:, oc:oc + 1] if is_q
                                              else BQSK[:, oc:oc + 1]),
                                        scale=(1.0 if is_q else 128.0)))
                        bank_free_se[bk] = ev[h]
                    r1 = SE.op("activation", waits=[(se_c, ev[1])], out=ROTB[0:64, :],
                               in_=RSC[64:128, 0:T], func=AF.Copy, bias=0.0, scale=-1.0)
                    r2 = SE.op("activation", out=ROTB[64:128, :],
                               in_=RSC[0:64, 0:T], func=AF.Copy, bias=0.0, scale=1.0)
                    if is_q:
                        dst = b1c(hh)
                        v1 = VE.op("tensor_tensor",
                                   waits=[(se_c, max(r1, r2)), (qsy, d_sq)],
                                   out=RSC[:, 0:T], in0=RSC[:, 0:T], in1=CQ[:], op=ALU.mult)
                        v2 = VE.op("tensor_tensor", waits=[(ve_c, v1)], out=ROTB[:],
                                   in0=ROTB[:], in1=SQ_[:], op=ALU.mult)
                        v2 = VE.op("tensor_copy", out=FSCR[:, 0:4],
                                   in_=ROTB[:, T - 4:T])
                        v3 = PO.op("tensor_tensor", waits=[(ve_c, v2)], out=dst,
                                   in0=RSC[:, 0:T], in1=ROTB[:], op=ALU.add)
                        qrope[hh] = v3
                        rsc_free_evt = (po_c, v3)
                        sq_ = PO.op("tensor_tensor",
                                    waits=[(po_c, v3), (qgp, kout_free.get(7, 0)),
                                           (pe_c, qn2mm.get(hh - 2, 0))],
                                    out=KOUT[:, (hh % 2) * T:(hh % 2 + 1) * T],
                                    in0=dst, in1=dst, op=ALU.mult)
                        pend_qn2.append((hh, sq_, dst))
                    else:
                        slot = oc % 2
                        dst = KOUT[:, slot * T:(slot + 1) * T]
                        w_t = [(se_c, max(r1, r2)), (qsy, d_sq),
                               (qgp, kout_free.get(oc - 2, 0))]
                        v1 = VE.op("tensor_tensor", waits=w_t, out=RSC[:, 0:T],
                                   in0=RSC[:, 0:T], in1=CQ[:], op=ALU.mult)
                        v2 = VE.op("tensor_tensor", waits=[(ve_c, v1)], out=ROTB[:],
                                   in0=ROTB[:], in1=SQ_[:], op=ALU.mult)
                        v2 = VE.op("tensor_copy", out=FSCR[:, 0:4],
                                   in_=ROTB[:, T - 4:T])
                        v3 = PO.op("tensor_tensor", waits=[(ve_c, v2)], out=dst,
                                   in0=RSC[:, 0:T], in1=ROTB[:], op=ALU.add)
                        rsc_free_evt = (po_c, v3)
                        kout_free[oc] = PO.dma(qgp, ccin[oc * 128:(oc + 1) * 128, :], dst,
                                               waits=[(po_c, v3)])
                    stW["pedone"][oc] = lastmm
                stream_weights(stW, oc + 2)
            flush_qn2()
            qkv_po_last = max(qrope.values())
            qkv_ve_last = max(qn2red.values())
            if "t_q" in tapd:
                SE.dma(qsc, tapd["t_q"], B1[:], waits=[(po_c, qkv_po_last)])

        with contextlib.ExitStack() as esC:
            KKV = sb("KKV", [128, 2 * 2048], BF16, esC)
            VKV = sb("VKV", [128, 2 * 2048], BF16, esC)
            PTS = sb("PTS", [128, 4 * 512], BF16, esC)
            MSK = sb("MSK", [128, max(MASK_COLS, 128)], BF16, esC)
            K2S = sb("K2S", [128, 2048], BF16, esC)
            RSB = sb("RSB", [128, 2 * 512], F32, esC)
            NEGC = sb("NEGC", [128, 64], F32, esC)
            RROW = sb("RROW", [128, 512], F32, esC)

            bar = p.barrier()
            ms_rr = PO.op("memset", RROW[:], 0.0)
            stW_d = new_wstream(wdv, DC)
            stream_weights(stW_d, 0)
            stream_weights(stW_d, 1)
            d_msk = SY.dma(qsy, MSK[:, 0:max(MASK_COLS, 128)], masks,
                           waits=[(ve_c, qkv_ve_last), (po_c, max(bar["po"], qkv_po_last)),
                                  (qgp, ccdma_last)])
            cco3 = ccout.ap().rearrange("(r p) t -> p r t", p=128)
            kvdma = {}
            pe_done_kv = {}
            ctx_free = {}
            sums_free = {}
            exp_by_stb = {}
            pts_war = {}

            kv_ldr = {}

            def issue_readback(kv):
                slot = kv % 2
                w = [(cc_c, 1), (ve_c, qkv_ve_last), (po_c, qkv_po_last),
                     (qgp, ccdma_last),
                     (pe_c, pe_done_kv.get(kv - 2, 0)),
                     (se_c, exp_by_stb.get(0, 0)), (se_c, exp_by_stb.get(1, 0))]
                SE.dma(qsc, KKV[:, slot * 2048:slot * 2048 + 1024],
                       cco3[:, kv:kv + 1, :].rearrange("p a t -> p (a t)"), waits=w)
                SE.dma(qsc, KKV[:, slot * 2048 + 1024:slot * 2048 + 2048],
                       cco3[:, 16 + kv:17 + kv, :].rearrange("p a t -> p (a t)"))
                SE.dma(qsc, VKV[:, slot * 2048:slot * 2048 + 1024]
                       .rearrange("p (c n) -> p c n", n=128),
                       cco3[:, 8:16, kv * 128:(kv + 1) * 128])
                last = SE.dma(qsc, VKV[:, slot * 2048 + 1024:slot * 2048 + 2048]
                              .rearrange("p (c n) -> p c n", n=128),
                              cco3[:, 24:32, kv * 128:(kv + 1) * 128])
                l1 = PO.op("tensor_copy", waits=[(qsc, last)],
                           out=KKV[:, slot * 2048:(slot + 1) * 2048],
                           in_=KKV[:, slot * 2048:(slot + 1) * 2048])
                kv_ldr[kv] = PO.op("tensor_copy",
                                   out=VKV[:, slot * 2048:(slot + 1) * 2048],
                                   in_=VKV[:, slot * 2048:(slot + 1) * 2048])
                return last

            kvdma[0] = issue_readback(0)
            kvdma[1] = issue_readback(1)
            kn2_mm_last = 0
            rsb_free = {}
            mmn_last = 0
            cpb_last = 0
            negc_ready = {}
            ctx_evac_all = []
            pend_nrm = []

            def flush_nrm():
                nonlocal cpb_last
                while pend_nrm:
                    kv_, m_, sums_bk_, ctx_bk_, last_sum_, last_pv_ =                         pend_nrm.pop(0)
                    t1 = SE.op("activation", waits=[(pe_c, last_sum_),
                                                    (po_c, ms_rr)],
                               out=RROW[0:1, 0:512],
                               in_=pb[sums_bk_][0:1, 0:512],
                               func=AF.Ln, scale=2.0 ** 40)
                    sums_free[sums_bk_] = t1
                    t2 = SE.op("activation", waits=[(se_c, t1)],
                               out=RROW[0:1, 0:512],
                               in_=RROW[0:1, 0:512], func=AF.Exp, scale=-1.0)
                    mmb = PE.op("matmul",
                                waits=[(se_c, max(cpb_last, t2))],
                                out=pb[6][:, 0:512], lhsT=BCON[:, 128:256],
                                rhs=RROW[:, 0:512], start=True, stop=True)
                    cpb_last = SE.op("copy",
                                     waits=[(pe_c, mmb),
                                            (ve_c, rsb_free.get(m_ % 2, 0))],
                                     out=RSB[:, (m_ % 2) * 512:(m_ % 2 + 1) * 512],
                                     in_=pb[6][:, 0:512])
                    ev = VE.op("tensor_tensor",
                               waits=[(se_c, cpb_last), (pe_c, last_pv_)],
                               out=b2v[:, 4 * kv_:4 * kv_ + 4,
                                       m_ * 128:(m_ + 1) * 128],
                               in0=pb[ctx_bk_][:, 0:512],
                               in1=RSB[:, (m_ % 2) * 512:(m_ % 2 + 1) * 512],
                               op=ALU.mult)
                    ctx_free[ctx_bk_] = ev
                    rsb_free[m_ % 2] = ev
                    ctx_evac_all.append(ev)
                    if attn_serial >= 1:
                        p.barrier()

            for kv in range(NKV):
                flush_nrm()
                slot = kv % 2
                Kk = KKV[:, slot * 2048:(slot + 1) * 2048]
                Vk = VKV[:, slot * 2048:(slot + 1) * 2048]
                k2 = PO.op("tensor_tensor",
                           waits=[(po_c, kv_ldr[kv]), (pe_c, kn2_mm_last)],
                           out=K2S[:], in0=Kk, in1=Kk, op=ALU.mult)
                piece_cnt = {m: 0 for m in range(8)}
                vlast = 0
                for j in range(4):
                    bk = 6 + (j % 2)
                    w = [(po_c, k2), (ve_c, vlast), (se_c, cpb_last), (pe_c, mmn_last)]
                    mm = PE.op("matmul", waits=w, out=pb[bk][:, 0:512],
                               lhsT=CSTB[:, 0:128], rhs=K2S[:, j * 512:(j + 1) * 512],
                               start=True, stop=True)
                    kn2_mm_last = mm
                    for m in range(8):
                        for (q, lo, hi) in VIS_RANGES[m]:
                            if q != j:
                                continue
                            if piece_cnt[m] == 0:
                                vlast = VE.op("reduce_max", waits=[(pe_c, mm)],
                                              out=TINY[0:1, 16 + m:17 + m],
                                              in_=pb[bk][0:1, lo:hi], axis=AX.X)
                            else:
                                vlast = VE.op("reduce_max", waits=[(pe_c, mm)],
                                              out=TINY[0:1, 32:33],
                                              in_=pb[bk][0:1, lo:hi], axis=AX.X)
                                vlast = VE.op("tensor_tensor", waits=[(ve_c, vlast)],
                                              out=TINY[0:1, 16 + m:17 + m],
                                              in0=TINY[0:1, 16 + m:17 + m],
                                              in1=TINY[0:1, 32:33], op=ALU.max)
                            piece_cnt[m] += 1
                qv = QN2M[0:1, 0:256].rearrange("a (h m) -> a m h", m=8)
                for m in range(8):
                    qg = VE.op("reduce_max", waits=[(ve_c, qn2red[4 * kv + 3])],
                               out=TINY[0:1, 33:34],
                               in_=qv[0:1, m, 4 * kv:4 * kv + 4], axis=AX.X)
                    vlast = VE.op("tensor_tensor", waits=[(ve_c, max(qg, vlast))],
                                  out=TINY[0:1, 24 + m:25 + m],
                                  in0=TINY[0:1, 16 + m:17 + m],
                                  in1=TINY[0:1, 33:34], op=ALU.mult)
                c2v = VE.op("tensor_scalar", waits=[(ve_c, vlast)], out=TINY[0:1, 24:32],
                            in0=TINY[0:1, 24:32], scalar1=1e-18, scalar2=None,
                            op0=ALU.add)
                s1 = SE.op("activation", waits=[(ve_c, c2v)], out=TINY[0:1, 24:32],
                           in_=TINY[0:1, 24:32], func=AF.Ln, scale=1.0)
                s2 = SE.op("activation", waits=[(se_c, s1)], out=TINY[0:1, 24:32],
                           in_=TINY[0:1, 24:32], func=AF.Exp, scale=0.5)
                mmn = PE.op("matmul", waits=[(se_c, s2)],
                            out=pb[7][:, 0:8], lhsT=BCON[:, 0:128],
                            rhs=TINY[:, 24:32], start=True, stop=True)
                mmn_last = mmn
                negc_ready[kv] = SE.op("activation", waits=[(pe_c, mmn)],
                                       out=NEGC[:, kv * 8:(kv + 1) * 8],
                                       in_=pb[7][:, 0:8], func=AF.Copy,
                                       bias=0.0, scale=-1.0)
                for m in range(8):
                    nu = len(UNITS[m])
                    sums_bk = 2 + (m % 2)
                    ctx_bk = 4 + (m % 2)
                    last_sum = last_pv = 0
                    for ui, (side, s0) in enumerate(UNITS[m]):
                        stb = ui % 2
                        pslot = ui % 4
                        kcol = side * 1024 + s0 * 64
                        vchunk = side * 8 + s0 // 2
                        w = [(po_c, kv_ldr[kv]), (se_c, exp_by_stb.get(stb, 0))]
                        mm_qk = PE.op("matmul", waits=w, out=pb[stb][:, 0:512],
                                      lhsT=Kk[:, kcol:kcol + 128],
                                      rhs=b1v[:, 4 * kv:4 * kv + 4, m * 128:(m + 1) * 128],
                                      start=True, stop=True)
                        w = [(pe_c, max(mm_qk, pts_war.get(pslot, 0))),
                             (se_c, negc_ready[kv]), (qsy, d_msk)]
                        e = SE.op("activation", waits=w,
                                  out=PTS[:, pslot * 512:(pslot + 1) * 512],
                                  in_=pb[stb][:, 0:512], func=AF.Exp,
                                  bias=NEGC[:, kv * 8 + m:kv * 8 + m + 1], scale=1.0)
                        exp_by_stb[stb] = e
                        gate = (se_c, e)
                        if (m, ui) in MASK_FLAG_SET:
                            off = MASK_OFF[(m, ui)]
                            pm = 0
                            for hq in range(4):
                                pm = PO.op("tensor_tensor", waits=[(se_c, e)],
                                           out=PTS[:, pslot * 512 + hq * 128:
                                                   pslot * 512 + (hq + 1) * 128],
                                           in0=PTS[:, pslot * 512 + hq * 128:
                                                   pslot * 512 + (hq + 1) * 128],
                                           in1=MSK[:, off:off + 128], op=ALU.mult)
                            gate = (po_c, pm)
                        w = [gate]
                        if ui == 0:
                            w.append((se_c, sums_free.get(sums_bk, 0)))
                        last_sum = PE.op("matmul", waits=w, out=pb[sums_bk][:, 0:512],
                                         lhsT=CSTB[:, 0:128],
                                         rhs=PTS[:, pslot * 512:(pslot + 1) * 512],
                                         start=(ui == 0), stop=(ui == nu - 1))
                        w = [gate]
                        if ui == 0:
                            w.append((ve_c, ctx_free.get(ctx_bk, 0)))
                        last_pv = PE.op("matmul", waits=w, out=pb[ctx_bk][:, 0:512],
                                        lhsT=Vk[:, vchunk * 128:(vchunk + 1) * 128],
                                        rhs=PTS[:, pslot * 512:(pslot + 1) * 512],
                                        start=(ui == 0), stop=(ui == nu - 1))
                        pts_war[pslot] = last_pv
                        if attn_serial >= 2:
                            p.barrier()
                    flush_nrm()
                    pend_nrm.append((kv, m, sums_bk, ctx_bk, last_sum, last_pv))
                pe_done_kv[kv] = last_pv
                if kv + 2 < NKV:
                    kvdma[kv + 2] = issue_readback(kv + 2)
            flush_nrm()
            ctx_done = max(ctx_evac_all)
            if "t_kv" in tapd:
                tkv = tapd["t_kv"].rearrange("p (a n) -> p a n", a=2)
                SE.dma(qsc, tkv[:, 0:1, :].rearrange("p a n -> p (a n)"),
                       KKV[:, 0:2048], waits=[(qsc, kvdma[0]), (pe_c, pe_done_kv[1])])
                SE.dma(qsc, tkv[:, 1:2, :].rearrange("p a n -> p (a n)"),
                       VKV[:, 0:2048])
            if "t_negc" in tapd:
                SE.dma(qsc, tapd["t_negc"], NEGC[:],
                       waits=[(se_c, max(negc_ready.values()))])
            if "t_ctx" in tapd:
                SE.dma(qsc, tapd["t_ctx"], B2[:], waits=[(ve_c, ctx_done)])

        with contextlib.ExitStack() as esD:
            XSTG = sb("XSTG2", [128, 2 * 1024], F32, esD)
            EVF = sb("EVF2", [128, 2 * 512], F32, esD)
            XOUT = sb("XOUT2", [128, 2 * 512], F32, esD)
            SQB = sb("SQB2", [128, 2 * T], BF16, esD)
            MUBC = sb("MUBC2", [128, 2 * T], F32, esD)
            SROW = sb("SROW2", [128, 2 * T], F32, esD)
            ms_sr2 = PO.op("memset", SROW[:], 0.0, waits=[(ve_c, ctx_done)])

            b2_po = PO.op("tensor_copy", waits=[(ve_c, ctx_done)],
                          out=B2[:], in_=B2[:])
            ctx_ready = {c: (po_c, b2_po) for c in range(DC)}
            st3, castc_b1, xw2 = emit_ffn(B2, B1, wdv, BDS, 1.0, False, x1s3, x2s3,
                                          True, ctx_ready, XSTG, EVF, XOUT, SQB)
            if "t_x2" in tapd:
                SE.dma(qsc, tapd["t_x2"], x2s, waits=[(qsc, max(xw2.values()))])
            bc3 = finalize_ln(st3, MUBC, SROW)
            ln3_done = apply_ln(B1, MUBC, bc3,
                                {c: (po_c, n) for c, n in castc_b1.items()})
            _, _, xw_out = emit_ffn(B1, None, wfv, BFS, 0.5, True, x2s3, out3,
                                    False, ln3_done, XSTG, EVF, XOUT, SQB)
            final_q = max(xw_out.values())

        for n in ("pe", "ve", "se", "po", "qsy", "qsc", "qgp", "cc"):
            p.ctr[n].h = es.enter_context(nc.semaphore(n))
        with nc.Block() as block:
            @block.tensor
            def _(eng):
                p.pe.replay(eng)

            @block.vector
            def _(eng):
                p.ve.replay(eng)

            @block.scalar
            def _(eng):
                p.se.replay(eng)

            @block.gpsimd
            def _(eng):
                p.po.replay(eng)

            @block.sync
            def _(eng):
                p.sy.replay(eng)
                eng.wait_ge(p.ctr["qsc"].h, final_q)
    return nc, tapd


def _host_prep(inputs):
    x = np.asarray(inputs["L0_attn_hidden_states"])
    w_qkv = np.asarray(inputs["L0_attn_w_qkv"])
    b_qkv = np.asarray(inputs["L0_attn_b_qkv"])
    w_dense = np.asarray(inputs["L0_attn_w_dense"])
    b_dense = np.asarray(inputs["L0_attn_b_dense"])
    w_ffn = np.asarray(inputs["L0_ffn_weight"])
    b_ffn = np.asarray(inputs["L0_ffn_bias"])
    lora_A = np.asarray(inputs["L0_ffn_lora_A"])
    lora_B = np.asarray(inputs["L0_ffn_lora_B"])
    lsi = np.asarray(inputs["L0_ffn_lora_start_indices"]).astype(np.int32)

    perm = []
    for kv in range(8):
        perm.append(np.arange((kv * 6 + 4) * 128, (kv * 6 + 5) * 128))
    for kv in range(8):
        perm.append(np.arange((kv * 6 + 5) * 128, (kv * 6 + 6) * 128))
    for kv in range(8):
        for j in range(4):
            perm.append(np.arange((kv * 6 + j) * 128, (kv * 6 + j + 1) * 128))
    perm = np.concatenate(perm)

    def wstream(Wp, nog):
        return np.ascontiguousarray(
            Wp.reshape(nog, 128, DC, 128).transpose(0, 3, 2, 1)).reshape(-1)

    wq_s = wstream(w_qkv[perm], QKV_OC)
    wd_s = wstream(w_dense, DC)
    wf_s = wstream(w_ffn, DC)
    lA_s = np.ascontiguousarray(lora_A.transpose(2, 0, 1).reshape(D, 48))
    lB_s = np.ascontiguousarray(lora_B.transpose(0, 2, 1).reshape(48, D))
    bq_pk = np.ascontiguousarray(b_qkv[perm].reshape(QKV_OC, 128).T)
    bv_row = np.ascontiguousarray(b_qkv[perm][8 * 128:16 * 128].reshape(1, 1024))
    bd_pk = np.ascontiguousarray(b_dense.reshape(DC, 128).T)
    bf_pk = np.ascontiguousarray(b_ffn.reshape(DC, 128).T)
    rowc = np.concatenate([np.ones((1, 128), np.float32),
                           -np.ones((1, 128), np.float32),
                           np.full((1, 128), 2.0 ** 40, np.float32)], axis=1)
    iotab = (np.arange(48, dtype=np.float32) // 16).reshape(48, 1)
    cstb = np.zeros((128, 130), np.float32)
    cstb[:, 0] = 1.0
    cstb[:, 1] = 1.0 / 4096
    cstb = cstb.astype(ml_dtypes.bfloat16)
    bcon = np.zeros((128, 256), np.float32)
    bcon[0, 0:128] = 1.0
    bcon[0, 128:256] = 2.0 ** 40
    inv = 1.0 / (1e6 ** (np.arange(0, 128, 2, dtype=np.float64) / 128))

    shared = dict(wq=wq_s, wd=wd_s, wf=wf_s, lA=lA_s, lB=lB_s, bq=bq_pk,
                  bv_r=bv_row, bd=bd_pk, bf=bf_pk, rowc=rowc, iotab=iotab,
                  cstb=cstb, bcon=bcon,
                  lsi=np.ascontiguousarray(lsi.reshape(1, 4)))
    per_core, tok_idx_all = [], []
    for c in range(8):
        b = c // 2
        half = c % 2
        blocks = [2 * i + half for i in range(16)]
        tok = np.concatenate([np.arange(bb * 64, (bb + 1) * 64) for bb in blocks])
        tok_idx_all.append(tok)
        xin = np.ascontiguousarray(x[b][tok].T)
        f = np.outer(inv, tok.astype(np.float64))
        cq, sq = np.cos(f), np.sin(f)
        cosq = (np.vstack([cq, cq]) / 128.0).astype(np.float32)
        sinq = (np.vstack([sq, sq]) / 128.0).astype(np.float32)
        mk = np.zeros((128, max(MASK_COLS, 128)), dtype=np.float32)
        for (m, ui) in MASK_FLAGGED:
            side, s0 = UNITS[m][ui]
            mk[:, MASK_OFF[(m, ui)]:MASK_OFF[(m, ui)] + 128] = \
                _unit_mask(half, m, side, s0)
        im = dict(shared)
        im.update(xin=xin, cosq=cosq, sinq=sinq,
                  masks=np.ascontiguousarray(mk.astype(ml_dtypes.bfloat16)),
                  bval=np.array([[float(b)]], dtype=np.float32))
        per_core.append(im)
    return per_core, tok_idx_all


_NC_CACHE = {}


def kernel(**inputs):
    per_core, tok_idx = _host_prep(inputs)
    if "plain" not in _NC_CACHE:
        _NC_CACHE["plain"] = build(())[0]
    nc = _NC_CACHE["plain"]
    res = run_bass_kernel_spmd(nc, per_core, core_ids=list(range(8)))
    x = np.asarray(inputs["L0_attn_hidden_states"])
    full = np.empty_like(x)
    for c in range(8):
        full[c // 2][tok_idx[c]] = res.results[c]["out"].T
    return full
